# version 01
# baseline (speedup 1.0000x reference)
import contextlib
import numpy as np

import concourse.bass as bass
import concourse.mybir as mybir
from concourse.bass_utils import run_bass_kernel_spmd

F32 = mybir.dt.float32
BF16 = mybir.dt.bfloat16
AF = mybir.ActivationFunctionType
ALU = mybir.AluOpType
AX = mybir.AxisListType

T, NT, D, DCH = 2048, 16, 1024, 8
NM, H, HDK, HDV, DK, DV = 4, 4, 128, 256, 512, 1024
CAP = 4096.0
GATE_NORM = 16.0
SUB, NSUB, CHUNK = 512, 4, 128
SCALE_Q = float(HDK) ** -0.5
LN_EPS = 1e-5
RMS_EPS = 1e-5
FP4_MAX, FP8_MAX = 6.0, 448.0
E16, EPC, I_FFN, I_EXP = 16, 2, 4096, 1024
ROUTE_SCALE = 2.826
H1, HD1 = 16, 64
K_HALF, K_ONE, K_TWO = 96.0, 192.0, 384.0
VELT = float(2 ** 20 + 1)
SUBN = 2.0 ** -6
K_SUBN = 1.5 * (2 ** 23) * (2 ** -9)
GRP4 = [[0, 1, 2, 3], [4, 5, 6, 7]]
GRP8 = [list(range(8))]


class Prog:
    def __init__(self, nc):
        self.nc = nc
        self.entries = []
        self.last_w = {}
        self.readers = {}
        self.gap_fn = None
        self._last_on = {}

    def _chan(self, e):
        return ("cc",) if e["kind"] == "cc" else (e["kind"], e["eng"])

    def emit(self, eng, fn, reads=(), writes=(), kind="compute", tag="", outw=None):
        ch = ("cc",) if kind == "cc" else (kind, eng)
        hist = self._last_on.get(ch, [])
        need_gap = False
        if kind == "compute" and self.gap_fn is not None and eng == "vector":
            rw = set(reads) | set(writes)
            if hist:
                p1 = hist[-1]
                if p1[1] is not None and p1[1] < 128 and (set(p1[0]) & rw):
                    need_gap = True
                if (len(hist) > 1 and p1[1] is not None and p1[1] < 128):
                    p2 = hist[-2]
                    if p2[1] is not None and p2[1] < 128 and (set(p2[0]) & rw):
                        need_gap = True
        if need_gap:
            ge = dict(idx=len(self.entries), eng=eng, fn=self.gap_fn(eng),
                      deps=set(), kind="compute", need_inc=False, tag="gap")
            self.entries.append(ge)
            hist = hist + [((), 1024)]
        self._last_on[ch] = (hist + [(tuple(writes), outw)])[-2:]
        idx = len(self.entries)
        deps = set()
        for k in reads:
            if k in self.last_w:
                deps.add(self.last_w[k])
        for k in writes:
            if k in self.last_w:
                deps.add(self.last_w[k])
            for r in self.readers.get(k, ()):
                deps.add(r)
        self.entries.append(dict(idx=idx, eng=eng, fn=fn, deps=deps,
                                 kind=kind, need_inc=False, tag=tag))
        for k in reads:
            self.readers.setdefault(k, []).append(idx)
        for k in writes:
            self.last_w[k] = idx
            self.readers[k] = []
        return idx

    def dma(self, eng, out, in_, reads=(), writes=(), tag="", **kw):
        return self.emit(eng,
                         lambda e, o=out, i=in_, kw=kw: e.dma_start(out=o, in_=i, **kw),
                         reads=reads, writes=writes, kind="dma", tag=tag)

    def cc(self, kind, op, groups, in_ap, out_ap, reads=(), writes=(), tag=""):
        return self.emit(
            "gpsimd",
            lambda e, k=kind, o=op, g=groups, ia=in_ap, oa=out_ap:
                e.collective_compute(k, o, replica_groups=g, ins=[ia], outs=[oa]),
            reads=reads, writes=writes, kind="cc", tag=tag)

    def finalize(self):
        for i, e in enumerate(self.entries):
            e["idx"] = i
        for e in self.entries:
            ce = self._chan(e)
            for d in e["deps"]:
                de = self.entries[d]
                if self._chan(de) != ce:
                    de["need_inc"] = True
        for e in self.entries:
            if e["kind"] in ("dma", "cc"):
                e["need_inc"] = True
        cnt = {}
        for e in self.entries:
            c = self._chan(e)
            n = cnt.get(c, 0) + (1 if e["need_inc"] else 0)
            cnt[c] = n
            e["cum"] = n
        self.totals = dict(cnt)
        hw = {}
        for e in self.entries:
            waits = {}
            for d in sorted(e["deps"]):
                de = self.entries[d]
                dc = self._chan(de)
                if dc == self._chan(e):
                    continue
                waits[dc] = max(waits.get(dc, 0), de["cum"])
            wl = []
            for dc in sorted(waits, key=str):
                cum = waits[dc]
                key = (e["eng"], dc)
                if hw.get(key, 0) >= cum:
                    continue
                hw[key] = cum
                wl.append((dc, cum * (16 if dc[0] == "dma" else 1)))
            e["waits"] = wl
        self.chans = sorted({self._chan(e) for e in self.entries}, key=str)

    def build_block(self):
        nc = self.nc
        sems = {}
        stack = contextlib.ExitStack()
        for c in self.chans:
            sems[c] = stack.enter_context(nc.semaphore("s_" + "_".join(map(str, c))))

        def run(eng_obj, name):
            for e in self.entries:
                if e["eng"] != name:
                    continue
                for dc, val in e["waits"]:
                    eng_obj.wait_ge(sems[dc], val)
                inst = e["fn"](eng_obj)
                if e["need_inc"]:
                    inst.then_inc(sems[self._chan(e)], 16 if e["kind"] == "dma" else 1)
            if name == "gpsimd":
                for c in self.chans:
                    tot = self.totals.get(c, 0)
                    if tot:
                        eng_obj.wait_ge(sems[c], tot * (16 if c[0] == "dma" else 1))

        with nc.Block() as block:
            block.sync(lambda e: run(e, "sync"))
            block.tensor(lambda e: run(e, "tensor"))
            block.vector(lambda e: run(e, "vector"))
            block.scalar(lambda e: run(e, "scalar"))
            block.gpsimd(lambda e: run(e, "gpsimd"))
        stack.close()


class KB:
    def __init__(self, taps=()):
        import ml_dtypes
        ml = ml_dtypes
        nc = bass.Bass("TRN2", target_bir_lowering=False)
        self.nc = nc
        self.taps = set(taps)
        self.tap_d = {}
        self.ctx = contextlib.ExitStack()

        def din(name, shape):
            return nc.dram_tensor(name, list(shape), F32, kind="ExternalInput").ap()

        self.x_in = din("x", (T, D))
        self.w_gate = din("w_gate", (D, NM))
        self.w_q = din("w_q", (D, DK))
        self.w_k = din("w_k", (NM, D, DK))
        self.w_v = din("w_v", (NM, D, DV))
        self.w_gk1 = din("w_gk1", (NM, D, 16))
        self.w_gk2 = din("w_gk2", (NM, 16, DK))
        self.b_gk = din("b_gk", (NM, DK))
        self.w_ks = din("w_ks", (D, DK))
        self.w_vs = din("w_vs", (D, DV))
        self.w_gk1s = din("w_gk1s", (D, 16))
        self.w_gk2s = din("w_gk2s", (16, DK))
        self.b_gks = din("b_gks", (1, DK))
        self.w_g = din("w_g", (D, DV))
        self.w_o = din("w_o", (DV, D))
        self.norm_w = din("norm_w", (1, HDV))
        self.fw_gshard = din("fw_gshard", (512, D))
        self.fw_ushard = din("fw_ushard", (512, D))
        self.fw_dshard = din("fw_dshard", (128, I_FFN))
        self.f_s_in = din("f_s_in", (1, 1))
        self.f_s_in_down = din("f_s_in_down", (1, 1))
        self.a_W = {n: din("a_W" + n, (D, D)) for n in "qkvgo"}
        self.a_b = {n: din("a_b" + n, (1, D)) for n in "qkvgo"}
        self.m_gate_w = din("m_gate_w", (E16, D))
        self.m_ebias = din("m_ebias", (1, E16))
        self.m_wg = din("m_wg", (EPC, I_EXP, D))
        self.m_wu = din("m_wu", (EPC, I_EXP, D))
        self.m_wd = din("m_wd", (EPC, D, I_EXP))
        self.m_sg = din("m_sg", (I_EXP, D))
        self.m_su = din("m_su", (I_EXP, D))
        self.m_sd = din("m_sd", (D, I_EXP))
        self.smask = din("smask", (128, 16))
        self.ssel = din("ssel", (128, 4))
        self.cmask = din("cmask", (4, 2))
        self.esel = din("esel", (16, EPC))
        self.out_d = nc.dram_tensor("out", [T, D], F32, kind="ExternalOutput").ap()

        tri = np.triu(np.ones((128, 128), np.float32))
        self.c_utri = nc.inline_tensor(np.ascontiguousarray(tri), name="c_utri").ap()
        self.c_utri_gk = nc.inline_tensor(
            np.ascontiguousarray((tri * np.float32(-1.0 / GATE_NORM)).astype(np.float32)),
            name="c_utri_gk").ap()
        self.c_ident = nc.inline_tensor(np.eye(128, dtype=ml.bfloat16), name="c_ident").ap()
        self.c_identf = nc.inline_tensor(np.eye(128, dtype=np.float32), name="c_identf").ap()
        self.c_ptmask = nc.inline_tensor(np.ascontiguousarray(tri).astype(ml.bfloat16),
                                         name="c_ptmask").ap()
        self.c_ones1f = nc.inline_tensor(np.ones((1, 128), np.float32), name="c_ones1f").ap()
        self.c_ones1b = nc.inline_tensor(np.ones((1, 128), ml.bfloat16), name="c_ones1b").ap()
        self.c_ones512b = nc.inline_tensor(np.ones((1, 512), ml.bfloat16), name="c_ones512b").ap()
        self.c_onescf = nc.inline_tensor(np.ones((128, 1), np.float32), name="c_onescf").ap()
        self.CBVALS = [LN_EPS, 1.0 / D, -0.5, 1.0 / HDV]
        cb_np = np.broadcast_to(np.array(self.CBVALS, np.float32)[None, :],
                                (128, len(self.CBVALS))).copy()
        self.c_cbank = nc.inline_tensor(cb_np, name="c_cbank").ap()

        def dscr(name, shape, dt=F32):
            return nc.dram_tensor(name, list(shape), dt, kind="Internal").ap()

        self.d_cnt_in = dscr("d_cnt_in", (1, 8))
        self.d_cnt_out = dscr("d_cnt_out", (1, 32))
        self.d_stg_in = dscr("d_stg_in", (128, 5 * 1024), BF16)
        self.d_stg_out = dscr("d_stg_out", (4, 128, 5 * 1024), BF16)
        self.d_stgA_in = dscr("d_stgA_in", (128, 20))
        self.d_stgA_out = dscr("d_stgA_out", (4, 128, 20))
        self.d_wmax_in = dscr("d_wmax_in", (1, 8))
        self.d_wmax_out = dscr("d_wmax_out", (1, 8))
        self.d_gwq_in = dscr("d_gwq_in", (D, 512), BF16)
        self.d_gwq = dscr("d_gwq", (8, D, 512), BF16)
        self.d_uwq_in = dscr("d_uwq_in", (D, 512), BF16)
        self.d_uwq = dscr("d_uwq", (8, D, 512), BF16)
        self.d_dwq_in = dscr("d_dwq_in", (I_FFN, 128), BF16)
        self.d_dwq = dscr("d_dwq", (8, I_FFN, 128), BF16)
        self.d_xqT = dscr("d_xqT", (NT, 128, DCH * 128), BF16)
        self.d_kv_in = dscr("d_kv_in", (64, 1024))
        self.d_kv_out = dscr("d_kv_out", (64, 1024))
        self.d_xnt_in = dscr("d_xnt_in", (D, T), BF16)
        self.d_xnt = dscr("d_xnt", (8, D, T), BF16)
        self.d_c_in = dscr("d_c_in", (T, E16))
        self.d_c_out = dscr("d_c_out", (8, T, E16))
        self.d_y_in = dscr("d_y_in", (8 * T, D), BF16)
        self.d_y_out = dscr("d_y_out", (T, D), BF16)
        self.d_ewT = dscr("d_ewT", (EPC, 3, I_EXP, D), BF16)
        self.d_swT = dscr("d_swT", (3, I_EXP, D), BF16)

        self.sb = {}

        def S(name, shape, dt=F32):
            t = self.ctx.enter_context(nc.sbuf_tensor(name, list(shape), dt))
            self.sb[name] = t
            return t
        self.S = S

        self.stream = S("stream", (128, NT * D))
        self.nT = S("nT", (128, DCH * T), BF16)
        self.sts = [S(f"st{i}", (128, H * HDV), BF16) for i in range(5)]
        self.XA = S("XA", (128, 24576), BF16)
        self.WS0 = S("WS0", (128, 4096), BF16)
        self.WS1 = S("WS1", (128, 4096), BF16)
        self.WS2 = S("WS2", (128, 4096), BF16)

        self.mrs = S("mrs", (128, 64))
        self.rt = S("rt", (128, 768))
        self.sm = S("sm", (128, 256))
        self.t1a = S("t1a", (32, 512), BF16)
        self.wgk2a = S("wgk2a", (32, 512), BF16)
        self.wgk1b = S("wgk1b", (128, 128), BF16)
        self.brow = {n: S("brow_" + n, (1, 1024), BF16) for n in "qkvgo"}
        self.bgkrow = S("bgkrow", (1, 512), BF16)
        self.bcol = S("bcol", (128, 32))
        self.gwt = S("gwt", (128, 128), BF16)
        self.rowb = S("rowb", (1, 128))
        self.wgate_b = S("wgate_b", (128, 32), BF16)
        self.cst_u = S("cst_u", (128, 128))
        self.cst_ug = S("cst_ug", (128, 128))
        self.cst_i = S("cst_i", (128, 128), BF16)
        self.cst_if = S("cst_if", (128, 128))
        self.cst_pt = S("cst_pt", (128, 128), BF16)
        self.cst_o1f = S("cst_o1f", (1, 128))
        self.cst_o1b = S("cst_o1b", (1, 128), BF16)
        self.cst_ocf = S("cst_ocf", (128, 1))
        self.nwb = S("nwb", (128, 256))
        self.msk_s = S("msk_s", (128, 16))
        self.msk_sel = S("msk_sel", (128, 4))
        self.msk_c = S("msk_c", (4, 2))
        self.esel_sb = S("esel_sb", (16, EPC))
        self.cnt4 = S("cnt4", (4, 8))
        self.wmax_row = S("wmax_row", (1, 8))
        self.srow = S("srow", (1, 8))
        self.nw_row = S("nw_row", (1, 256))
        self.cbank = S("cbank", (128, len(self.CBVALS)))
        self.junk = S("junk", (128, 256))
        for i, v in enumerate(self.CBVALS):
            nc.const_aps.register(F32, float(v), self.cbank[:, i:i + 1])

        def xav(off_b, cols, dt):
            es = 4 if dt == F32 else 2
            ap = self.XA[:, off_b // 2: (off_b + cols * es) // 2].bitcast(dt)
            g0, g1 = off_b // 4096, (off_b + cols * es - 1) // 4096
            return ap, tuple(f"xa{g}" for g in range(g0, g1 + 1))
        self.xav = xav

        self.oacc, self.K_oacc = xav(0, 4096, F32)
        self.qfm, self.K_qfm = xav(16384, 2048, BF16)
        self.kfm, self.K_kfm = xav(20480, 1024, BF16)
        self.eab, self.K_eab = xav(22528, 512, BF16)
        self.eib, self.K_eib = xav(23552, 512, BF16)
        self.vtok, self.K_vtok = xav(24576, 2048, BF16)
        self.gktok, self.K_gktok = xav(28672, 1024, F32)
        self.qcb, self.K_qcb = xav(32768, 512, BF16)
        self.kpb, self.K_kpb = xav(33792, 512, BF16)
        self.kptb, self.K_kptb = xav(34816, 512, BF16)
        self.ptb, self.K_ptb = xav(35840, 512, BF16)
        self.tmpA, self.K_tmpA = xav(36864, 1024, F32)
        self.tmpB, self.K_tmpB = xav(40960, 1024, F32)
        self.tmpC, self.K_tmpC = xav(45056, 1024, BF16)
        self.tmpD, self.K_tmpD = xav(47104, 1024, BF16)
        self.wstg, self.K_wstg = xav(36864, 2048, F32)
        self.xtr, self.K_xtr = xav(32768, 1024, BF16)
        self.yacc, self.K_yacc = xav(0, 4096, F32)
        self.xqt_s, self.K_xqt = xav(16384, 1024, BF16)
        self.f_sil, self.K_fsil = xav(18432, 512, F32)
        self.hqt_s, self.K_hqt = xav(20480, 2048, BF16)
        self.f_hh, self.K_fhh = xav(24576, 2048, F32)
        self.f_am, self.K_fam = xav(32768, 128, F32)
        self.f_rc, self.K_frc = xav(33280, 128, F32)
        self.f_w1, self.K_fw1 = xav(33792, 128, F32)
        self.f_w2, self.K_fw2 = xav(34304, 128, F32)
        self.f_bs1, self.K_fbs1 = xav(34816, 2048, BF16)
        self.f_ph, self.K_fph = xav(38912, 2048, BF16)
        self.l1q, self.K_l1q = xav(0, 4096, BF16)
        self.l1kf, self.K_l1kf = xav(8192, 1024, BF16)
        self.l1vf, self.K_l1vf = xav(10240, 1024, BF16)
        self.l1e1, self.K_l1e1 = xav(12288, 1024, F32)
        self.l1e2, self.K_l1e2 = xav(16384, 1024, F32)
        self.l1sg, self.K_l1sg = xav(20480, 1024, F32)
        self.l1ga, self.K_l1ga = xav(24576, 1024, BF16)
        self.l1gt, self.K_l1gt = xav(26624, 1024, BF16)
        self.e_hT, self.K_ehT = xav(4096, 1024, BF16)
        self.e_cmy, self.K_ecmy = xav(6144, 256, F32)
        self.e_xnt, self.K_ex = xav(8192, 1024, BF16)
        self.e_h, self.K_eh = xav(10240, 1024, BF16)
        self.e_sil, self.K_esil = xav(12288, 1024, F32)
        self.e_wu, self.K_ewu = xav(16384, 8192, BF16)
        self.e_wg2, self.K_ewg2 = xav(32768, 4096, BF16)
        self.e_y, self.K_ey = xav(40960, 1024, BF16)
        self.e_st, self.K_est = xav(43008, 1024, BF16)
        _kv, self.K_kvb = xav(28672, 1024, F32)
        self.kvb = _kv[0:64, :]
        self.kvb2, self.K_kvb2 = xav(32768, 1024, BF16)
        self.rowc = self.wstg[0:1, 0:1024]
        self.K_rowc = self.K_wstg

        self.W0 = self.ctx.enter_context(nc.psum_tensor("W0", [128, 1024], F32))
        self.W1 = self.ctx.enter_context(nc.psum_tensor("W1", [128, 1024], F32))
        self.PA = self.ctx.enter_context(nc.psum_tensor("PA", [128, 512], F32))
        self.PB = self.ctx.enter_context(nc.psum_tensor("PB", [128, 512], F32))
        self.PC = self.ctx.enter_context(nc.psum_tensor("PC", [128, 512], F32))
        self.PD = self.ctx.enter_context(nc.psum_tensor("PD", [128, 512], F32))
        self.PCb = self.PC[:, :].bitcast(BF16)
        self.PDb = self.PD[:, :].bitcast(BF16)

        self.p = Prog(nc)

        def _gap(eng):
            j = self.junk
            if eng == "scalar":
                return lambda e: e.copy(j[:, 128:256], j[:, 0:128])
            return lambda e: e.tensor_copy(j[:, 128:256], j[:, 0:128])
        self.p.gap_fn = _gap

    def mm(self, out, lhsT, rhs, start, stop, reads, writes, tag=""):
        self.p.emit("tensor",
                    lambda e, o=out, l=lhsT, r=rhs, s=start, t=stop:
                        e.matmul(o, l, r, start=s, stop=t),
                    reads=tuple(reads), writes=tuple(writes), tag=tag)

    def tr(self, out, in_, reads, writes, tag="", f32=False):
        idn = self.cst_if if f32 else self.cst_i
        ikey = "cst_if" if f32 else "cst_i"
        kp = in_.shape[0]
        self.p.emit("tensor",
                    lambda e, o=out, i=in_, n=idn, k=kp: e.transpose(o, i, n[0:k, 0:k]),
                    reads=tuple(reads) + (ikey,), writes=tuple(writes), tag=tag)

    def act(self, out, in_, func, reads, writes, bias=0.0, scale=1.0, accum=None, tag=""):
        w = 1 if accum is not None else out.free_size()
        self.p.emit("scalar",
                    lambda e, o=out, i=in_, f=func, b=bias, s=scale, a=accum:
                        e.activation(o, i, f, bias=b, scale=s, accum_out=a),
                    reads=tuple(reads), writes=tuple(writes), tag=tag, outw=w)

    def ts(self, out, in0, s1, s2, op0, op1=None, reads=(), writes=(), eng="vector", tag=""):
        w = out.free_size()
        if s2 is None:
            self.p.emit(eng, lambda e, o=out, i=in0, a=s1, p0=op0:
                        e.tensor_scalar(o, i, a, None, p0),
                        reads=tuple(reads), writes=tuple(writes), tag=tag, outw=w)
        else:
            self.p.emit(eng, lambda e, o=out, i=in0, a=s1, b=s2, p0=op0, p1=op1:
                        e.tensor_scalar(o, i, a, b, p0, p1),
                        reads=tuple(reads), writes=tuple(writes), tag=tag, outw=w)

    def tt(self, out, in0, in1, op, reads, writes, eng="vector", tag=""):
        self.p.emit(eng, lambda e, o=out, a=in0, b=in1, q=op: e.tensor_tensor(o, a, b, q),
                    reads=tuple(reads), writes=tuple(writes), tag=tag,
                    outw=out.free_size())

    def stt(self, out, in0, s, in1, op0, op1, reads, writes, eng="vector", tag=""):
        self.p.emit(eng, lambda e, o=out, a=in0, c=s, b=in1, p0=op0, p1=op1:
                    e.scalar_tensor_tensor(o, a, c, b, p0, p1),
                    reads=tuple(reads), writes=tuple(writes), tag=tag,
                    outw=out.free_size())

    def cp(self, out, in_, reads, writes, eng="vector", tag=""):
        w = out.free_size()
        if eng == "scalar":
            self.p.emit(eng, lambda e, o=out, i=in_: e.copy(o, i),
                        reads=tuple(reads), writes=tuple(writes), tag=tag, outw=w)
        else:
            self.p.emit(eng, lambda e, o=out, i=in_: e.tensor_copy(o, i),
                        reads=tuple(reads), writes=tuple(writes), tag=tag, outw=w)

    def red(self, out, in_, op, reads, writes, absval=False, tag=""):
        self.p.emit("vector", lambda e, o=out, i=in_, q=op, a=absval:
                    e.tensor_reduce(o, i, AX.X, q, apply_absolute_value=a),
                    reads=tuple(reads), writes=tuple(writes), tag=tag,
                    outw=0)

    def recip(self, out, in_, reads, writes, tag=""):
        self.p.emit("vector", lambda e, o=out, i=in_: e.reciprocal(o, i),
                    reads=tuple(reads), writes=tuple(writes), tag=tag, outw=0)

    def mset(self, ap, val, writes, eng="vector", tag=""):
        self.p.emit(eng, lambda e, a=ap, v=val: e.memset(a, v),
                    reads=(), writes=tuple(writes), tag=tag, outw=ap.free_size())

    def scan(self, out, d0, d1, init, op0, op1, reads, writes, tag=""):
        self.p.emit("vector", lambda e, o=out, a=d0, b=d1, i=init, p0=op0, p1=op1:
                    e.tensor_tensor_scan(o, a, b, i, p0, p1),
                    reads=tuple(reads), writes=tuple(writes), tag=tag, outw=0)

    def load_cast(self, dram_ap, dst, dkeys, tag="lc"):
        rows = dram_ap.shape[-2] if dram_ap.ndim >= 2 else 1
        cols = dram_ap.shape[-1]
        stage = self.wstg[0:rows, 0:cols]
        self.p.dma("sync", stage, dram_ap, reads=(), writes=self.K_wstg, tag=tag)
        self.cp(dst, stage, self.K_wstg, tuple(dkeys), tag=tag)

    def tap(self, name, ap, reads):
        if name in self.taps:
            d = self.nc.dram_tensor("tap_" + name, list(ap.shape), ap.dtype,
                                    kind="ExternalOutput").ap()
            self.tap_d[name] = d
            self.p.dma("gpsimd", d, ap, reads=tuple(reads), writes=(), tag="tap")

    def ph_consts(self):
        p, S = self.p, self
        p.dma("sync", S.stream[:, :].rearrange("p (t d) -> p t d", d=D),
              S.x_in.rearrange("(t p) d -> p t d", p=128), writes=("stream",), tag="ldx")
        for dst, src, k in ((S.cst_u, S.c_utri, "cst_u"), (S.cst_ug, S.c_utri_gk, "cst_ug"),
                            (S.cst_i, S.c_ident, "cst_i"), (S.cst_if, S.c_identf, "cst_if"),
                            (S.cst_pt, S.c_ptmask, "cst_pt"), (S.cst_o1f, S.c_ones1f, "cst_o1f"),
                            (S.cst_o1b, S.c_ones1b, "cst_o1b"), (S.cst_ocf, S.c_onescf, "cst_ocf"),
                            (S.msk_s, S.smask, "msk_s"), (S.msk_sel, S.ssel, "msk_sel"),
                            (S.msk_c, S.cmask, "msk_c"), (S.esel_sb, S.esel, "esel_sb"),
                            (S.nw_row, S.norm_w, "nw_row")):
            p.dma("sync", dst[:, :], src, writes=(k,))
        p.dma("sync", S.cbank[:, :], S.c_cbank, writes=("cbank",))
        S.cp(S.sm[:, 255:256], S.cbank[:, 0:1], ("cbank",), ("sm",), eng="scalar")
        S.cp(S.sm[:, 254:255], S.cbank[:, 0:1], ("cbank",), ("sm",), eng="vector")
        p.dma("sync", S.srow[:, 0:1], S.f_s_in, writes=("srow",))
        p.dma("sync", S.srow[:, 1:2], S.f_s_in_down, writes=("srow",))
        S.mm(S.W0[:, 0:256], S.cst_o1f[:, :], S.nw_row[:, :], True, True,
             ("cst_o1f", "nw_row"), ("W0",))
        S.cp(S.nwb[:, :], S.W0[:, 0:256], ("W0",), ("nwb",))
        p.dma("sync", S.t1a[16:17, :], S.c_ones512b, writes=("t1a_one",))

    def ph_wamax(self):
        p, S, sm = self.p, self, self.sm

        def shard_amax(shard_ap, nrow, ncol, col):
            for i, r0 in enumerate(range(0, nrow, 128)):
                npiece = min(ncol, 2048)
                for j, c0 in enumerate(range(0, ncol, npiece)):
                    stage = S.wstg[:, 0:npiece]
                    p.dma("sync", stage, shard_ap[r0:r0 + 128, c0:c0 + npiece],
                          writes=S.K_wstg, tag="wamax")
                    dst = sm[:, col:col + 1] if (i == 0 and j == 0) else sm[:, col + 4:col + 5]
                    S.red(dst, stage, ALU.max, S.K_wstg, ("sm",), absval=True)
                    if not (i == 0 and j == 0):
                        S.tt(sm[:, col:col + 1], sm[:, col:col + 1], sm[:, col + 4:col + 5],
                             ALU.max, ("sm",), ("sm",))

        shard_amax(S.fw_gshard, 512, D, 0)
        shard_amax(S.fw_ushard, 512, D, 1)
        shard_amax(S.fw_dshard, 128, I_FFN, 2)
        S.tr(S.PA[0:4, 0:128], sm[:, 0:4], ("sm",), ("PA",), f32=True)
        S.red(sm[0:4, 8:9], S.PA[0:4, 0:128], ALU.max, ("PA",), ("sm",))
        p.dma("gpsimd", S.d_wmax_in[:, 0:4], sm[0:4, 8:9],
              reads=("sm",), writes=("d_wmax_in",))
        p.cc("AllReduce", ALU.max, GRP8, S.d_wmax_in, S.d_wmax_out,
             reads=("d_wmax_in",), writes=("d_wmax_out",), tag="wAR")
        p.dma("sync", S.wmax_row[:, :], S.d_wmax_out, reads=("d_wmax_out",),
              writes=("wmax_row",))
        S.mm(S.PA[:, 0:8], S.cst_o1f[:, :], S.wmax_row[:, :], True, True,
             ("cst_o1f", "wmax_row"), ("PA",))
        S.cp(sm[:, 16:24], S.PA[:, 0:8], ("PA",), ("sm",))
        S.mm(S.PA[:, 8:16], S.cst_o1f[:, :], S.srow[:, :], True, True,
             ("cst_o1f", "srow"), ("PA",))
        S.cp(sm[:, 24:32], S.PA[:, 8:16], ("PA",), ("sm",))
        for j in range(3):
            S.recip(sm[:, 32 + j:33 + j], sm[:, 16 + j:17 + j], ("sm",), ("sm",))
            S.ts(sm[:, 32 + j:33 + j], sm[:, 32 + j:33 + j], FP8_MAX * FP4_MAX, None,
                 ALU.mult, reads=("sm",), writes=("sm",))
            S.ts(sm[:, 40 + j:41 + j], sm[:, 32 + j:33 + j], 1.0 / 6.0, None,
                 ALU.mult, reads=("sm",), writes=("sm",))
        for j, gsa in enumerate((24, 24, 25)):
            S.recip(sm[:, 35 + j:36 + j], sm[:, gsa:gsa + 1], ("sm",), ("sm",))
            S.tt(sm[:, 35 + j:36 + j], sm[:, 35 + j:36 + j], sm[:, 16 + j:17 + j],
                 ALU.mult, ("sm",), ("sm",))
            S.ts(sm[:, 35 + j:36 + j], sm[:, 35 + j:36 + j],
                 1.0 / (FP8_MAX * FP4_MAX), None, ALU.mult, reads=("sm",), writes=("sm",))
        S.ts(sm[:, 38:39], sm[:, 24:25], 1.0 / 6.0, None, ALU.mult,
             reads=("sm",), writes=("sm",))
        S.ts(sm[:, 39:40], sm[:, 25:26], 1.0 / 6.0, None, ALU.mult,
             reads=("sm",), writes=("sm",))

    def e4m3_round(self, io_ap, w1, w2, keys):
        S = self
        S.ts(w1, io_ap, VELT, None, ALU.mult, reads=keys, writes=keys)
        S.tt(w2, w1, io_ap, ALU.subtract, keys, keys)
        S.tt(w1, w1, w2, ALU.subtract, keys, keys)
        S.ts(w2, io_ap, K_SUBN, None, ALU.add, reads=keys, writes=keys)
        S.ts(w2, w2, K_SUBN, None, ALU.subtract, reads=keys, writes=keys)
        S.ts(io_ap, io_ap, SUBN, None, ALU.is_ge, reads=keys, writes=keys)
        S.tt(w1, w1, w2, ALU.subtract, keys, keys)
        S.tt(w1, w1, io_ap, ALU.mult, keys, keys)
        S.tt(io_ap, w1, w2, ALU.add, keys, keys)

    def fp4_quant(self, x_ap, n, gs6_col, gs_col, sc8, rcp, w1, w2,
                  out_bf, s1, keys):
        S = self
        nb = n // 16
        keys = tuple(keys) + ("sm",)
        S.red(sc8, x_ap.rearrange("p (b k) -> p b k", k=16), ALU.max, keys, keys,
              absval=True)
        S.ts(sc8, sc8, gs6_col, FP8_MAX, ALU.mult, ALU.min, reads=keys, writes=keys)
        S.e4m3_round(sc8, w1, w2, keys)
        S.ts(w1, sc8, 0.0, None, ALU.is_equal, reads=keys, writes=keys)
        S.tt(w2, sc8, w1, ALU.add, keys, keys)
        S.recip(rcp, w2, keys, keys)
        S.ts(w2, w1, -1.0, 1.0, ALU.mult, ALU.add, reads=keys, writes=keys)
        S.stt(w2, w2, gs_col, w1, ALU.mult, ALU.add, keys, keys)
        S.tt(rcp, rcp, w2, ALU.mult, keys, keys)
        rb = rcp.rearrange("p b -> p b ()").broadcast_to([128, nb, 16])
        S.tt(x_ap.rearrange("p (b k) -> p b k", k=16),
             x_ap.rearrange("p (b k) -> p b k", k=16), rb, ALU.mult, keys, keys)
        S.ts(x_ap, x_ap, FP4_MAX, -FP4_MAX, ALU.min, ALU.max, reads=keys, writes=keys)

        def rne(dst, K):
            S.ts(dst, x_ap, K, None, ALU.add, reads=keys, writes=keys)
            S.ts(dst, dst, K, None, ALU.subtract, reads=keys, writes=keys)

        def clip(dst, lim):
            S.ts(dst, dst, lim, -lim, ALU.min, ALU.max, reads=keys, writes=keys)

        rne(s1, K_HALF); clip(s1, 2.0)
        rne(out_bf, K_ONE); clip(out_bf, 4.0)
        S.tt(s1, s1, out_bf, ALU.add, keys, keys)
        clip(out_bf, 2.0)
        S.tt(s1, s1, out_bf, ALU.subtract, keys, keys)
        rne(out_bf, K_TWO); clip(out_bf, 6.0)
        S.tt(s1, s1, out_bf, ALU.add, keys, keys)
        clip(out_bf, 4.0)
        S.tt(s1, s1, out_bf, ALU.subtract, keys, keys)
        sb8 = sc8.rearrange("p b -> p b ()").broadcast_to([128, nb, 16])
        S.tt(out_bf.rearrange("p (b k) -> p b k", k=16),
             s1.rearrange("p (b k) -> p b k", k=16), sb8, ALU.mult, keys, keys)

    def ph_ln(self):
        S = self
        for t in range(NT):
            xs = S.stream[:, t * D:(t + 1) * D]
            S.red(S.sm[:, 60:61], xs, ALU.add, ("stream",), ("sm",))
            S.ts(S.mrs[:, t:t + 1], S.sm[:, 60:61], 1.0 / D, None,
                 ALU.mult, reads=("sm",), writes=("mrs",))
            S.ts(S.tmpA, xs, S.mrs[:, t:t + 1], None, ALU.subtract,
                 reads=("stream", "mrs"), writes=S.K_tmpA)
            S.act(S.tmpB, S.tmpA, AF.Square, S.K_tmpA, S.K_tmpB + ("sm",),
                  accum=S.sm[:, 61:62])
            S.act(S.sm[:, 62:63], S.sm[:, 61:62], AF.Ln, ("sm",), ("sm",),
                  scale=1.0 / D, bias=LN_EPS)
            S.act(S.mrs[:, 16 + t:17 + t], S.sm[:, 62:63], AF.Exp,
                  ("sm",), ("mrs",), scale=-0.5)
            S.ts(S.tmpC, S.tmpA, S.mrs[:, 16 + t:17 + t], None, ALU.mult,
                 reads=S.K_tmpA + ("mrs",), writes=S.K_tmpC)
            for c in range(DCH):
                S.tr(S.PCb[:, c * 128:(c + 1) * 128], S.tmpC[:, c * 128:(c + 1) * 128],
                     S.K_tmpC, ("PC",))
            dst = S.nT[:, :].rearrange("p (c k) -> p c k", k=T)[:, :, t * 128:(t + 1) * 128]
            S.cp(dst, S.PCb[:, 0:1024].rearrange("p (c k) -> p c k", k=128),
                 ("PC",), ("nT",))
        S.tap("nT", S.nT[:, :], ("nT",))
        S.tap("mrs", S.mrs[:, :], ("mrs",))

    def ph_routing(self):
        S, p, sm, rt = self, self.p, self.sm, self.rt
        LG, RK, KP, WW, RR, EE = 0, 64, 128, 192, 256, 320
        self.RT_LG, self.RT_RK, self.RT_KP, self.RT_WW = LG, RK, KP, WW
        for c in range(DCH):
            S.load_cast(S.w_gate[c * 128:(c + 1) * 128, :],
                        S.wgate_b[:, c * 4:(c + 1) * 4], ("wgate_b",), tag="wgate")
        for t in range(NT):
            for c in range(DCH):
                S.mm(S.PD[:, 0:4], S.nT[:, c * T + t * 128: c * T + (t + 1) * 128],
                     S.wgate_b[:, c * 4:(c + 1) * 4], c == 0, c == DCH - 1,
                     ("nT", "wgate_b"), ("PD",))
            dst = rt[:, LG:LG + 64].rearrange("p (m k) -> p m k", k=16)[:, :, t:t + 1]
            S.cp(dst, S.PD[:, 0:4].rearrange("p (m o) -> p m o", o=1),
                 ("PD",), ("rt",))
        for m in range(NM):
            first = True
            for j in range(NM):
                if j == m:
                    continue
                S.tt(rt[:, EE:EE + 16], rt[:, LG + j * 16:LG + j * 16 + 16],
                     rt[:, LG + m * 16:LG + m * 16 + 16], ALU.is_gt,
                     ("rt",), ("rt",))
                if first:
                    S.cp(rt[:, RK + m * 16:RK + m * 16 + 16], rt[:, EE:EE + 16],
                         ("rt",), ("rt",))
                    first = False
                else:
                    S.tt(rt[:, RK + m * 16:RK + m * 16 + 16],
                         rt[:, RK + m * 16:RK + m * 16 + 16], rt[:, EE:EE + 16],
                         ALU.add, ("rt", "rt"), ("rt",))
                if j < m:
                    S.tt(rt[:, EE:EE + 16], rt[:, LG + j * 16:LG + j * 16 + 16],
                         rt[:, LG + m * 16:LG + m * 16 + 16], ALU.is_equal,
                         ("rt",), ("rt",))
                    S.tt(rt[:, RK + m * 16:RK + m * 16 + 16],
                         rt[:, RK + m * 16:RK + m * 16 + 16], rt[:, EE:EE + 16],
                         ALU.add, ("rt", "rt"), ("rt",))
        S.ts(rt[:, RR:RR + 64], rt[:, RK:RK + 64], 1.5, None, ALU.is_lt,
             reads=("rt",), writes=("rt",))
        S.tt(rt[:, EE:EE + 16], rt[:, LG:LG + 16], rt[:, LG + 16:LG + 32], ALU.max,
             ("rt",), ("rt",))
        S.tt(sm[:, 64:80], rt[:, LG + 32:LG + 48], rt[:, LG + 48:LG + 64], ALU.max,
             ("rt",), ("sm",))
        S.tt(rt[:, EE:EE + 16], rt[:, EE:EE + 16], sm[:, 64:80], ALU.max,
             ("rt", "sm"), ("rt",))
        for m in range(NM):
            S.tt(rt[:, WW + m * 16:WW + m * 16 + 16],
                 rt[:, LG + m * 16:LG + m * 16 + 16], rt[:, EE:EE + 16],
                 ALU.subtract, ("rt", "rt"), ("rt",))
        S.act(rt[:, WW:WW + 64], rt[:, WW:WW + 64], AF.Exp, ("rt",), ("rt",))
        S.tt(rt[:, WW:WW + 64], rt[:, WW:WW + 64], rt[:, RR:RR + 64], ALU.mult,
             ("rt", "rt"), ("rt",))
        S.tt(sm[:, 64:80], rt[:, WW:WW + 16], rt[:, WW + 16:WW + 32], ALU.add,
             ("rt",), ("sm",))
        S.tt(sm[:, 80:96], rt[:, WW + 32:WW + 48], rt[:, WW + 48:WW + 64], ALU.add,
             ("rt",), ("sm",))
        S.tt(sm[:, 64:80], sm[:, 64:80], sm[:, 80:96], ALU.add, ("sm",), ("sm",))
        S.recip(sm[:, 80:96], sm[:, 64:80], ("sm",), ("sm",))
        for m in range(NM):
            S.tt(rt[:, WW + m * 16:WW + m * 16 + 16],
                 rt[:, WW + m * 16:WW + m * 16 + 16], sm[:, 80:96], ALU.mult,
                 ("rt", "sm"), ("rt",))
        S.mm(S.W0[:, 0:64], S.cst_u[:, :], rt[:, RR:RR + 64], True, False,
             ("cst_u", "rt"), ("W0",))
        S.mm(S.PD[0:1, 0:64], S.cst_ocf[:, :], rt[:, RR:RR + 64], True, True,
             ("cst_ocf", "rt"), ("PD",))
        S.cp(S.rowb[0:1, 0:64], S.PD[0:1, 0:64], ("PD",), ("rowb",))
        for m in range(NM):
            S.scan(S.rowb[0:1, 64 + m * 16 + 1:64 + m * 16 + 16],
                   S.rowb[0:1, m * 16:m * 16 + 15], S.rowb[0:1, m * 16:m * 16 + 15],
                   0.0, ALU.add, ALU.bypass, ("rowb",), ("rowb",))
            S.mset(S.rowb[0:1, 64 + m * 16:64 + m * 16 + 1], 0.0, ("rowb",))
        S.mm(S.W0[:, 0:64], S.cst_o1f[:, :], S.rowb[0:1, 64:128], False, True,
             ("cst_o1f", "rowb"), ("W0",))
        S.cp(rt[:, RK:RK + 64], S.W0[:, 0:64], ("W0",), ("rt",))
        offv = S.rowb[0:1, 64:128].rearrange("a (m k) -> a m k", k=16)[:, :, 15:16]
        totv = S.rowb[0:1, 0:64].rearrange("a (m k) -> a m k", k=16)[:, :, 15:16]
        S.tt(S.rowb[0:1, 124:128].rearrange("a (m o) -> a m o", o=1), offv, totv,
             ALU.add, ("rowb",), ("rowb",))
        p.dma("gpsimd", S.d_cnt_in[:, 0:4], S.rowb[0:1, 124:128],
              reads=("rowb",), writes=("d_cnt_in",))
        p.cc("AllGather", ALU.bypass, GRP4, S.d_cnt_in, S.d_cnt_out,
             reads=("d_cnt_in",), writes=("d_cnt_out",), tag="cntAG")
        p.dma("sync", S.cnt4[:, :], S.d_cnt_out.rearrange("a (r c) -> (a r) c", c=8),
              reads=("d_cnt_out",), writes=("cnt4",))
        S.mm(S.PD[0:4, 0:2], S.cnt4[:, 0:4], S.msk_c[:, :], True, True,
             ("cnt4", "msk_c"), ("PD",))
        S.cp(sm[0:4, 104:106], S.PD[0:4, 0:2], ("PD",), ("sm",))
        S.tt(sm[0:4, 106:107], sm[0:4, 105:106], sm[0:4, 104:105], ALU.subtract,
             ("sm",), ("sm",))
        S.ts(sm[0:4, 106:107], sm[0:4, 106:107], CAP, None, ALU.subtract,
             reads=("sm",), writes=("sm",))
        p.dma("sync", S.rowb[0:1, 96:100], sm[0:4, 106:107],
              reads=("sm",), writes=("rowb",))
        S.mm(S.PD[:, 8:12], S.cst_o1f[:, :], S.rowb[0:1, 96:100], True, True,
             ("cst_o1f", "rowb"), ("PD",))
        S.cp(sm[:, 112:116], S.PD[:, 8:12], ("PD",), ("sm",))
        for m in range(NM):
            S.ts(rt[:, EE:EE + 16], rt[:, RK + m * 16:RK + m * 16 + 16],
                 sm[:, 112 + m:113 + m], None, ALU.is_gt,
                 reads=("rt", "sm"), writes=("rt",))
            S.tt(rt[:, KP + m * 16:KP + m * 16 + 16], rt[:, EE:EE + 16],
                 rt[:, RR + m * 16:RR + m * 16 + 16], ALU.mult,
                 ("rt", "rt"), ("rt",))
            S.tt(rt[:, WW + m * 16:WW + m * 16 + 16],
                 rt[:, WW + m * 16:WW + m * 16 + 16],
                 rt[:, KP + m * 16:KP + m * 16 + 16], ALU.mult,
                 ("rt", "rt"), ("rt",))
        S.tap("rt", rt[:, :], ("rt", "rt", "rt", "rt", "rt"))

    def lane_load_weights(self, lane):
        S = self
        if lane < NM:
            wk = S.w_k[lane]
            wgk1, wgk2 = S.w_gk1[lane], S.w_gk2[lane]
            bgk = S.b_gk[lane:lane + 1, :]
        else:
            wk = S.w_ks
            wgk1, wgk2, bgk = S.w_gk1s, S.w_gk2s, S.b_gks
        for dc in range(DCH):
            S.load_cast(wk[dc * 128:(dc + 1) * 128, :],
                        S.WS0[:, dc * 512:(dc + 1) * 512], ("WS0",), tag="wk")
        stage1 = S.wstg[:, 0:128]
        S.p.dma("sync", stage1.rearrange("p (c r) -> p c r", r=16),
                wgk1.rearrange("(c p) r -> p c r", p=128), writes=S.K_wstg,
                tag="wgk1")
        S.cp(S.wgk1b[:, :], stage1, S.K_wstg, ("wgk1b",), tag="wgk1")
        S.load_cast(wgk2[:, :], S.wgk2a[0:16, :], ("wgk2a",), tag="wgk2")
        S.load_cast(bgk, S.bgkrow[:, :], ("bgkrow",), tag="bgk")
        S.p.dma("sync", S.wgk2a[16:17, :], S.bgkrow[:, :], reads=("bgkrow",),
                writes=("wgk2a",), tag="bgk2")

    def lane_proj(self, lane, sub, hb):
        S = self
        wv = S.w_v[lane] if lane < NM else S.w_vs
        base = sub * 4 + hb * 2
        ncol = sub * SUB + hb * 256
        KPc = self.RT_KP
        for h in range(H):
            for dc in range(DCH):
                S.mm(S.PA[:, 0:256],
                     S.WS0[:, dc * 512 + h * 128: dc * 512 + (h + 1) * 128],
                     S.nT[:, dc * T + ncol: dc * T + ncol + 256],
                     dc == 0, dc == DCH - 1, ("WS0", "nT"), ("PA",))
            S.cp(S.kfm[:, h * 256:(h + 1) * 256], S.PA[:, 0:256], ("PA",), S.K_kfm)
        for vh in range(2):
            for dci in range(4):
                dc = vh * 4 + dci
                S.load_cast(wv[dc * 128:(dc + 1) * 128, :],
                            S.WS1[:, dci * 1024:(dci + 1) * 1024], ("WS1",), tag="wv")
            for vt in range(2):
                W = S.W0 if vt == 0 else S.W1
                wkey = "W0" if vt == 0 else "W1"
                for dci in range(4):
                    dc = vh * 4 + dci
                    lhs = S.nT[:, dc * T + ncol + vt * 128: dc * T + ncol + vt * 128 + 128]
                    for nh in range(2):
                        S.mm(W[:, nh * 512:(nh + 1) * 512], lhs,
                             S.WS1[:, dci * 1024 + nh * 512: dci * 1024 + (nh + 1) * 512],
                             dc == 0, dc == DCH - 1, ("nT", "WS1"), (wkey,))
        for vt in range(2):
            W = S.W0 if vt == 0 else S.W1
            wkey = "W0" if vt == 0 else "W1"
            if lane < NM:
                kcol = S.rt[:, KPc + lane * 16 + base + vt: KPc + lane * 16 + base + vt + 1]
                S.ts(S.vtok[:, vt * 1024:(vt + 1) * 1024], W[:, :], kcol, None,
                     ALU.mult, reads=(wkey, "rt"), writes=S.K_vtok)
            else:
                S.cp(S.vtok[:, vt * 1024:(vt + 1) * 1024], W[:, :], (wkey,), S.K_vtok)
        for dc in range(DCH):
            S.mm(S.PD[0:16, 0:256], S.wgk1b[:, dc * 16:(dc + 1) * 16],
                 S.nT[:, dc * T + ncol: dc * T + ncol + 256],
                 dc == 0, dc == DCH - 1, ("wgk1b", "nT"), ("PD",))
        S.cp(S.t1a[0:16, 0:256], S.PD[0:16, 0:256], ("PD",), ("t1a",))
        for vt in range(2):
            S.mm(S.PC[:, :], S.t1a[0:17, vt * 128:(vt + 1) * 128],
                 S.wgk2a[0:17, :], True, True, ("t1a", "t1a_one", "wgk2a"), ("PC",))
            gslice = S.gktok[:, vt * 512:(vt + 1) * 512]
            S.act(gslice, S.PC[:, :], AF.Exp, ("PC",), S.K_gktok, scale=-1.0)
            S.ts(gslice, gslice, 1.0, None, ALU.add, reads=S.K_gktok, writes=S.K_gktok)
            S.act(gslice, gslice, AF.Ln, S.K_gktok, S.K_gktok)
            if lane < NM:
                kcol = S.rt[:, KPc + lane * 16 + base + vt: KPc + lane * 16 + base + vt + 1]
                S.ts(gslice, gslice, kcol, None, ALU.mult,
                     reads=S.K_gktok + ("rt",), writes=S.K_gktok)

    def lane_chunk(self, lane, sub, hb, c, mainpass):
        S = self
        base = sub * 4 + hb * 2
        g = base + c
        st = S.sts[lane]
        stk = f"st{lane}"
        for h in range(H):
            S.mm(S.PA[:, h * 128:(h + 1) * 128],
                 S.gktok[:, c * 512 + h * 128: c * 512 + (h + 1) * 128],
                 S.cst_ug[:, :], True, True, S.K_gktok + ("cst_ug",), ("PA",))
        aC = S.PA[:, :].rearrange("p (h k) -> p h k", k=128)[:, :, 127:128]
        S.act(S.sm[:, 52:56].rearrange("p (h o) -> p h o", o=1), aC, AF.Identity,
              ("PA",), ("sm",))
        S.act(S.sm[:, 48:52], S.sm[:, 52:56], AF.Exp, ("sm",), ("sm",))
        if mainpass:
            S.act(S.eab, S.PA[:, :], AF.Exp, ("PA",), S.K_eab)
            S.act(S.eib, S.PA[:, :], AF.Exp, ("PA",), S.K_eib, scale=-1.0)
            qv = S.qfm.rearrange("p (h k) -> p h k", k=512)[:, :, hb * 256 + c * 128: hb * 256 + c * 128 + 128]
            S.tt(S.qcb.rearrange("p (h k) -> p h k", k=128), qv,
                 S.eab.rearrange("p (h k) -> p h k", k=128), ALU.mult,
                 S.K_qfm + S.K_eab, S.K_qcb)
            kv_ = S.kfm.rearrange("p (h k) -> p h k", k=256)[:, :, c * 128:(c + 1) * 128]
            S.tt(S.kpb.rearrange("p (h k) -> p h k", k=128), kv_,
                 S.eib.rearrange("p (h k) -> p h k", k=128), ALU.mult,
                 S.K_kfm + S.K_eib, S.K_kpb)
            for h in range(H):
                S.mm(S.PB[:, h * 128:(h + 1) * 128], S.kpb[:, h * 128:(h + 1) * 128],
                     S.qcb[:, h * 128:(h + 1) * 128], True, True,
                     S.K_kpb + S.K_qcb, ("PB",))
            ptm = S.cst_pt[:, :].rearrange("p (o k) -> p o k", o=1).broadcast_to([128, 4, 128])
            S.tt(S.ptb.rearrange("p (h k) -> p h k", k=128),
                 S.PB[:, :].rearrange("p (h k) -> p h k", k=128), ptm, ALU.mult,
                 ("PB", "cst_pt"), S.K_ptb)
            for h in range(H):
                S.mm(S.W0[:, h * 256:(h + 1) * 256], S.ptb[:, h * 128:(h + 1) * 128],
                     S.vtok[:, c * 1024 + h * 256: c * 1024 + (h + 1) * 256],
                     True, False, S.K_ptb + S.K_vtok, ("W0",))
                S.mm(S.W0[:, h * 256:(h + 1) * 256], S.qcb[:, h * 128:(h + 1) * 128],
                     st[:, h * 256:(h + 1) * 256], False, True,
                     S.K_qcb + (stk,), ("W0",))
            for h in range(H):
                S.ts(S.kpb[:, h * 128:(h + 1) * 128], S.kpb[:, h * 128:(h + 1) * 128],
                     S.sm[:, 48 + h:49 + h], None, ALU.mult,
                     reads=S.K_kpb + ("sm",), writes=S.K_kpb)
        else:
            for h in range(H):
                S.act(S.eab[:, h * 128:(h + 1) * 128], S.PA[:, h * 128:(h + 1) * 128],
                      AF.Exp, ("PA", "sm"), S.K_eab, scale=-1.0,
                      bias=S.sm[:, 52 + h:53 + h])
            kv_ = S.kfm.rearrange("p (h k) -> p h k", k=256)[:, :, c * 128:(c + 1) * 128]
            S.tt(S.kpb.rearrange("p (h k) -> p h k", k=128), kv_,
                 S.eab.rearrange("p (h k) -> p h k", k=128), ALU.mult,
                 S.K_kfm + S.K_eab, S.K_kpb)
            S.tt(S.sm[:, 128 + lane * 4:132 + lane * 4].rearrange("p (h o) -> p h o", o=1),
                 S.sm[:, 128 + lane * 4:132 + lane * 4].rearrange("p (h o) -> p h o", o=1),
                 S.sm[:, 52:56].rearrange("p (h o) -> p h o", o=1), ALU.add,
                 ("sm", "sm"), ("sm",))
        for h in range(H):
            S.tr(S.PCb[:, h * 128:(h + 1) * 128], S.kpb[:, h * 128:(h + 1) * 128],
                 S.K_kpb, ("PC",))
        S.cp(S.kptb, S.PCb[:, 0:512], ("PC",), S.K_kptb)
        for h in range(H):
            S.mm(S.W1[:, h * 256:(h + 1) * 256], S.kptb[:, h * 128:(h + 1) * 128],
                 S.vtok[:, c * 1024 + h * 256: c * 1024 + (h + 1) * 256],
                 True, True, S.K_kptb + S.K_vtok, ("W1",))
        for h in range(H):
            S.stt(st[:, h * 256:(h + 1) * 256], st[:, h * 256:(h + 1) * 256],
                  S.sm[:, 48 + h:49 + h], S.W1[:, h * 256:(h + 1) * 256],
                  ALU.mult, ALU.add, (stk, "sm", "W1"), (stk,))
        if mainpass:
            osl = S.oacc[:, (hb * 2 + c) * 1024:(hb * 2 + c + 1) * 1024]
            if lane == 0:
                wcol = S.rt[:, self.RT_WW + lane * 16 + g: self.RT_WW + lane * 16 + g + 1]
                S.ts(osl, S.W0[:, :], wcol, None, ALU.mult,
                     reads=("W0", "rt"), writes=S.K_oacc)
            elif lane < NM:
                wcol = S.rt[:, self.RT_WW + lane * 16 + g: self.RT_WW + lane * 16 + g + 1]
                S.stt(osl, S.W0[:, :], wcol, osl, ALU.mult, ALU.add,
                      ("W0", "rt") + S.K_oacc, S.K_oacc)
            else:
                S.tt(osl, osl, S.W0[:, :], ALU.add, S.K_oacc + ("W0",), S.K_oacc)

    def ph_prepass(self):
        S, p = self, self.p
        for L in range(5):
            S.mset(S.sts[L][:, :], 0.0, (f"st{L}",))
        S.mset(S.sm[:, 128:148], 0.0, ("sm",))
        for sub in range(NSUB):
            for lane in range(5):
                S.lane_load_weights(lane)
                for hb in range(2):
                    S.lane_proj(lane, sub, hb)
                    for c in range(2):
                        S.lane_chunk(lane, sub, hb, c, mainpass=False)
        for L in range(5):
            p.dma("gpsimd", S.d_stg_in[:, L * 1024:(L + 1) * 1024], S.sts[L][:, :],
                  reads=(f"st{L}",), writes=("d_stg_in",))
        p.dma("gpsimd", S.d_stgA_in[:, :], S.sm[:, 128:148],
              reads=("sm",), writes=("d_stgA_in",))
        p.cc("AllGather", ALU.bypass, GRP4, S.d_stg_in, S.d_stg_out,
             reads=("d_stg_in",), writes=("d_stg_out",), tag="stAG")
        p.cc("AllGather", ALU.bypass, GRP4, S.d_stgA_in, S.d_stgA_out,
             reads=("d_stgA_in",), writes=("d_stgA_out",), tag="sAAG")

    def ph_combine(self):
        S, p, sm = self, self.p, self.sm
        for j in range(4):
            p.dma("sync", sm[:, 152 + j * 20:172 + j * 20], S.d_stgA_out[j],
                  reads=("d_stgA_out",), writes=("sm",))
        for L in range(5):
            stg = [S.tmpC, S.tmpD]
            stgk = [S.K_tmpC, S.K_tmpD]
            for j in range(4):
                p.dma("sync", stg[j % 2], S.d_stg_out[j, :, L * 1024:(L + 1) * 1024],
                      reads=("d_stg_out",), writes=stgk[j % 2])
                for li in range(4):
                    scol = S.msk_s[:, j * 4 + li: j * 4 + li + 1]
                    acol = sm[:, 152 + li * 20 + L * 4: 152 + li * 20 + L * 4 + 4]
                    if li == 0:
                        S.ts(sm[:, 232:236], acol, scol, None, ALU.mult,
                             reads=("sm", "msk_s"), writes=("sm",))
                    else:
                        S.stt(sm[:, 232:236], acol, scol, sm[:, 232:236],
                              ALU.mult, ALU.add, ("sm", "msk_s", "sm"),
                              ("sm",))
                S.act(sm[:, 232:236], sm[:, 232:236], AF.Exp, ("sm",), ("sm",))
                S.ts(sm[:, 232:236], sm[:, 232:236], S.msk_sel[:, j:j + 1], None,
                     ALU.mult, reads=("sm", "msk_sel"), writes=("sm",))
                for h in range(H):
                    dsl = stg[j % 2][:, h * 256:(h + 1) * 256]
                    ssl = S.sts[L][:, h * 256:(h + 1) * 256]
                    if j == 0:
                        S.ts(ssl, dsl, sm[:, 232 + h:233 + h], None, ALU.mult,
                             reads=stgk[j % 2] + ("sm",), writes=(f"st{L}",))
                    else:
                        S.stt(ssl, dsl, sm[:, 232 + h:233 + h], ssl,
                              ALU.mult, ALU.add,
                              stgk[j % 2] + ("sm", f"st{L}"), (f"st{L}",))

    def ph_attn_main(self):
        S = self
        for sub in range(NSUB):
            for dc in range(DCH):
                S.load_cast(S.w_q[dc * 128:(dc + 1) * 128, :],
                            S.WS0[:, dc * 512:(dc + 1) * 512], ("WS0",), tag="wq")
            for h in range(H):
                for dc in range(DCH):
                    S.mm(S.PA[:, :],
                         S.WS0[:, dc * 512 + h * 128: dc * 512 + (h + 1) * 128],
                         S.nT[:, dc * T + sub * SUB: dc * T + (sub + 1) * SUB],
                         dc == 0, dc == DCH - 1, ("WS0", "nT"), ("PA",))
                S.ts(S.qfm[:, h * 512:(h + 1) * 512], S.PA[:, :], SCALE_Q, None,
                     ALU.mult, reads=("PA",), writes=S.K_qfm)
            for lane in range(5):
                S.lane_load_weights(lane)
                for hb in range(2):
                    S.lane_proj(lane, sub, hb)
                    for c in range(2):
                        S.lane_chunk(lane, sub, hb, c, mainpass=True)
            if sub == NSUB - 1:
                S.tap("oacc", S.oacc[:, :], S.K_oacc)
            self.ph_epilogue(sub)

    def ph_epilogue(self, sub):
        S = self
        for dc in range(DCH):
            dst = S.WS0 if dc < 4 else S.WS2
            S.load_cast(S.w_g[dc * 128:(dc + 1) * 128, :],
                        dst[:, (dc % 4) * 1024:(dc % 4 + 1) * 1024],
                        ("WS0" if dc < 4 else "WS2",), tag="wg")
        for r in range(4):
            g = sub * 4 + r
            osl = S.oacc[:, r * 1024:(r + 1) * 1024]
            for h in range(H):
                S.act(S.tmpB[:, h * 256:(h + 1) * 256], osl[:, h * 256:(h + 1) * 256],
                      AF.Square, S.K_oacc, S.K_tmpB + ("sm",),
                      accum=S.sm[:, 56 + h:57 + h])
            S.act(S.sm[:, 60:64], S.sm[:, 56:60], AF.Ln, ("sm",), ("sm",),
                  scale=1.0 / HDV, bias=RMS_EPS)
            S.act(S.sm[:, 56:60], S.sm[:, 60:64], AF.Exp, ("sm",), ("sm",),
                  scale=-0.5)
            for h in range(H):
                S.stt(S.tmpA[:, h * 256:(h + 1) * 256], osl[:, h * 256:(h + 1) * 256],
                      S.sm[:, 56 + h:57 + h], S.nwb[:, :], ALU.mult, ALU.mult,
                      S.K_oacc + ("sm", "nwb"), S.K_tmpA)
            for dc in range(DCH):
                src = S.WS0 if dc < 4 else S.WS2
                skey = "WS0" if dc < 4 else "WS2"
                lhs = S.nT[:, dc * T + g * 128: dc * T + (g + 1) * 128]
                for nh in range(2):
                    S.mm(S.W1[:, nh * 512:(nh + 1) * 512], lhs,
                         src[:, (dc % 4) * 1024 + nh * 512:(dc % 4) * 1024 + (nh + 1) * 512],
                         dc == 0, dc == DCH - 1, ("nT", skey), ("W1",))
            S.act(S.tmpB, S.W1[:, :], AF.Silu, ("W1",), S.K_tmpB)
            S.tt(S.tmpC, S.tmpA, S.tmpB, ALU.mult, S.K_tmpA + S.K_tmpB, S.K_tmpC)
            for cdc in range(DCH):
                S.tr(S.PCb[:, cdc * 128:(cdc + 1) * 128],
                     S.tmpC[:, cdc * 128:(cdc + 1) * 128], S.K_tmpC, ("PC",))
            S.cp(S.tmpD, S.PCb[:, 0:1024], ("PC",), S.K_tmpD)
            for oh in range(2):
                for dci in range(4):
                    dc = oh * 4 + dci
                    S.load_cast(S.w_o[dc * 128:(dc + 1) * 128, :],
                                S.WS1[:, dci * 1024:(dci + 1) * 1024], ("WS1",), tag="wo")
                for dci in range(4):
                    dc = oh * 4 + dci
                    for nh in range(2):
                        S.mm(S.W0[:, nh * 512:(nh + 1) * 512],
                             S.tmpD[:, dc * 128:(dc + 1) * 128],
                             S.WS1[:, dci * 1024 + nh * 512: dci * 1024 + (nh + 1) * 512],
                             dc == 0, dc == DCH - 1, (S.K_tmpD[0], "WS1"), ("W0",))
            ssl = S.stream[:, g * D:(g + 1) * D]
            S.tt(ssl, ssl, S.W0[:, :], ALU.add, ("stream", "W0"), ("stream",))

    def quant_2048(self, tag=""):
        S = self
        ks = tuple(set(S.K_fhh) | set(S.K_fam) | set(S.K_frc) | set(S.K_fw1)
                   | set(S.K_fw2) | set(S.K_fbs1) | set(S.K_fph))
        S.fp4_quant(S.f_hh, 2048, self._q_gs6, self._q_gs, S.f_am, S.f_rc,
                    S.f_w1, S.f_w2, S.f_ph, S.f_bs1, ks)
        for c in range(8):
            S.tr(S.PCb[:, c * 128:(c + 1) * 128], S.f_ph[:, c * 128:(c + 1) * 128],
                 S.K_fph, ("PC",), tag=tag)
            S.tr(S.PDb[:, c * 128:(c + 1) * 128],
                 S.f_ph[:, 1024 + c * 128:1024 + (c + 1) * 128], S.K_fph,
                 ("PD",), tag=tag)
        S.cp(S.hqt_s[:, 0:1024], S.PCb[:, 0:1024], ("PC",), S.K_hqt)
        S.cp(S.hqt_s[:, 1024:2048], S.PDb[:, 0:1024], ("PD",), S.K_hqt)

    def ph_wquant(self):
        S, p, sm = self, self.p, self.sm

        def qpair(srcs, dsts, gcol, wcol, dkey, tag):
            for half, sap in enumerate(srcs):
                p.dma("sync", S.f_hh[:, half * 1024:(half + 1) * 1024], sap,
                      writes=S.K_fhh, tag=tag)
            self._q_gs6 = sm[:, gcol:gcol + 1]
            self._q_gs = sm[:, wcol:wcol + 1]
            self.quant_2048(tag=tag)
            for half, dap in enumerate(dsts):
                p.dma("gpsimd", dap,
                      S.hqt_s[:, half * 1024:(half + 1) * 1024]
                      .rearrange("p (c k) -> p c k", k=128),
                      reads=S.K_hqt, writes=(dkey,), tag=tag)

        for name, shard, dout, gcol, wcol in (
                ("g", S.fw_gshard, S.d_gwq_in, 40, 32),
                ("u", S.fw_ushard, S.d_uwq_in, 41, 33)):
            for rt2 in range(2):
                srcs = [shard[(rt2 * 2 + h) * 128:(rt2 * 2 + h + 1) * 128, :]
                        for h in range(2)]
                dv = dout.rearrange("(c p) r -> p c r", p=128)
                dsts = [dv[:, :, (rt2 * 2 + h) * 128:(rt2 * 2 + h + 1) * 128]
                        for h in range(2)]
                qpair(srcs, dsts, gcol, wcol, "d_" + name + "wq_in", "wq" + name)
        for ic2 in range(2):
            srcs = [S.fw_dshard[:, (ic2 * 2 + h) * 1024:(ic2 * 2 + h + 1) * 1024]
                    for h in range(2)]
            dsts = [S.d_dwq_in[(ic2 * 2 + h) * 1024:(ic2 * 2 + h + 1) * 1024, :]
                    .rearrange("(c p) r -> p c r", p=128) for h in range(2)]
            qpair(srcs, dsts, 42, 34, "d_dwq_in", "wqd")
        p.cc("AllGather", ALU.bypass, GRP8, S.d_gwq_in, S.d_gwq,
             reads=("d_gwq_in",), writes=("d_gwq",), tag="gwAG")
        p.cc("AllGather", ALU.bypass, GRP8, S.d_uwq_in, S.d_uwq,
             reads=("d_uwq_in",), writes=("d_uwq",), tag="uwAG")
        p.cc("AllGather", ALU.bypass, GRP8, S.d_dwq_in, S.d_dwq,
             reads=("d_dwq_in",), writes=("d_dwq",), tag="dwAG")

    def ph_ffn0(self):
        S, p, sm = self, self.p, self.sm
        for t2 in range(8):
            for h in range(2):
                t = t2 * 2 + h
                xs = S.stream[:, t * D:(t + 1) * D]
                S.ts(S.f_hh[:, h * 1024:(h + 1) * 1024], xs, S.mrs[:, t:t + 1],
                     S.mrs[:, 16 + t:17 + t], ALU.subtract, ALU.mult,
                     reads=("stream", "mrs"), writes=S.K_fhh)
            self._q_gs6 = sm[:, 38:39]
            self._q_gs = sm[:, 24:25]
            self.quant_2048(tag="xq")
            for h in range(2):
                p.dma("gpsimd", S.d_xqT[t2 * 2 + h],
                      S.hqt_s[:, h * 1024:(h + 1) * 1024], reads=S.K_hqt,
                      writes=("d_xqT",), tag="xqW")
            if t2 == 0:
                S.tap("xq0", S.f_ph[:, :], S.K_fph)
                S.tap("smsc", S.sm[:, :], ("sm",))
        for sb in range(4):
            for ic in range(8):
                p.dma("sync", S.WS0[:, :].rearrange("p (c r) -> p c r", r=512),
                      S.d_gwq[ic].rearrange("(c p) r -> p c r", p=128),
                      reads=("d_gwq",), writes=("WS0",), tag="ldgw")
                p.dma("sync", S.WS1[:, :].rearrange("p (c r) -> p c r", r=512),
                      S.d_uwq[ic].rearrange("(c p) r -> p c r", p=128),
                      reads=("d_uwq",), writes=("WS1",), tag="lduw")
                for r8 in range(8):
                    for ib in range(4):
                        p.dma("sync", S.WS2[:, ib * 1024 + r8 * 128: ib * 1024 + (r8 + 1) * 128],
                              S.d_dwq[r8, ic * 512 + ib * 128: ic * 512 + (ib + 1) * 128, :],
                              reads=("d_dwq",), writes=("WS2",), tag="lddw")
                for tl in range(4):
                    t = sb * 4 + tl
                    p.dma("sync", S.xqt_s, S.d_xqT[t], reads=("d_xqT",),
                          writes=S.K_xqt, tag="ldxq")
                    for dc in range(DCH):
                        S.mm(S.PA[:, :], S.xqt_s[:, dc * 128:(dc + 1) * 128],
                             S.WS0[:, dc * 512:(dc + 1) * 512], dc == 0,
                             dc == DCH - 1, S.K_xqt + ("WS0",), ("PA",))
                    for dc in range(DCH):
                        S.mm(S.PB[:, :], S.xqt_s[:, dc * 128:(dc + 1) * 128],
                             S.WS1[:, dc * 512:(dc + 1) * 512], dc == 0,
                             dc == DCH - 1, S.K_xqt + ("WS1",), ("PB",))
                    S.act(S.f_sil, S.PA[:, :], AF.Silu, ("PA", "sm"), S.K_fsil,
                          scale=sm[:, 35:36])
                    S.stt(S.f_hh[:, tl * 512:(tl + 1) * 512], S.PB[:, :],
                          sm[:, 36:37], S.f_sil, ALU.mult, ALU.mult,
                          ("PB", "sm") + S.K_fsil, S.K_fhh)
                if sb == 0 and ic == 0:
                    S.tap("hh0", S.f_hh[:, :], S.K_fhh)
                self._q_gs6 = sm[:, 39:40]
                self._q_gs = sm[:, 25:26]
                self.quant_2048(tag="hq")
                if sb == 0 and ic == 0:
                    S.tap("ph0", S.f_ph[:, :], S.K_fph)
                for tl in range(4):
                    for ib in range(4):
                        for nh in range(2):
                            S.mm(S.W0[:, nh * 512:(nh + 1) * 512],
                                 S.hqt_s[:, tl * 512 + ib * 128: tl * 512 + (ib + 1) * 128],
                                 S.WS2[:, ib * 1024 + nh * 512: ib * 1024 + (nh + 1) * 512],
                                 ib == 0, ib == 3, S.K_hqt + ("WS2",), ("W0",))
                    ysl = S.yacc[:, tl * 1024:(tl + 1) * 1024]
                    if ic == 0:
                        S.cp(ysl, S.W0[:, :], ("W0",), S.K_yacc)
                    else:
                        S.tt(ysl, ysl, S.W0[:, :], ALU.add, S.K_yacc + ("W0",),
                             S.K_yacc)
            for tl in range(4):
                t = sb * 4 + tl
                ssl = S.stream[:, t * D:(t + 1) * D]
                S.stt(ssl, S.yacc[:, tl * 1024:(tl + 1) * 1024], sm[:, 37:38],
                      ssl, ALU.mult, ALU.add, S.K_yacc + ("sm", "stream"),
                      ("stream",))

    def ph_l1_prep(self):
        S, p = self, self.p
        for n in "qkvgo":
            p.dma("sync", S.rowc[:, :], S.a_b[n], writes=S.K_rowc, tag="l1b")
            S.cp(S.brow[n][:, :], S.rowc[:, :], S.K_rowc, ("brow",))
        p.dma("sync", S.bcol[:, 0:8],
              S.a_b["q"].rearrange("o (c p) -> (o p) c", p=128), writes=("bcol",),
              allow_slow_non_contiguous=True)
        S.ts(S.bcol[:, 8:16], S.bcol[:, 0:8], -1.0, None, ALU.mult,
             reads=("bcol",), writes=("bcol",))

    def ph_l1_attn(self, limit=99):
        S, p = self, self.p
        self._l1lim = limit
        self.ph_l1_prep()
        if limit < 1:
            return
        for t in range(NT):
            for W, ps, wname in ((S.W0, "W0", "k"), (S.W1, "W1", "v")):
                for nh in range(2):
                    S.mm(W[:, nh * 512:(nh + 1) * 512], S.cst_o1b[:, :],
                         S.brow[wname][:, nh * 512:(nh + 1) * 512], True, False,
                         ("cst_o1b", "brow"), (ps,))
                Wd = S.a_W[wname]
                for vh in range(2):
                    for dci in range(4):
                        dc = vh * 4 + dci
                        S.load_cast(Wd[dc * 128:(dc + 1) * 128, :],
                                    S.WS1[:, dci * 1024:(dci + 1) * 1024],
                                    ("WS1",), tag="l1w" + wname)
                    for dci in range(4):
                        dc = vh * 4 + dci
                        lhs = S.nT[:, dc * T + t * 128: dc * T + (t + 1) * 128]
                        for nh in range(2):
                            S.mm(W[:, nh * 512:(nh + 1) * 512], lhs,
                                 S.WS1[:, dci * 1024 + nh * 512: dci * 1024 + (nh + 1) * 512],
                                 False, dc == DCH - 1, ("nT", "WS1"), (ps,))
            S.act(S.l1e1, S.W0[:, :], AF.Relu, ("W0",), S.K_l1e1)
            S.act(S.l1e2, S.W0[:, :], AF.Relu, ("W0",), S.K_l1e2, scale=-1.0)
            S.act(S.l1e2, S.l1e2, AF.Exp, S.K_l1e2, S.K_l1e2, scale=-1.0)
            S.tt(S.l1kf, S.l1e2, S.l1e1, ALU.add, S.K_l1e1 + S.K_l1e2, S.K_l1kf)
            S.cp(S.l1vf, S.W1[:, :], ("W1",), S.K_l1vf)
            if t == 5:
                S.tap("l1kf5", S.l1kf, S.K_l1kf)
                S.tap("l1vf5", S.l1vf, S.K_l1vf)
            for h in range(H1):
                ps = S.PC if h < 8 else S.PD
                psk = "PC" if h < 8 else "PD"
                S.mm(ps[0:64, (h % 8) * 64:(h % 8) * 64 + 64],
                     S.l1kf[:, h * 64:(h + 1) * 64], S.l1vf[:, h * 64:(h + 1) * 64],
                     True, True, S.K_l1kf + S.K_l1vf, (psk,))
            if t == 0:
                S.cp(S.kvb[0:64, 0:512], S.PC[0:64, :], ("PC",), S.K_kvb)
                S.cp(S.kvb[0:64, 512:1024], S.PD[0:64, :], ("PD",), S.K_kvb)
            else:
                S.tt(S.kvb[0:64, 0:512], S.kvb[0:64, 0:512], S.PC[0:64, :],
                     ALU.add, S.K_kvb + ("PC",), S.K_kvb)
                S.tt(S.kvb[0:64, 512:1024], S.kvb[0:64, 512:1024], S.PD[0:64, :],
                     ALU.add, S.K_kvb + ("PD",), S.K_kvb)
        p.dma("gpsimd", S.d_kv_in, S.kvb[:, :], reads=S.K_kvb, writes=("d_kv_in",))
        p.cc("AllReduce", ALU.add, GRP4, S.d_kv_in, S.d_kv_out,
             reads=("d_kv_in",), writes=("d_kv_out",), tag="kvAR")
        p.dma("sync", S.kvb[:, :], S.d_kv_out, reads=("d_kv_out",), writes=S.K_kvb)
        S.mset(S.kvb2[:, :], 0.0, S.K_kvb2)
        S.cp(S.kvb2[0:64, :].rearrange("q (h k) -> q h k", k=128)[:, :, 0:64],
             S.kvb[0:64, :].rearrange("q (h k) -> q h k", k=64)[:, 0:16:2, :],
             S.K_kvb, S.K_kvb2)
        p.dma("gpsimd",
              S.kvb2[64:128, :].rearrange("q (h k) -> q h k", k=128)[:, :, 64:128],
              S.kvb[0:64, :].rearrange("q (h k) -> q h k", k=64)[:, 1:16:2, :],
              reads=S.K_kvb, writes=S.K_kvb2)
        S.tap("kvb2", S.kvb2[:, :], S.K_kvb2)
        if limit < 3:
            return
        p.dma("gpsimd", S.d_xnt_in.rearrange("(c p) t -> p c t", p=128),
              S.nT[:, :].rearrange("p (c t) -> p c t", t=T),
              reads=("nT",), writes=("d_xnt_in",), tag="xntW")
        p.cc("AllGather", ALU.bypass, GRP8, S.d_xnt_in, S.d_xnt,
             reads=("d_xnt_in",), writes=("d_xnt",), tag="xntAG")
        for sub in range(NSUB):
            for ec in range(DCH):
                for dc in range(DCH):
                    S.load_cast(S.a_W["q"][dc * 128:(dc + 1) * 128, ec * 128:(ec + 1) * 128],
                                S.WS0[:, dc * 128:(dc + 1) * 128], ("WS0",), tag="l1wq")
                for dc in range(DCH):
                    S.mm(S.PA[:, :], S.WS0[:, dc * 128:(dc + 1) * 128],
                         S.nT[:, dc * T + sub * SUB: dc * T + (sub + 1) * SUB],
                         dc == 0, dc == DCH - 1, ("WS0", "nT"), ("PA",))
                S.act(S.l1e1[:, 0:512], S.PA[:, :], AF.Relu, ("PA", "bcol"),
                      S.K_l1e1, bias=S.bcol[:, ec:ec + 1])
                S.act(S.l1e2[:, 0:512], S.PA[:, :], AF.Relu, ("PA", "bcol"),
                      S.K_l1e2, scale=-1.0, bias=S.bcol[:, 8 + ec:9 + ec])
                S.act(S.l1e2[:, 0:512], S.l1e2[:, 0:512], AF.Exp, S.K_l1e2,
                      S.K_l1e2, scale=-1.0)
                S.tt(S.l1q[:, ec * 512:(ec + 1) * 512], S.l1e2[:, 0:512],
                     S.l1e1[:, 0:512], ALU.add, S.K_l1e1 + S.K_l1e2, S.K_l1q)
            if self._l1lim < 5:
                continue
            for dc in range(DCH):
                dst = S.WS1 if dc < 4 else S.WS2
                S.load_cast(S.a_W["g"][dc * 128:(dc + 1) * 128, :],
                            dst[:, (dc % 4) * 1024:(dc % 4 + 1) * 1024],
                            ("WS1" if dc < 4 else "WS2",), tag="l1wg")
            for tl in range(4):
                t = sub * 4 + tl
                for nh in range(2):
                    S.mm(S.W0[:, nh * 512:(nh + 1) * 512], S.cst_o1b[:, :],
                         S.brow["g"][:, nh * 512:(nh + 1) * 512], True, False,
                         ("cst_o1b", "brow"), ("W0",))
                for dc in range(DCH):
                    src = S.WS1 if dc < 4 else S.WS2
                    skey = "WS1" if dc < 4 else "WS2"
                    lhs = S.nT[:, dc * T + t * 128: dc * T + (t + 1) * 128]
                    for nh in range(2):
                        S.mm(S.W0[:, nh * 512:(nh + 1) * 512], lhs,
                             src[:, (dc % 4) * 1024 + nh * 512:(dc % 4) * 1024 + (nh + 1) * 512],
                             False, dc == DCH - 1, ("nT", skey), ("W0",))
                S.act(S.l1sg, S.W0[:, :], AF.Sigmoid, ("W0",), S.K_l1sg)
                if self._l1lim < 6:
                    continue
                for ec in range(DCH):
                    S.mm(S.W1[:, ec * 128:(ec + 1) * 128],
                         S.l1q[:, ec * 512 + tl * 128: ec * 512 + (tl + 1) * 128],
                         S.kvb2[:, ec * 128:(ec + 1) * 128],
                         True, True, S.K_l1q + S.K_kvb2, ("W1",))
                S.tt(S.l1ga, S.l1sg, S.W1[:, :], ALU.mult, S.K_l1sg + ("W1",),
                     S.K_l1ga)
                if sub == 0 and tl == 0:
                    S.tap("l1q0", S.l1q[:, :], S.K_l1q)
                    S.tap("l1sg0", S.l1sg, S.K_l1sg)
                    S.tap("l1ga0", S.l1ga, S.K_l1ga)
                for cdc in range(DCH):
                    S.tr(S.PCb[:, cdc * 128:(cdc + 1) * 128],
                         S.l1ga[:, cdc * 128:(cdc + 1) * 128], S.K_l1ga, ("PC",))
                S.cp(S.l1gt, S.PCb[:, 0:1024], ("PC",), S.K_l1gt)
                if self._l1lim < 7:
                    continue
                for nh in range(2):
                    S.mm(S.W0[:, nh * 512:(nh + 1) * 512], S.cst_o1b[:, :],
                         S.brow["o"][:, nh * 512:(nh + 1) * 512], True, False,
                         ("cst_o1b", "brow"), ("W0",))
                for oh in range(2):
                    for dci in range(4):
                        dc = oh * 4 + dci
                        S.load_cast(S.a_W["o"][dc * 128:(dc + 1) * 128, :],
                                    S.WS0[:, dci * 1024:(dci + 1) * 1024],
                                    ("WS0",), tag="l1wo")
                    for dci in range(4):
                        dc = oh * 4 + dci
                        for nh in range(2):
                            S.mm(S.W0[:, nh * 512:(nh + 1) * 512],
                                 S.l1gt[:, dc * 128:(dc + 1) * 128],
                                 S.WS0[:, dci * 1024 + nh * 512: dci * 1024 + (nh + 1) * 512],
                                 False, dc == DCH - 1,
                                 (S.K_l1gt[0], "WS0"), ("W0",))
                ssl = S.stream[:, t * D:(t + 1) * D]
                S.tt(ssl, ssl, S.W0[:, :], ALU.add, ("stream", "W0"), ("stream",))

    def transpose_weight_to_dram(self, src_dram, dst_dram, skey, dkey):
        S, p = self, self.p
        for rtile in range(8):
            stage = S.wstg[:, 0:1024]
            p.dma("sync", stage, src_dram[rtile * 128:(rtile + 1) * 128, :],
                  reads=(skey,), writes=S.K_wstg, tag="twl")
            S.cp(S.tmpC, stage, S.K_wstg, S.K_tmpC)
            for c in range(DCH):
                S.tr(S.PCb[:, c * 128:(c + 1) * 128], S.tmpC[:, c * 128:(c + 1) * 128],
                     S.K_tmpC, ("PC",))
            S.cp(S.tmpD, S.PCb[:, 0:1024], ("PC",), S.K_tmpD)
            dst = dst_dram.rearrange("(c p) r -> p c r", p=128)[:, :, rtile * 128:(rtile + 1) * 128]
            p.dma("gpsimd", dst, S.tmpD[:, :].rearrange("p (c k) -> p c k", k=128),
                  reads=S.K_tmpD, writes=(dkey,), tag="twS")

    def ph_l1_route(self):
        S, p, sm, rt = self, self.p, self.sm, self.rt
        stage = S.wstg[0:16, 0:1024]
        p.dma("sync", stage, S.m_gate_w, writes=S.K_wstg, tag="l1gw")
        S.cp(S.tmpC[0:16, :], stage, S.K_wstg, S.K_tmpC)
        for c in range(DCH):
            S.tr(S.PCb[:, c * 16:(c + 1) * 16], S.tmpC[0:16, c * 128:(c + 1) * 128],
                 S.K_tmpC, ("PC",))
        S.cp(S.gwt[:, :], S.PCb[:, 0:128], ("PC",), ("gwt",))
        for t in range(NT):
            for dc in range(DCH):
                S.mm(S.PD[:, 0:16], S.nT[:, dc * T + t * 128: dc * T + (t + 1) * 128],
                     S.gwt[:, dc * 16:(dc + 1) * 16], dc == 0, dc == DCH - 1,
                     ("nT", "gwt"), ("PD",))
            dst = rt[:, 0:256].rearrange("p (m k) -> p m k", k=16)[:, :, t:t + 1]
            S.act(dst, S.PD[:, 0:16].rearrange("p (m o) -> p m o", o=1),
                  AF.Sigmoid, ("PD",), ("rt",))
        p.dma("sync", S.rowb[0:1, 0:16], S.m_ebias, writes=("rowb",))
        S.mm(S.PD[:, 0:16], S.cst_o1f[:, :], S.rowb[0:1, 0:16], True, True,
             ("cst_o1f", "rowb"), ("PD",))
        S.cp(sm[:, 144:160], S.PD[:, 0:16], ("PD",), ("sm",))
        for e in range(E16):
            S.ts(rt[:, 256 + e * 16:256 + e * 16 + 16], rt[:, e * 16:e * 16 + 16],
                 sm[:, 144 + e:145 + e], None, ALU.add,
                 reads=("rt", "sm"), writes=("rt",))
        for e in range(E16):
            first = True
            for j in range(E16):
                if j == e:
                    continue
                S.tt(sm[:, 160:176], rt[:, 256 + j * 16:256 + j * 16 + 16],
                     rt[:, 256 + e * 16:256 + e * 16 + 16], ALU.is_gt,
                     ("rt",), ("sm",))
                if first:
                    S.cp(rt[:, 512 + e * 16:512 + e * 16 + 16], sm[:, 160:176],
                         ("sm",), ("rt",))
                    first = False
                else:
                    S.tt(rt[:, 512 + e * 16:512 + e * 16 + 16],
                         rt[:, 512 + e * 16:512 + e * 16 + 16], sm[:, 160:176],
                         ALU.add, ("rt", "sm"), ("rt",))
                if j < e:
                    S.tt(sm[:, 160:176], rt[:, 256 + j * 16:256 + j * 16 + 16],
                         rt[:, 256 + e * 16:256 + e * 16 + 16], ALU.is_equal,
                         ("rt",), ("sm",))
                    S.tt(rt[:, 512 + e * 16:512 + e * 16 + 16],
                         rt[:, 512 + e * 16:512 + e * 16 + 16], sm[:, 160:176],
                         ALU.add, ("rt", "sm"), ("rt",))
        S.ts(rt[:, 512:768], rt[:, 512:768], 7.5, None, ALU.is_lt,
             reads=("rt",), writes=("rt",))
        S.tt(rt[:, 256:512], rt[:, 512:768], rt[:, 0:256], ALU.mult,
             ("rt", "rt"), ("rt",))
        S.cp(sm[:, 176:192], rt[:, 256:272], ("rt",), ("sm",))
        for e in range(1, E16):
            S.tt(sm[:, 176:192], sm[:, 176:192], rt[:, 256 + e * 16:256 + e * 16 + 16],
                 ALU.add, ("sm", "rt"), ("sm",))
        S.recip(sm[:, 192:208], sm[:, 176:192], ("sm",), ("sm",))
        for e in range(E16):
            S.stt(rt[:, 256 + e * 16:256 + e * 16 + 16],
                  rt[:, 256 + e * 16:256 + e * 16 + 16], ROUTE_SCALE,
                  sm[:, 192:208], ALU.mult, ALU.mult, ("rt", "sm"), ("rt",))
        for t in range(NT):
            csrc = rt[:, 256:512].rearrange("p (m k) -> p m k", k=16)[:, :, t:t + 1]
            S.cp(sm[:, 224:240].rearrange("p (m o) -> p m o", o=1), csrc,
                 ("rt",), ("sm",))
            p.dma("gpsimd", S.d_c_in[t * 128:(t + 1) * 128, :], sm[:, 224:240],
                  reads=("sm",), writes=("d_c_in",), tag="cW")
        p.cc("AllGather", ALU.bypass, GRP8, S.d_c_in, S.d_c_out,
             reads=("d_c_in",), writes=("d_c_out",), tag="cAG")
        for gtl in range(8 * NT):
            stage = S.l1e1[:, 0:16]
            p.dma("sync", stage, S.d_c_out[gtl // 16, (gtl % 16) * 128:(gtl % 16) * 128 + 128, :],
                  reads=("d_c_out",), writes=S.K_l1e1, tag="cR")
            S.tr(S.PD[0:16, 0:128], stage, S.K_l1e1, ("PD",), f32=True)
            S.cp(S.l1e2[0:16, 0:128], S.PD[0:16, 0:128], ("PD",), S.K_l1e2)
            S.mm(S.PA[:, 0:2], S.l1e2[0:16, 0:128], S.esel_sb[:, :], True, True,
                 S.K_l1e2 + ("esel_sb",), ("PA",))
            dst = S.e_cmy.rearrange("p (e k) -> p e k", k=128)[:, :, gtl:gtl + 1]
            S.cp(dst, S.PA[:, 0:2].rearrange("p (e o) -> p e o", o=1),
                 ("PA",), S.K_ecmy)

    def ph_l1_experts(self):
        S, p = self, self.p
        for e in range(EPC):
            self.transpose_weight_to_dram(S.m_wg[e], S.d_ewT[e, 0], "m_wg", "d_ewT")
            self.transpose_weight_to_dram(S.m_wu[e], S.d_ewT[e, 1], "m_wu", "d_ewT")
            self.transpose_weight_to_dram(S.m_wd[e], S.d_ewT[e, 2], "m_wd", "d_ewT")
        for e in range(EPC):
            p.dma("sync", S.WS0[:, :].rearrange("p (c r) -> p c r", r=1024),
                  S.d_ewT[e, 2, 0:512, :].rearrange("(c p) r -> p c r", p=128),
                  reads=("d_ewT",), writes=("WS0",), tag="ewd0")
            p.dma("sync", S.WS1[:, :].rearrange("p (c r) -> p c r", r=1024),
                  S.d_ewT[e, 2, 512:1024, :].rearrange("(c p) r -> p c r", p=128),
                  reads=("d_ewT",), writes=("WS1",), tag="ewd1")
            p.dma("sync", S.WS2[:, :].rearrange("p (c r) -> p c r", r=1024),
                  S.d_ewT[e, 0, 0:512, :].rearrange("(c p) r -> p c r", p=128),
                  reads=("d_ewT",), writes=("WS2",), tag="ewg0")
            p.dma("sync", S.e_wg2.rearrange("p (c r) -> p c r", r=1024),
                  S.d_ewT[e, 0, 512:1024, :].rearrange("(c p) r -> p c r", p=128),
                  reads=("d_ewT",), writes=S.K_ewg2, tag="ewg1")
            p.dma("sync", S.e_wu.rearrange("p (c r) -> p c r", r=1024),
                  S.d_ewT[e, 1].rearrange("(c p) r -> p c r", p=128),
                  reads=("d_ewT",), writes=S.K_ewu, tag="ewu")
            for gtl in range(8 * NT):
                p.dma("sync", S.e_xnt.rearrange("p (c t) -> p c t", t=128),
                      S.d_xnt[gtl // 16][:, (gtl % 16) * 128:(gtl % 16) * 128 + 128]
                      .rearrange("(c p) t -> p c t", p=128),
                      reads=("d_xnt",), writes=S.K_ex, tag="exnt")
                for dc in range(DCH):
                    wgb = S.WS2 if dc < 4 else S.e_wg2
                    wgk = "WS2" if dc < 4 else S.K_ewg2[0]
                    for nh in range(2):
                        S.mm(S.W0[:, nh * 512:(nh + 1) * 512],
                             S.e_xnt[:, dc * 128:(dc + 1) * 128],
                             wgb[:, (dc % 4) * 1024 + nh * 512:(dc % 4) * 1024 + (nh + 1) * 512],
                             dc == 0, dc == DCH - 1, S.K_ex + (wgk,), ("W0",))
                for dc in range(DCH):
                    for nh in range(2):
                        S.mm(S.W1[:, nh * 512:(nh + 1) * 512],
                             S.e_xnt[:, dc * 128:(dc + 1) * 128],
                             S.e_wu[:, dc * 1024 + nh * 512: dc * 1024 + (nh + 1) * 512],
                             dc == 0, dc == DCH - 1, S.K_ex + S.K_ewu, ("W1",))
                S.act(S.e_sil, S.W0[:, :], AF.Silu, ("W0",), S.K_esil)
                ccol = S.e_cmy[:, e * 128 + gtl: e * 128 + gtl + 1]
                S.stt(S.e_h, S.W1[:, :], ccol, S.e_sil, ALU.mult, ALU.mult,
                      ("W1",) + S.K_ecmy + S.K_esil, S.K_eh)
                for c in range(DCH):
                    S.tr(S.PCb[:, c * 128:(c + 1) * 128], S.e_h[:, c * 128:(c + 1) * 128],
                         S.K_eh, ("PC",))
                S.cp(S.e_hT, S.PCb[:, 0:1024], ("PC",), S.K_ehT)
                for ib in range(DCH):
                    wdb = S.WS0 if ib < 4 else S.WS1
                    wdk = "WS0" if ib < 4 else "WS1"
                    for nh in range(2):
                        S.mm(S.W0[:, nh * 512:(nh + 1) * 512],
                             S.e_hT[:, ib * 128:(ib + 1) * 128],
                             wdb[:, (ib % 4) * 1024 + nh * 512:(ib % 4) * 1024 + (nh + 1) * 512],
                             ib == 0, ib == DCH - 1, S.K_ehT + (wdk,), ("W0",))
                S.cp(S.e_y, S.W0[:, :], ("W0",), S.K_ey)
                if e == 0:
                    p.dma("gpsimd", S.d_y_in[gtl * 128:(gtl + 1) * 128, :], S.e_y,
                          reads=S.K_ey, writes=("d_y_in",), tag="yW")
                else:
                    p.dma("gpsimd", S.d_y_in[gtl * 128:(gtl + 1) * 128, :], S.e_y,
                          reads=S.K_ey + ("d_y_in",), writes=("d_y_in",), tag="yWa",
                          accum_op=ALU.add)
        p.cc("ReduceScatter", ALU.add, GRP8, S.d_y_in, S.d_y_out,
             reads=("d_y_in",), writes=("d_y_out",), tag="yRS")

    def ph_l1_shared(self):
        S, p = self, self.p
        self.transpose_weight_to_dram(S.m_sg, S.d_swT[0], "m_sg", "d_swT")
        self.transpose_weight_to_dram(S.m_su, S.d_swT[1], "m_su", "d_swT")
        self.transpose_weight_to_dram(S.m_sd, S.d_swT[2], "m_sd", "d_swT")
        p.dma("sync", S.WS0[:, :].rearrange("p (c r) -> p c r", r=1024),
              S.d_swT[2, 0:512, :].rearrange("(c p) r -> p c r", p=128),
              reads=("d_swT",), writes=("WS0",), tag="swd0")
        p.dma("sync", S.WS1[:, :].rearrange("p (c r) -> p c r", r=1024),
              S.d_swT[2, 512:1024, :].rearrange("(c p) r -> p c r", p=128),
              reads=("d_swT",), writes=("WS1",), tag="swd1")
        p.dma("sync", S.WS2[:, :].rearrange("p (c r) -> p c r", r=1024),
              S.d_swT[0, 0:512, :].rearrange("(c p) r -> p c r", p=128),
              reads=("d_swT",), writes=("WS2",), tag="swg0")
        p.dma("sync", S.e_wg2.rearrange("p (c r) -> p c r", r=1024),
              S.d_swT[0, 512:1024, :].rearrange("(c p) r -> p c r", p=128),
              reads=("d_swT",), writes=S.K_ewg2, tag="swg1")
        p.dma("sync", S.e_wu.rearrange("p (c r) -> p c r", r=1024),
              S.d_swT[1].rearrange("(c p) r -> p c r", p=128),
              reads=("d_swT",), writes=S.K_ewu, tag="swu")
        for t in range(NT):
            for dc in range(DCH):
                wgb = S.WS2 if dc < 4 else S.e_wg2
                wgk = "WS2" if dc < 4 else S.K_ewg2[0]
                lhs = S.nT[:, dc * T + t * 128: dc * T + (t + 1) * 128]
                for nh in range(2):
                    S.mm(S.W0[:, nh * 512:(nh + 1) * 512], lhs,
                         wgb[:, (dc % 4) * 1024 + nh * 512:(dc % 4) * 1024 + (nh + 1) * 512],
                         dc == 0, dc == DCH - 1, ("nT", wgk), ("W0",))
            for dc in range(DCH):
                lhs = S.nT[:, dc * T + t * 128: dc * T + (t + 1) * 128]
                for nh in range(2):
                    S.mm(S.W1[:, nh * 512:(nh + 1) * 512], lhs,
                         S.e_wu[:, dc * 1024 + nh * 512: dc * 1024 + (nh + 1) * 512],
                         dc == 0, dc == DCH - 1, ("nT",) + S.K_ewu, ("W1",))
            S.act(S.e_sil, S.W0[:, :], AF.Silu, ("W0",), S.K_esil)
            S.tt(S.e_h, S.W1[:, :], S.e_sil, ALU.mult, ("W1",) + S.K_esil, S.K_eh)
            for c in range(DCH):
                S.tr(S.PCb[:, c * 128:(c + 1) * 128], S.e_h[:, c * 128:(c + 1) * 128],
                     S.K_eh, ("PC",))
            S.cp(S.e_hT, S.PCb[:, 0:1024], ("PC",), S.K_ehT)
            for ib in range(DCH):
                wdb = S.WS0 if ib < 4 else S.WS1
                wdk = "WS0" if ib < 4 else "WS1"
                for nh in range(2):
                    S.mm(S.W0[:, nh * 512:(nh + 1) * 512],
                         S.e_hT[:, ib * 128:(ib + 1) * 128],
                         wdb[:, (ib % 4) * 1024 + nh * 512:(ib % 4) * 1024 + (nh + 1) * 512],
                         ib == 0, ib == DCH - 1, S.K_ehT + (wdk,), ("W0",))
            ssl = S.stream[:, t * D:(t + 1) * D]
            S.tt(ssl, ssl, S.W0[:, :], ALU.add, ("stream", "W0"), ("stream",))
        for t in range(NT):
            p.dma("sync", S.e_st, S.d_y_out[t * 128:(t + 1) * 128, :],
                  reads=("d_y_out",), writes=S.K_est, tag="yR")
            if t == NT - 1:
                S.tap("yrs15", S.e_st, S.K_est)
                S.tap("ecmy", S.e_cmy, S.K_ecmy)
            ssl = S.stream[:, t * D:(t + 1) * D]
            S.tt(ssl, ssl, S.e_st, ALU.add, ("stream",) + S.K_est, ("stream",))

    def build(self, upto=None, skip=()):
        order = ["consts", "wamax", "ln", "routing", "wquant", "prepass",
                 "ffn0", "combine", "attn", "ln1", "l1a0", "l1a2", "l1a4", "l1a5", "l1a6", "l1attn",
                 "l1route", "l1experts", "l1shared"]
        todo = order[:order.index(upto) + 1] if upto else order
        todo = [t for t in todo if t not in skip]
        for ph in todo:
            if ph == "consts":
                self.ph_consts()
            elif ph == "wamax":
                self.ph_wamax()
            elif ph == "ln":
                self.ph_ln()
            elif ph == "routing":
                self.ph_routing()
            elif ph == "wquant":
                self.ph_wquant()
            elif ph == "prepass":
                self.ph_prepass()
            elif ph == "ffn0":
                self.ph_ffn0()
            elif ph == "combine":
                self.ph_combine()
            elif ph == "attn":
                self.ph_attn_main()
            elif ph == "ln1":
                self.ph_ln()
            elif ph == "l1attn":
                self.ph_l1_attn()
            elif ph == "l1a0":
                if upto == "l1a0":
                    self.ph_l1_attn(limit=0)
            elif ph == "l1a2":
                if upto == "l1a2":
                    self.ph_l1_attn(limit=2)
            elif ph in ("l1a4", "l1a5", "l1a6"):
                if upto == ph:
                    self.ph_l1_attn(limit=int(ph[-1]))
            elif ph == "l1route":
                self.ph_l1_route()
            elif ph == "l1experts":
                self.ph_l1_experts()
            elif ph == "l1shared":
                self.ph_l1_shared()
        p = self.p
        p.dma("gpsimd", self.out_d.rearrange("(t p) d -> p t d", p=128),
              self.stream[:, :].rearrange("p (t d) -> p t d", d=D),
              reads=("stream",), writes=("out_d",), tag="stout")
        p.finalize()
        p.build_block()
        return self


def host_inputs(inputs, core):
    b, qt = core // 4, core % 4
    x = np.asarray(inputs["L0_attn_x"])
    d = {}
    d["x"] = np.ascontiguousarray(x[b, qt * T:(qt + 1) * T, :], dtype=np.float32)
    ren = {
        "w_gate": "L0_attn_w_gate", "w_q": "L0_attn_w_q", "w_k": "L0_attn_w_k",
        "w_v": "L0_attn_w_v", "w_gk1": "L0_attn_w_gk1", "w_gk2": "L0_attn_w_gk2",
        "b_gk": "L0_attn_b_gk", "w_ks": "L0_attn_w_ks", "w_vs": "L0_attn_w_vs",
        "w_gk1s": "L0_attn_w_gk1s", "w_gk2s": "L0_attn_w_gk2s",
        "w_g": "L0_attn_w_g", "w_o": "L0_attn_w_o",
        "a_Wq": "L1_attn_Wq", "a_Wk": "L1_attn_Wk", "a_Wv": "L1_attn_Wv",
        "a_Wg": "L1_attn_Wg", "a_Wo": "L1_attn_Wo",
        "m_gate_w": "L1_ffn_gate_w",
        "m_sg": "L1_ffn_shared_gate_w", "m_su": "L1_ffn_shared_up_w",
        "m_sd": "L1_ffn_shared_down_w",
    }
    for k, v in ren.items():
        d[k] = np.ascontiguousarray(np.asarray(inputs[v], dtype=np.float32))
    d["b_gks"] = np.asarray(inputs["L0_attn_b_gks"], np.float32).reshape(1, DK)
    d["norm_w"] = np.asarray(inputs["L0_attn_norm_w"], np.float32).reshape(1, HDV)
    d["f_s_in"] = np.asarray(inputs["L0_ffn_s_in"], np.float32).reshape(1, 1)
    d["f_s_in_down"] = np.asarray(inputs["L0_ffn_s_in_down"], np.float32).reshape(1, 1)
    gw = np.asarray(inputs["L0_ffn_gate_w"], np.float32)
    uw = np.asarray(inputs["L0_ffn_up_w"], np.float32)
    dw = np.asarray(inputs["L0_ffn_down_w"], np.float32)
    d["fw_gshard"] = np.ascontiguousarray(gw[core * 512:(core + 1) * 512, :])
    d["fw_ushard"] = np.ascontiguousarray(uw[core * 512:(core + 1) * 512, :])
    d["fw_dshard"] = np.ascontiguousarray(dw[core * 128:(core + 1) * 128, :])
    for n in "qkvgo":
        d["a_b" + n] = np.asarray(inputs["L1_attn_b" + n], np.float32).reshape(1, D)
    d["m_ebias"] = np.asarray(inputs["L1_ffn_expert_bias"], np.float32).reshape(1, E16)
    d["m_wg"] = np.ascontiguousarray(
        np.asarray(inputs["L1_ffn_w_gate"], np.float32)[core * EPC:(core + 1) * EPC])
    d["m_wu"] = np.ascontiguousarray(
        np.asarray(inputs["L1_ffn_w_up"], np.float32)[core * EPC:(core + 1) * EPC])
    d["m_wd"] = np.ascontiguousarray(
        np.asarray(inputs["L1_ffn_w_down"], np.float32)[core * EPC:(core + 1) * EPC])
    sm = np.zeros((128, 16), np.float32)
    sl = np.zeros((128, 4), np.float32)
    for j in range(4):
        if j < qt:
            sl[:, j] = 1.0
        for l in range(4):
            if j < l < qt:
                sm[:, j * 4 + l] = 1.0
    d["smask"], d["ssel"] = sm, sl
    cm = np.zeros((4, 2), np.float32)
    cm[:, 1] = 1.0
    for j in range(4):
        if j < qt:
            cm[j, 0] = 1.0
    d["cmask"] = cm
    es = np.zeros((16, EPC), np.float32)
    for i in range(EPC):
        es[core * EPC + i, i] = 1.0
    d["esel"] = es
    return d


_BUILD_CACHE = {}


def kernel(**inputs):
    key = "full"
    if key not in _BUILD_CACHE:
        _BUILD_CACHE[key] = KB().build()
    kb = _BUILD_CACHE[key]
    in_maps = [host_inputs(inputs, c) for c in range(8)]
    res = run_bass_kernel_spmd(kb.nc, in_maps, core_ids=list(range(8)))
    out = np.empty((2, 8192, 1024), np.float32)
    for c in range(8):
        b, qt = c // 4, c % 4
        out[b, qt * T:(qt + 1) * T, :] = res.results[c]["out"]
    return out


# version 03
# speedup vs baseline: 1.3282x; 1.3282x over previous
import contextlib
import numpy as np

import concourse.bass as bass
import concourse.mybir as mybir
from concourse.bass_utils import run_bass_kernel_spmd

F32 = mybir.dt.float32
BF16 = mybir.dt.bfloat16
AF = mybir.ActivationFunctionType
ALU = mybir.AluOpType
AX = mybir.AxisListType

T, NT, D, DCH = 2048, 16, 1024, 8
NM, H, HDK, HDV, DK, DV = 4, 4, 128, 256, 512, 1024
CAP = 4096.0
GATE_NORM = 16.0
SUB, NSUB, CHUNK = 512, 4, 128
SCALE_Q = float(HDK) ** -0.5
LN_EPS = 1e-5
RMS_EPS = 1e-5
FP4_MAX, FP8_MAX = 6.0, 448.0
E16, EPC, I_FFN, I_EXP = 16, 2, 4096, 1024
ROUTE_SCALE = 2.826
H1, HD1 = 16, 64
K_HALF, K_ONE, K_TWO = 96.0, 192.0, 384.0
VELT = float(2 ** 20 + 1)
SUBN = 2.0 ** -6
K_SUBN = 1.5 * (2 ** 23) * (2 ** -9)
GRP4 = [[0, 1, 2, 3], [4, 5, 6, 7]]
GRP8 = [list(range(8))]


class Prog:
    def __init__(self, nc):
        self.nc = nc
        self.entries = []
        self.last_w = {}
        self.readers = {}
        self.gap_fn = None
        self._last_on = {}

    def _chan(self, e):
        return ("cc",) if e["kind"] == "cc" else (e["kind"], e["eng"])

    def emit(self, eng, fn, reads=(), writes=(), kind="compute", tag="", outw=None):
        ch = ("cc",) if kind == "cc" else (kind, eng)
        hist = self._last_on.get(ch, [])
        need_gap = False
        if kind == "compute" and self.gap_fn is not None and eng == "vector":
            rw = set(reads) | set(writes)
            if hist:
                p1 = hist[-1]
                if p1[1] is not None and p1[1] < 128 and (set(p1[0]) & rw):
                    need_gap = True
                if (len(hist) > 1 and p1[1] is not None and p1[1] < 128):
                    p2 = hist[-2]
                    if p2[1] is not None and p2[1] < 128 and (set(p2[0]) & rw):
                        need_gap = True
        if need_gap:
            ge = dict(idx=len(self.entries), eng=eng, fn=self.gap_fn(eng),
                      deps=set(), kind="compute", need_inc=False, tag="gap")
            self.entries.append(ge)
            hist = hist + [((), 1024)]
        self._last_on[ch] = (hist + [(tuple(writes), outw)])[-2:]
        idx = len(self.entries)
        deps = set()
        for k in reads:
            if k in self.last_w:
                deps.add(self.last_w[k])
        for k in writes:
            if k in self.last_w:
                deps.add(self.last_w[k])
            for r in self.readers.get(k, ()):
                deps.add(r)
        self.entries.append(dict(idx=idx, eng=eng, fn=fn, deps=deps,
                                 kind=kind, need_inc=False, tag=tag))
        for k in reads:
            self.readers.setdefault(k, []).append(idx)
        for k in writes:
            self.last_w[k] = idx
            self.readers[k] = []
        return idx

    def dma(self, eng, out, in_, reads=(), writes=(), tag="", **kw):
        return self.emit(eng,
                         lambda e, o=out, i=in_, kw=kw: e.dma_start(out=o, in_=i, **kw),
                         reads=reads, writes=writes, kind="dma", tag=tag)

    def cc(self, kind, op, groups, in_ap, out_ap, reads=(), writes=(), tag=""):
        return self.emit(
            "gpsimd",
            lambda e, k=kind, o=op, g=groups, ia=in_ap, oa=out_ap:
                e.collective_compute(k, o, replica_groups=g, ins=[ia], outs=[oa]),
            reads=reads, writes=writes, kind="cc", tag=tag)

    def finalize(self):
        for i, e in enumerate(self.entries):
            e["idx"] = i
        for e in self.entries:
            ce = self._chan(e)
            for d in e["deps"]:
                de = self.entries[d]
                if self._chan(de) != ce:
                    de["need_inc"] = True
        for e in self.entries:
            if e["kind"] in ("dma", "cc"):
                e["need_inc"] = True
        cnt = {}
        for e in self.entries:
            c = self._chan(e)
            n = cnt.get(c, 0) + (1 if e["need_inc"] else 0)
            cnt[c] = n
            e["cum"] = n
        self.totals = dict(cnt)
        hw = {}
        for e in self.entries:
            waits = {}
            for d in sorted(e["deps"]):
                de = self.entries[d]
                dc = self._chan(de)
                if dc == self._chan(e):
                    continue
                waits[dc] = max(waits.get(dc, 0), de["cum"])
            wl = []
            for dc in sorted(waits, key=str):
                cum = waits[dc]
                key = (e["eng"], dc)
                if hw.get(key, 0) >= cum:
                    continue
                hw[key] = cum
                wl.append((dc, cum * (16 if dc[0] == "dma" else 1)))
            e["waits"] = wl
        self.chans = sorted({self._chan(e) for e in self.entries}, key=str)

    def build_block(self):
        nc = self.nc
        sems = {}
        stack = contextlib.ExitStack()
        for c in self.chans:
            sems[c] = stack.enter_context(nc.semaphore("s_" + "_".join(map(str, c))))

        def run(eng_obj, name):
            for e in self.entries:
                if e["eng"] != name:
                    continue
                for dc, val in e["waits"]:
                    eng_obj.wait_ge(sems[dc], val)
                inst = e["fn"](eng_obj)
                if e["need_inc"]:
                    inst.then_inc(sems[self._chan(e)], 16 if e["kind"] == "dma" else 1)
            if name == "gpsimd":
                for c in self.chans:
                    tot = self.totals.get(c, 0)
                    if tot:
                        eng_obj.wait_ge(sems[c], tot * (16 if c[0] == "dma" else 1))

        with nc.Block() as block:
            block.sync(lambda e: run(e, "sync"))
            block.tensor(lambda e: run(e, "tensor"))
            block.vector(lambda e: run(e, "vector"))
            block.scalar(lambda e: run(e, "scalar"))
            block.gpsimd(lambda e: run(e, "gpsimd"))
        stack.close()


class KB:
    def __init__(self, taps=()):
        import ml_dtypes
        ml = ml_dtypes
        nc = bass.Bass("TRN2", target_bir_lowering=False)
        self.nc = nc
        self.taps = set(taps)
        self.tap_d = {}
        self.ctx = contextlib.ExitStack()

        def din(name, shape):
            return nc.dram_tensor(name, list(shape), F32, kind="ExternalInput").ap()

        self.x_in = din("x", (T, D))
        self.w_gate = din("w_gate", (D, NM))
        self.w_q = din("w_q", (D, DK))
        self.w_k = din("w_k", (NM, D, DK))
        self.w_v = din("w_v", (NM, D, DV))
        self.w_gk1 = din("w_gk1", (NM, D, 16))
        self.w_gk2 = din("w_gk2", (NM, 16, DK))
        self.b_gk = din("b_gk", (NM, DK))
        self.w_ks = din("w_ks", (D, DK))
        self.w_vs = din("w_vs", (D, DV))
        self.w_gk1s = din("w_gk1s", (D, 16))
        self.w_gk2s = din("w_gk2s", (16, DK))
        self.b_gks = din("b_gks", (1, DK))
        self.w_g = din("w_g", (D, DV))
        self.w_o = din("w_o", (DV, D))
        self.norm_w = din("norm_w", (1, HDV))
        self.fw_gshard = din("fw_gshard", (512, D))
        self.fw_ushard = din("fw_ushard", (512, D))
        self.fw_dshard = din("fw_dshard", (128, I_FFN))
        self.f_s_in = din("f_s_in", (1, 1))
        self.f_s_in_down = din("f_s_in_down", (1, 1))
        self.a_W = {n: din("a_W" + n, (D, D)) for n in "qkvgo"}
        self.a_b = {n: din("a_b" + n, (1, D)) for n in "qkvgo"}
        self.m_gate_w = din("m_gate_w", (E16, D))
        self.m_ebias = din("m_ebias", (1, E16))
        self.m_wg = din("m_wg", (EPC, I_EXP, D))
        self.m_wu = din("m_wu", (EPC, I_EXP, D))
        self.m_wd = din("m_wd", (EPC, D, I_EXP))
        self.m_sg = din("m_sg", (I_EXP, D))
        self.m_su = din("m_su", (I_EXP, D))
        self.m_sd = din("m_sd", (D, I_EXP))
        self.smask = din("smask", (128, 16))
        self.ssel = din("ssel", (128, 4))
        self.cmask = din("cmask", (4, 2))
        self.esel = din("esel", (128, 4 * EPC))
        self.out_d = nc.dram_tensor("out", [T, D], F32, kind="ExternalOutput").ap()

        tri = np.triu(np.ones((128, 128), np.float32))
        self.c_utri = nc.inline_tensor(np.ascontiguousarray(tri), name="c_utri").ap()
        self.c_utri_gk = nc.inline_tensor(
            np.ascontiguousarray((tri * np.float32(-1.0 / GATE_NORM)).astype(np.float32)),
            name="c_utri_gk").ap()
        self.c_ident = nc.inline_tensor(np.eye(128, dtype=ml.bfloat16), name="c_ident").ap()
        self.c_identf = nc.inline_tensor(np.eye(128, dtype=np.float32), name="c_identf").ap()
        self.c_ptmask = nc.inline_tensor(np.ascontiguousarray(tri).astype(ml.bfloat16),
                                         name="c_ptmask").ap()
        self.c_ones1f = nc.inline_tensor(np.ones((1, 128), np.float32), name="c_ones1f").ap()
        self.c_ones1b = nc.inline_tensor(np.ones((1, 128), ml.bfloat16), name="c_ones1b").ap()
        self.c_ones512b = nc.inline_tensor(np.ones((1, 512), ml.bfloat16), name="c_ones512b").ap()
        self.c_onescf = nc.inline_tensor(np.ones((128, 1), np.float32), name="c_onescf").ap()
        self.CBVALS = [LN_EPS, 1.0 / D, -0.5, 1.0 / HDV]
        cb_np = np.broadcast_to(np.array(self.CBVALS, np.float32)[None, :],
                                (128, len(self.CBVALS))).copy()
        self.c_cbank = nc.inline_tensor(cb_np, name="c_cbank").ap()

        def dscr(name, shape, dt=F32):
            return nc.dram_tensor(name, list(shape), dt, kind="Internal").ap()

        self.d_wkb = dscr("d_wkb", (5, D, DK), BF16)
        self.d_wvb = dscr("d_wvb", (5, D, DV), BF16)
        self.d_wqb = dscr("d_wqb", (D, DK), BF16)
        self.d_wgb = dscr("d_wgb", (D, DV), BF16)
        self.d_wob = dscr("d_wob", (DV, D), BF16)
        self.d_aWb = {n: dscr("d_aWb" + n, (D, D), BF16) for n in "qkvgo"}
        self.d_cnt_in = dscr("d_cnt_in", (1, 8))
        self.d_cnt_out = dscr("d_cnt_out", (1, 32))
        self.d_stg_in = dscr("d_stg_in", (128, 5 * 1024), BF16)
        self.d_stg_out = dscr("d_stg_out", (4, 128, 5 * 1024), BF16)
        self.d_stgA_in = dscr("d_stgA_in", (128, 20))
        self.d_stgA_out = dscr("d_stgA_out", (4, 128, 20))
        self.d_wmax_in = dscr("d_wmax_in", (1, 8))
        self.d_wmax_out = dscr("d_wmax_out", (1, 8))
        self.d_gwq_in = dscr("d_gwq_in", (D, 512), BF16)
        self.d_gwq = dscr("d_gwq", (8, D, 512), BF16)
        self.d_uwq_in = dscr("d_uwq_in", (D, 512), BF16)
        self.d_uwq = dscr("d_uwq", (8, D, 512), BF16)
        self.d_dwq_in = dscr("d_dwq_in", (I_FFN, 128), BF16)
        self.d_dwq = dscr("d_dwq", (8, I_FFN, 128), BF16)
        self.d_xqT = dscr("d_xqT", (NT, 128, DCH * 128), BF16)
        self.d_kv_in = dscr("d_kv_in", (64, 1024))
        self.d_kv_out = dscr("d_kv_out", (64, 1024))
        self.d_xnt_in = dscr("d_xnt_in", (D, T), BF16)
        self.d_xnt = dscr("d_xnt", (8, D, T), BF16)
        self.d_c_in = dscr("d_c_in", (T, E16))
        self.d_c_out = dscr("d_c_out", (8, T, E16))
        self.d_y_in = dscr("d_y_in", (8 * T, D), BF16)
        self.d_y_out = dscr("d_y_out", (T, D), BF16)
        self.d_ewT = dscr("d_ewT", (EPC, 3, I_EXP, D), BF16)
        self.d_swT = dscr("d_swT", (3, I_EXP, D), BF16)

        self.sb = {}

        def S(name, shape, dt=F32):
            t = self.ctx.enter_context(nc.sbuf_tensor(name, list(shape), dt))
            self.sb[name] = t
            return t
        self.S = S

        self.stream = S("stream", (128, NT * D))
        self.nT = S("nT", (128, DCH * T), BF16)
        self.sts = [S(f"st{i}", (128, H * HDV), BF16) for i in range(5)]
        self.XA = S("XA", (128, 24576), BF16)
        self.WS0 = S("WS0", (128, 4096), BF16)
        self.WS1 = S("WS1", (128, 4096), BF16)
        self.WS2 = S("WS2", (128, 4096), BF16)

        self.mrs = S("mrs", (128, 32))
        self.rt = S("rt", (128, 768))
        self.sm = S("sm", (128, 256))
        self.t1a = S("t1a", (32, 512), BF16)
        self.wgk2a = S("wgk2a", (32, 5 * 512), BF16)
        self.wgk1b = S("wgk1b", (128, 5 * 128), BF16)
        self.brow = {n: S("brow_" + n, (1, 1024), BF16) for n in "qkvgo"}
        self.bgkrow = S("bgkrow", (1, 512), BF16)
        self.bcol = S("bcol", (128, 16))
        self.gwt = S("gwt", (128, 128), BF16)
        self.rowb = S("rowb", (1, 128))
        self.wgate_b = S("wgate_b", (128, 32), BF16)
        self.cst_u = S("cst_u", (128, 128))
        self.cst_ug = S("cst_ug", (128, 128))
        self.cst_i = S("cst_i", (128, 128), BF16)
        self.cst_if = S("cst_if", (128, 128))
        self.cst_pt = S("cst_pt", (128, 128), BF16)
        self.cst_o1f = S("cst_o1f", (1, 128))
        self.cst_o1b = S("cst_o1b", (1, 128), BF16)
        self.cst_ocf = S("cst_ocf", (128, 1))
        self.nwb = S("nwb", (128, 256))
        self.msk_s = S("msk_s", (128, 16))
        self.msk_sel = S("msk_sel", (128, 4))
        self.msk_c = S("msk_c", (4, 2))
        self.esel_sb = S("esel_sb", (128, 4 * EPC))
        self.cnt4 = S("cnt4", (4, 8))
        self.wmax_row = S("wmax_row", (1, 8))
        self.srow = S("srow", (1, 8))
        self.nw_row = S("nw_row", (1, 256))
        self.cbank = S("cbank", (128, len(self.CBVALS)))
        self.cmy = S("cmy", (128, 256))
        self.junk = S("junk", (128, 256), BF16)
        for i, v in enumerate(self.CBVALS):
            nc.const_aps.register(F32, float(v), self.cbank[:, i:i + 1])

        def xav(off_b, cols, dt):
            es = 4 if dt == F32 else 2
            ap = self.XA[:, off_b // 2: (off_b + cols * es) // 2].bitcast(dt)
            g0, g1 = off_b // 4096, (off_b + cols * es - 1) // 4096
            return ap, tuple(f"xa{g}" for g in range(g0, g1 + 1))
        self.xav = xav

        self.oacc, self.K_oacc = xav(0, 4096, F32)
        self.qfm, self.K_qfm = xav(16384, 2048, BF16)
        self.kfm, self.K_kfm = xav(20480, 1024, BF16)
        self.eab, self.K_eab = xav(22528, 512, BF16)
        self.eib, self.K_eib = xav(23552, 512, BF16)
        self.vtok, self.K_vtok = xav(24576, 2048, BF16)
        self.gktok, self.K_gktok = xav(28672, 1024, F32)
        self.qcb, self.K_qcb = xav(32768, 512, BF16)
        self.kpb, self.K_kpb = xav(33792, 512, BF16)
        self.kptb, self.K_kptb = xav(34816, 512, BF16)
        self.ptb, self.K_ptb = xav(35840, 512, BF16)
        self.tmpA, self.K_tmpA = xav(36864, 1024, F32)
        self.tmpB, self.K_tmpB = xav(40960, 1024, F32)
        self.tmpC, self.K_tmpC = xav(45056, 1024, BF16)
        self.tmpD, self.K_tmpD = xav(47104, 1024, BF16)
        self.wstg, self.K_wstg = xav(36864, 2048, F32)
        self.xtr, self.K_xtr = xav(32768, 1024, BF16)
        self.yacc, self.K_yacc = xav(0, 4096, F32)
        self.xqt_s, self.K_xqt = xav(16384, 1024, BF16)
        self.f_sil, self.K_fsil = xav(18432, 512, F32)
        self.hqt_s, self.K_hqt = xav(20480, 2048, BF16)
        self.f_hh, self.K_fhh = xav(24576, 2048, F32)
        self.f_am, self.K_fam = xav(32768, 128, F32)
        self.f_rc, self.K_frc = xav(33280, 128, F32)
        self.f_w1, self.K_fw1 = xav(33792, 128, F32)
        self.f_w2, self.K_fw2 = xav(34304, 128, F32)
        self.f_bs1, self.K_fbs1 = xav(34816, 2048, BF16)
        self.f_ph, self.K_fph = xav(38912, 2048, BF16)
        self.l1q, self.K_l1q = xav(0, 4096, BF16)
        self.l1kf, self.K_l1kf = xav(8192, 1024, BF16)
        self.l1vf, self.K_l1vf = xav(10240, 1024, BF16)
        self.l1e1, self.K_l1e1 = xav(12288, 1024, F32)
        self.l1e2, self.K_l1e2 = xav(16384, 1024, F32)
        self.l1sg, self.K_l1sg = xav(20480, 1024, F32)
        self.l1ga, self.K_l1ga = xav(24576, 1024, BF16)
        self.l1gt, self.K_l1gt = xav(26624, 1024, BF16)
        self.e_xnt2 = [xav(0, 1024, BF16), xav(4096, 1024, BF16)]
        self.e_hT2 = [xav(2048, 1024, BF16), xav(6144, 1024, BF16)]
        self.e_h2 = [xav(8192, 1024, BF16), xav(12288, 1024, BF16)]
        self.e_y2 = [xav(10240, 1024, BF16), xav(14336, 1024, BF16)]
        self.e_sil2 = [xav(16384, 1024, F32), xav(20480, 1024, F32)]
        self.e_wu, self.K_ewu = xav(24576, 8192, BF16)
        self.e_wg2, self.K_ewg2 = xav(40960, 4096, BF16)
        self.e_st, self.K_est = xav(8192, 1024, BF16)
        _kv, self.K_kvb = xav(28672, 1024, F32)
        self.kvb = _kv[0:64, :]
        self.kvb2, self.K_kvb2 = xav(32768, 1024, BF16)
        self.rowc = self.wstg[0:1, 0:1024]
        self.K_rowc = self.K_wstg
        self.pstg = [xav(i * 4096, 1024, F32) for i in range(3)]
        self.pout = [xav(12288 + i * 4096, 2048, BF16) for i in range(3)]

        self.W0 = self.ctx.enter_context(nc.psum_tensor("W0", [128, 1024], F32))
        self.W1 = self.ctx.enter_context(nc.psum_tensor("W1", [128, 1024], F32))
        self.PA = self.ctx.enter_context(nc.psum_tensor("PA", [128, 512], F32))
        self.PB = self.ctx.enter_context(nc.psum_tensor("PB", [128, 512], F32))
        self.PC = self.ctx.enter_context(nc.psum_tensor("PC", [128, 512], F32))
        self.PD = self.ctx.enter_context(nc.psum_tensor("PD", [128, 512], F32))
        self.PCb = self.PC[:, :].bitcast(BF16)
        self.PDb = self.PD[:, :].bitcast(BF16)
        self.PAb = self.PA[:, :].bitcast(BF16)
        self.W0b = self.W0[:, :].bitcast(BF16)
        self.moe_ps = [
            dict(G=[self.W0[:, 0:512], self.W0[:, 512:1024]], GK=("W0",),
                 Gb=self.W0b[:, 0:1024], U=[self.W1[:, 0:512], self.W1[:, 512:1024]],
                 UK=("W1",), UKn=[("W1",), ("W1",)]),
            dict(G=[self.PA[:, :], self.PB[:, :]], GK=("PA", "PB"),
                 Gb=self.PAb[:, 0:1024], U=[self.PC[:, :], self.PD[:, :]],
                 UK=("PC", "PD"), UKn=[("PC",), ("PD",)]),
        ]

        self.p = Prog(nc)

        def _gap(eng):
            j = self.junk
            if eng == "scalar":
                return lambda e: e.copy(j[:, 128:256], j[:, 0:128])
            return lambda e: e.tensor_copy(j[:, 128:256], j[:, 0:128])
        self.p.gap_fn = _gap

    def mm(self, out, lhsT, rhs, start, stop, reads, writes, tag=""):
        self.p.emit("tensor",
                    lambda e, o=out, l=lhsT, r=rhs, s=start, t=stop:
                        e.matmul(o, l, r, start=s, stop=t),
                    reads=tuple(reads), writes=tuple(writes), tag=tag)

    def tr(self, out, in_, reads, writes, tag="", f32=False):
        idn = self.cst_if if f32 else self.cst_i
        ikey = "cst_if" if f32 else "cst_i"
        kp = in_.shape[0]
        self.p.emit("tensor",
                    lambda e, o=out, i=in_, n=idn, k=kp: e.transpose(o, i, n[0:k, 0:k]),
                    reads=tuple(reads) + (ikey,), writes=tuple(writes), tag=tag)

    def act(self, out, in_, func, reads, writes, bias=0.0, scale=1.0, accum=None, tag=""):
        w = 1 if accum is not None else out.free_size()
        self.p.emit("scalar",
                    lambda e, o=out, i=in_, f=func, b=bias, s=scale, a=accum:
                        e.activation(o, i, f, bias=b, scale=s, accum_out=a),
                    reads=tuple(reads), writes=tuple(writes), tag=tag, outw=w)

    def ts(self, out, in0, s1, s2, op0, op1=None, reads=(), writes=(), eng="vector", tag=""):
        w = out.free_size()
        if s2 is None:
            self.p.emit(eng, lambda e, o=out, i=in0, a=s1, p0=op0:
                        e.tensor_scalar(o, i, a, None, p0),
                        reads=tuple(reads), writes=tuple(writes), tag=tag, outw=w)
        else:
            self.p.emit(eng, lambda e, o=out, i=in0, a=s1, b=s2, p0=op0, p1=op1:
                        e.tensor_scalar(o, i, a, b, p0, p1),
                        reads=tuple(reads), writes=tuple(writes), tag=tag, outw=w)

    def tt(self, out, in0, in1, op, reads, writes, eng="vector", tag=""):
        self.p.emit(eng, lambda e, o=out, a=in0, b=in1, q=op: e.tensor_tensor(o, a, b, q),
                    reads=tuple(reads), writes=tuple(writes), tag=tag,
                    outw=out.free_size())

    def stt(self, out, in0, s, in1, op0, op1, reads, writes, eng="vector", tag=""):
        self.p.emit(eng, lambda e, o=out, a=in0, c=s, b=in1, p0=op0, p1=op1:
                    e.scalar_tensor_tensor(o, a, c, b, p0, p1),
                    reads=tuple(reads), writes=tuple(writes), tag=tag,
                    outw=out.free_size())

    def cp(self, out, in_, reads, writes, eng="vector", tag=""):
        w = out.free_size()
        if eng == "scalar":
            self.p.emit(eng, lambda e, o=out, i=in_: e.copy(o, i),
                        reads=tuple(reads), writes=tuple(writes), tag=tag, outw=w)
        else:
            self.p.emit(eng, lambda e, o=out, i=in_: e.tensor_copy(o, i),
                        reads=tuple(reads), writes=tuple(writes), tag=tag, outw=w)

    def red(self, out, in_, op, reads, writes, absval=False, tag=""):
        self.p.emit("vector", lambda e, o=out, i=in_, q=op, a=absval:
                    e.tensor_reduce(o, i, AX.X, q, apply_absolute_value=a),
                    reads=tuple(reads), writes=tuple(writes), tag=tag,
                    outw=0)

    def recip(self, out, in_, reads, writes, tag=""):
        self.p.emit("vector", lambda e, o=out, i=in_: e.reciprocal(o, i),
                    reads=tuple(reads), writes=tuple(writes), tag=tag, outw=0)

    def mset(self, ap, val, writes, eng="vector", tag=""):
        self.p.emit(eng, lambda e, a=ap, v=val: e.memset(a, v),
                    reads=(), writes=tuple(writes), tag=tag, outw=ap.free_size())

    def scan(self, out, d0, d1, init, op0, op1, reads, writes, tag=""):
        self.p.emit("vector", lambda e, o=out, a=d0, b=d1, i=init, p0=op0, p1=op1:
                    e.tensor_tensor_scan(o, a, b, i, p0, p1),
                    reads=tuple(reads), writes=tuple(writes), tag=tag, outw=0)

    def load_cast(self, dram_ap, dst, dkeys, tag="lc"):
        rows = dram_ap.shape[-2] if dram_ap.ndim >= 2 else 1
        cols = dram_ap.shape[-1]
        stage = self.wstg[0:rows, 0:cols]
        self.p.dma("sync", stage, dram_ap, reads=(), writes=self.K_wstg, tag=tag)
        self.cp(dst, stage, self.K_wstg, tuple(dkeys), tag=tag)

    def tap(self, name, ap, reads):
        if name in self.taps:
            d = self.nc.dram_tensor("tap_" + name, list(ap.shape), ap.dtype,
                                    kind="ExternalOutput").ap()
            self.tap_d[name] = d
            self.p.dma("gpsimd", d, ap, reads=tuple(reads), writes=(), tag="tap")

    def ph_wprep(self):
        S, p = self, self.p
        jobs = []

        def mat(src, dst, dkey):
            rows, cols = src.shape[-2], src.shape[-1]
            for c in range(rows // 128):
                jobs.append((src[c * 128:(c + 1) * 128, :],
                             dst[c * 128:(c + 1) * 128, :], cols, dkey))
        for lane in range(NM):
            mat(S.w_k[lane], S.d_wkb[lane], "d_wkb")
        mat(S.w_ks, S.d_wkb[4], "d_wkb")
        for lane in range(NM):
            mat(S.w_v[lane], S.d_wvb[lane], "d_wvb")
        mat(S.w_vs, S.d_wvb[4], "d_wvb")
        mat(S.w_q, S.d_wqb, "d_wqb")
        mat(S.w_g, S.d_wgb, "d_wgb")
        mat(S.w_o, S.d_wob, "d_wob")
        for n in "qkvgo":
            mat(S.a_W[n], S.d_aWb[n], "d_aWb" + n)
        for i, (src, dst, cols, dkey) in enumerate(jobs):
            b = i % 3
            stg, sk = S.pstg[b][0][:, 0:cols], S.pstg[b][1]
            out, ok = S.pout[b][0][:, 0:cols], S.pout[b][1]
            p.dma("sync", stg, src, writes=sk, tag="wp")
            S.cp(out, stg, sk, ok, eng=("gpsimd" if i % 2 == 0 else "scalar"),
                 tag="wp")
            p.dma("sync", dst, out, reads=ok, writes=(dkey,), tag="wp")

    def ph_gkload(self):
        S, p = self, self.p
        for lane in range(5):
            if lane < NM:
                wgk1, wgk2 = S.w_gk1[lane], S.w_gk2[lane]
                bgk = S.b_gk[lane:lane + 1, :]
            else:
                wgk1, wgk2, bgk = S.w_gk1s, S.w_gk2s, S.b_gks
            stage1 = S.wstg[:, 0:128]
            p.dma("sync", stage1.rearrange("p (c r) -> p c r", r=16),
                  wgk1.rearrange("(c p) r -> p c r", p=128), writes=S.K_wstg,
                  tag="wgk1")
            S.cp(S.wgk1b[:, lane * 128:(lane + 1) * 128], stage1, S.K_wstg,
                 ("wgk1b",), tag="wgk1")
            S.load_cast(wgk2[:, :], S.wgk2a[0:16, lane * 512:(lane + 1) * 512],
                        ("wgk2a",), tag="wgk2")
            S.load_cast(bgk, S.bgkrow[:, :], ("bgkrow",), tag="bgk")
            p.dma("sync", S.wgk2a[16:17, lane * 512:(lane + 1) * 512],
                  S.bgkrow[:, :], reads=("bgkrow",), writes=("wgk2a",), tag="bgk2")

    def ph_consts(self):
        p, S = self.p, self
        p.dma("sync", S.stream[:, :].rearrange("p (t d) -> p t d", d=D),
              S.x_in.rearrange("(t p) d -> p t d", p=128), writes=("stream",), tag="ldx")
        for dst, src, k in ((S.cst_u, S.c_utri, "cst_u"), (S.cst_ug, S.c_utri_gk, "cst_ug"),
                            (S.cst_i, S.c_ident, "cst_i"), (S.cst_if, S.c_identf, "cst_if"),
                            (S.cst_pt, S.c_ptmask, "cst_pt"), (S.cst_o1f, S.c_ones1f, "cst_o1f"),
                            (S.cst_o1b, S.c_ones1b, "cst_o1b"), (S.cst_ocf, S.c_onescf, "cst_ocf"),
                            (S.msk_s, S.smask, "msk_s"), (S.msk_sel, S.ssel, "msk_sel"),
                            (S.msk_c, S.cmask, "msk_c"), (S.esel_sb, S.esel, "esel_sb"),
                            (S.nw_row, S.norm_w, "nw_row")):
            p.dma("sync", dst[:, :], src, writes=(k,))
        p.dma("sync", S.cbank[:, :], S.c_cbank, writes=("cbank",))
        S.cp(S.sm[:, 255:256], S.cbank[:, 0:1], ("cbank",), ("sm",), eng="scalar")
        S.cp(S.sm[:, 254:255], S.cbank[:, 0:1], ("cbank",), ("sm",), eng="vector")
        p.dma("sync", S.srow[:, 0:1], S.f_s_in, writes=("srow",))
        p.dma("sync", S.srow[:, 1:2], S.f_s_in_down, writes=("srow",))
        S.mm(S.W0[:, 0:256], S.cst_o1f[:, :], S.nw_row[:, :], True, True,
             ("cst_o1f", "nw_row"), ("W0",))
        S.cp(S.nwb[:, :], S.W0[:, 0:256], ("W0",), ("nwb",))
        p.dma("sync", S.t1a[16:17, :], S.c_ones512b, writes=("t1a_one",))

    def ph_wamax(self):
        p, S, sm = self.p, self, self.sm

        def shard_amax(shard_ap, nrow, ncol, col):
            for i, r0 in enumerate(range(0, nrow, 128)):
                npiece = min(ncol, 2048)
                for j, c0 in enumerate(range(0, ncol, npiece)):
                    stage = S.wstg[:, 0:npiece]
                    p.dma("sync", stage, shard_ap[r0:r0 + 128, c0:c0 + npiece],
                          writes=S.K_wstg, tag="wamax")
                    dst = sm[:, col:col + 1] if (i == 0 and j == 0) else sm[:, col + 4:col + 5]
                    S.red(dst, stage, ALU.max, S.K_wstg, ("sm",), absval=True)
                    if not (i == 0 and j == 0):
                        S.tt(sm[:, col:col + 1], sm[:, col:col + 1], sm[:, col + 4:col + 5],
                             ALU.max, ("sm",), ("sm",))

        shard_amax(S.fw_gshard, 512, D, 0)
        shard_amax(S.fw_ushard, 512, D, 1)
        shard_amax(S.fw_dshard, 128, I_FFN, 2)
        S.tr(S.PA[0:4, 0:128], sm[:, 0:4], ("sm",), ("PA",), f32=True)
        S.red(sm[0:4, 8:9], S.PA[0:4, 0:128], ALU.max, ("PA",), ("sm",))
        p.dma("gpsimd", S.d_wmax_in[:, 0:4], sm[0:4, 8:9],
              reads=("sm",), writes=("d_wmax_in",))
        p.cc("AllReduce", ALU.max, GRP8, S.d_wmax_in, S.d_wmax_out,
             reads=("d_wmax_in",), writes=("d_wmax_out",), tag="wAR")
        p.dma("sync", S.wmax_row[:, :], S.d_wmax_out, reads=("d_wmax_out",),
              writes=("wmax_row",))
        S.mm(S.PA[:, 0:8], S.cst_o1f[:, :], S.wmax_row[:, :], True, True,
             ("cst_o1f", "wmax_row"), ("PA",))
        S.cp(sm[:, 16:24], S.PA[:, 0:8], ("PA",), ("sm",))
        S.mm(S.PA[:, 8:16], S.cst_o1f[:, :], S.srow[:, :], True, True,
             ("cst_o1f", "srow"), ("PA",))
        S.cp(sm[:, 24:32], S.PA[:, 8:16], ("PA",), ("sm",))
        for j in range(3):
            S.recip(sm[:, 32 + j:33 + j], sm[:, 16 + j:17 + j], ("sm",), ("sm",))
            S.ts(sm[:, 32 + j:33 + j], sm[:, 32 + j:33 + j], FP8_MAX * FP4_MAX, None,
                 ALU.mult, reads=("sm",), writes=("sm",))
            S.ts(sm[:, 40 + j:41 + j], sm[:, 32 + j:33 + j], 1.0 / 6.0, None,
                 ALU.mult, reads=("sm",), writes=("sm",))
        for j, gsa in enumerate((24, 24, 25)):
            S.recip(sm[:, 35 + j:36 + j], sm[:, gsa:gsa + 1], ("sm",), ("sm",))
            S.tt(sm[:, 35 + j:36 + j], sm[:, 35 + j:36 + j], sm[:, 16 + j:17 + j],
                 ALU.mult, ("sm",), ("sm",))
            S.ts(sm[:, 35 + j:36 + j], sm[:, 35 + j:36 + j],
                 1.0 / (FP8_MAX * FP4_MAX), None, ALU.mult, reads=("sm",), writes=("sm",))
        S.ts(sm[:, 38:39], sm[:, 24:25], 1.0 / 6.0, None, ALU.mult,
             reads=("sm",), writes=("sm",))
        S.ts(sm[:, 39:40], sm[:, 25:26], 1.0 / 6.0, None, ALU.mult,
             reads=("sm",), writes=("sm",))

    def e4m3_round(self, io_ap, w1, w2, keys):
        S = self
        S.ts(w1, io_ap, VELT, None, ALU.mult, reads=keys, writes=keys)
        S.tt(w2, w1, io_ap, ALU.subtract, keys, keys)
        S.tt(w1, w1, w2, ALU.subtract, keys, keys)
        S.ts(w2, io_ap, K_SUBN, None, ALU.add, reads=keys, writes=keys)
        S.ts(w2, w2, K_SUBN, None, ALU.subtract, reads=keys, writes=keys)
        S.ts(io_ap, io_ap, SUBN, None, ALU.is_ge, reads=keys, writes=keys)
        S.tt(w1, w1, w2, ALU.subtract, keys, keys)
        S.tt(w1, w1, io_ap, ALU.mult, keys, keys)
        S.tt(io_ap, w1, w2, ALU.add, keys, keys)

    def fp4_quant(self, x_ap, n, gs6_col, gs_col, sc8, rcp, w1, w2,
                  out_bf, s1, keys):
        S = self
        nb = n // 16
        keys = tuple(keys) + ("sm",)
        S.red(sc8, x_ap.rearrange("p (b k) -> p b k", k=16), ALU.max, keys, keys,
              absval=True)
        S.ts(sc8, sc8, gs6_col, FP8_MAX, ALU.mult, ALU.min, reads=keys, writes=keys)
        S.e4m3_round(sc8, w1, w2, keys)
        S.ts(w1, sc8, 0.0, None, ALU.is_equal, reads=keys, writes=keys)
        S.tt(w2, sc8, w1, ALU.add, keys, keys)
        S.recip(rcp, w2, keys, keys)
        S.ts(w2, w1, -1.0, 1.0, ALU.mult, ALU.add, reads=keys, writes=keys)
        S.stt(w2, w2, gs_col, w1, ALU.mult, ALU.add, keys, keys)
        S.tt(rcp, rcp, w2, ALU.mult, keys, keys)
        rb = rcp.rearrange("p b -> p b ()").broadcast_to([128, nb, 16])
        S.tt(x_ap.rearrange("p (b k) -> p b k", k=16),
             x_ap.rearrange("p (b k) -> p b k", k=16), rb, ALU.mult, keys, keys)
        S.ts(x_ap, x_ap, FP4_MAX, -FP4_MAX, ALU.min, ALU.max, reads=keys, writes=keys)

        def rne(dst, K):
            S.ts(dst, x_ap, K, None, ALU.add, reads=keys, writes=keys)
            S.ts(dst, dst, K, None, ALU.subtract, reads=keys, writes=keys)

        def clip(dst, lim):
            S.ts(dst, dst, lim, -lim, ALU.min, ALU.max, reads=keys, writes=keys)

        rne(s1, K_HALF); clip(s1, 2.0)
        rne(out_bf, K_ONE); clip(out_bf, 4.0)
        S.tt(s1, s1, out_bf, ALU.add, keys, keys)
        clip(out_bf, 2.0)
        S.tt(s1, s1, out_bf, ALU.subtract, keys, keys)
        rne(out_bf, K_TWO); clip(out_bf, 6.0)
        S.tt(s1, s1, out_bf, ALU.add, keys, keys)
        clip(out_bf, 4.0)
        S.tt(s1, s1, out_bf, ALU.subtract, keys, keys)
        sb8 = sc8.rearrange("p b -> p b ()").broadcast_to([128, nb, 16])
        S.tt(out_bf.rearrange("p (b k) -> p b k", k=16),
             s1.rearrange("p (b k) -> p b k", k=16), sb8, ALU.mult, keys, keys)

    def ph_ln(self):
        S = self
        for t in range(NT):
            xs = S.stream[:, t * D:(t + 1) * D]
            S.red(S.sm[:, 60:61], xs, ALU.add, ("stream",), ("sm",))
            S.ts(S.mrs[:, t:t + 1], S.sm[:, 60:61], 1.0 / D, None,
                 ALU.mult, reads=("sm",), writes=("mrs",))
            S.ts(S.tmpA, xs, S.mrs[:, t:t + 1], None, ALU.subtract,
                 reads=("stream", "mrs"), writes=S.K_tmpA)
            S.act(S.tmpB, S.tmpA, AF.Square, S.K_tmpA, S.K_tmpB + ("sm",),
                  accum=S.sm[:, 61:62])
            S.act(S.sm[:, 62:63], S.sm[:, 61:62], AF.Ln, ("sm",), ("sm",),
                  scale=1.0 / D, bias=LN_EPS)
            S.act(S.mrs[:, 16 + t:17 + t], S.sm[:, 62:63], AF.Exp,
                  ("sm",), ("mrs",), scale=-0.5)
            S.ts(S.tmpC, S.tmpA, S.mrs[:, 16 + t:17 + t], None, ALU.mult,
                 reads=S.K_tmpA + ("mrs",), writes=S.K_tmpC)
            for c in range(DCH):
                S.tr(S.PCb[:, c * 128:(c + 1) * 128], S.tmpC[:, c * 128:(c + 1) * 128],
                     S.K_tmpC, ("PC",))
            dst = S.nT[:, :].rearrange("p (c k) -> p c k", k=T)[:, :, t * 128:(t + 1) * 128]
            S.cp(dst, S.PCb[:, 0:1024].rearrange("p (c k) -> p c k", k=128),
                 ("PC",), ("nT",))
        S.tap("nT", S.nT[:, :], ("nT",))
        S.tap("mrs", S.mrs[:, :], ("mrs",))

    def ph_routing(self):
        S, p, sm, rt = self, self.p, self.sm, self.rt
        LG, RK, KP, WW, RR, EE = 0, 64, 128, 192, 256, 320
        self.RT_LG, self.RT_RK, self.RT_KP, self.RT_WW = LG, RK, KP, WW
        for c in range(DCH):
            S.load_cast(S.w_gate[c * 128:(c + 1) * 128, :],
                        S.wgate_b[:, c * 4:(c + 1) * 4], ("wgate_b",), tag="wgate")
        for t in range(NT):
            for c in range(DCH):
                S.mm(S.PD[:, 0:4], S.nT[:, c * T + t * 128: c * T + (t + 1) * 128],
                     S.wgate_b[:, c * 4:(c + 1) * 4], c == 0, c == DCH - 1,
                     ("nT", "wgate_b"), ("PD",))
            dst = rt[:, LG:LG + 64].rearrange("p (m k) -> p m k", k=16)[:, :, t:t + 1]
            S.cp(dst, S.PD[:, 0:4].rearrange("p (m o) -> p m o", o=1),
                 ("PD",), ("rt",))
        for m in range(NM):
            first = True
            for j in range(NM):
                if j == m:
                    continue
                S.tt(rt[:, EE:EE + 16], rt[:, LG + j * 16:LG + j * 16 + 16],
                     rt[:, LG + m * 16:LG + m * 16 + 16], ALU.is_gt,
                     ("rt",), ("rt",))
                if first:
                    S.cp(rt[:, RK + m * 16:RK + m * 16 + 16], rt[:, EE:EE + 16],
                         ("rt",), ("rt",))
                    first = False
                else:
                    S.tt(rt[:, RK + m * 16:RK + m * 16 + 16],
                         rt[:, RK + m * 16:RK + m * 16 + 16], rt[:, EE:EE + 16],
                         ALU.add, ("rt", "rt"), ("rt",))
                if j < m:
                    S.tt(rt[:, EE:EE + 16], rt[:, LG + j * 16:LG + j * 16 + 16],
                         rt[:, LG + m * 16:LG + m * 16 + 16], ALU.is_equal,
                         ("rt",), ("rt",))
                    S.tt(rt[:, RK + m * 16:RK + m * 16 + 16],
                         rt[:, RK + m * 16:RK + m * 16 + 16], rt[:, EE:EE + 16],
                         ALU.add, ("rt", "rt"), ("rt",))
        S.ts(rt[:, RR:RR + 64], rt[:, RK:RK + 64], 1.5, None, ALU.is_lt,
             reads=("rt",), writes=("rt",))
        S.tt(rt[:, EE:EE + 16], rt[:, LG:LG + 16], rt[:, LG + 16:LG + 32], ALU.max,
             ("rt",), ("rt",))
        S.tt(sm[:, 64:80], rt[:, LG + 32:LG + 48], rt[:, LG + 48:LG + 64], ALU.max,
             ("rt",), ("sm",))
        S.tt(rt[:, EE:EE + 16], rt[:, EE:EE + 16], sm[:, 64:80], ALU.max,
             ("rt", "sm"), ("rt",))
        for m in range(NM):
            S.tt(rt[:, WW + m * 16:WW + m * 16 + 16],
                 rt[:, LG + m * 16:LG + m * 16 + 16], rt[:, EE:EE + 16],
                 ALU.subtract, ("rt", "rt"), ("rt",))
        S.act(rt[:, WW:WW + 64], rt[:, WW:WW + 64], AF.Exp, ("rt",), ("rt",))
        S.tt(rt[:, WW:WW + 64], rt[:, WW:WW + 64], rt[:, RR:RR + 64], ALU.mult,
             ("rt", "rt"), ("rt",))
        S.tt(sm[:, 64:80], rt[:, WW:WW + 16], rt[:, WW + 16:WW + 32], ALU.add,
             ("rt",), ("sm",))
        S.tt(sm[:, 80:96], rt[:, WW + 32:WW + 48], rt[:, WW + 48:WW + 64], ALU.add,
             ("rt",), ("sm",))
        S.tt(sm[:, 64:80], sm[:, 64:80], sm[:, 80:96], ALU.add, ("sm",), ("sm",))
        S.recip(sm[:, 80:96], sm[:, 64:80], ("sm",), ("sm",))
        for m in range(NM):
            S.tt(rt[:, WW + m * 16:WW + m * 16 + 16],
                 rt[:, WW + m * 16:WW + m * 16 + 16], sm[:, 80:96], ALU.mult,
                 ("rt", "sm"), ("rt",))
        S.mm(S.W0[:, 0:64], S.cst_u[:, :], rt[:, RR:RR + 64], True, False,
             ("cst_u", "rt"), ("W0",))
        S.mm(S.PD[0:1, 0:64], S.cst_ocf[:, :], rt[:, RR:RR + 64], True, True,
             ("cst_ocf", "rt"), ("PD",))
        S.cp(S.rowb[0:1, 0:64], S.PD[0:1, 0:64], ("PD",), ("rowb",))
        for m in range(NM):
            S.scan(S.rowb[0:1, 64 + m * 16 + 1:64 + m * 16 + 16],
                   S.rowb[0:1, m * 16:m * 16 + 15], S.rowb[0:1, m * 16:m * 16 + 15],
                   0.0, ALU.add, ALU.bypass, ("rowb",), ("rowb",))
            S.mset(S.rowb[0:1, 64 + m * 16:64 + m * 16 + 1], 0.0, ("rowb",))
        S.mm(S.W0[:, 0:64], S.cst_o1f[:, :], S.rowb[0:1, 64:128], False, True,
             ("cst_o1f", "rowb"), ("W0",))
        S.cp(rt[:, RK:RK + 64], S.W0[:, 0:64], ("W0",), ("rt",))
        offv = S.rowb[0:1, 64:128].rearrange("a (m k) -> a m k", k=16)[:, :, 15:16]
        totv = S.rowb[0:1, 0:64].rearrange("a (m k) -> a m k", k=16)[:, :, 15:16]
        S.tt(S.rowb[0:1, 124:128].rearrange("a (m o) -> a m o", o=1), offv, totv,
             ALU.add, ("rowb",), ("rowb",))
        p.dma("gpsimd", S.d_cnt_in[:, 0:4], S.rowb[0:1, 124:128],
              reads=("rowb",), writes=("d_cnt_in",))
        p.cc("AllGather", ALU.bypass, GRP4, S.d_cnt_in, S.d_cnt_out,
             reads=("d_cnt_in",), writes=("d_cnt_out",), tag="cntAG")
        p.dma("sync", S.cnt4[:, :], S.d_cnt_out.rearrange("a (r c) -> (a r) c", c=8),
              reads=("d_cnt_out",), writes=("cnt4",))
        S.mm(S.PD[0:4, 0:2], S.cnt4[:, 0:4], S.msk_c[:, :], True, True,
             ("cnt4", "msk_c"), ("PD",))
        S.cp(sm[0:4, 104:106], S.PD[0:4, 0:2], ("PD",), ("sm",))
        S.tt(sm[0:4, 106:107], sm[0:4, 105:106], sm[0:4, 104:105], ALU.subtract,
             ("sm",), ("sm",))
        S.ts(sm[0:4, 106:107], sm[0:4, 106:107], CAP, None, ALU.subtract,
             reads=("sm",), writes=("sm",))
        p.dma("sync", S.rowb[0:1, 96:100], sm[0:4, 106:107],
              reads=("sm",), writes=("rowb",))
        S.mm(S.PD[:, 8:12], S.cst_o1f[:, :], S.rowb[0:1, 96:100], True, True,
             ("cst_o1f", "rowb"), ("PD",))
        S.cp(sm[:, 112:116], S.PD[:, 8:12], ("PD",), ("sm",))
        for m in range(NM):
            S.ts(rt[:, EE:EE + 16], rt[:, RK + m * 16:RK + m * 16 + 16],
                 sm[:, 112 + m:113 + m], None, ALU.is_gt,
                 reads=("rt", "sm"), writes=("rt",))
            S.tt(rt[:, KP + m * 16:KP + m * 16 + 16], rt[:, EE:EE + 16],
                 rt[:, RR + m * 16:RR + m * 16 + 16], ALU.mult,
                 ("rt", "rt"), ("rt",))
            S.tt(rt[:, WW + m * 16:WW + m * 16 + 16],
                 rt[:, WW + m * 16:WW + m * 16 + 16],
                 rt[:, KP + m * 16:KP + m * 16 + 16], ALU.mult,
                 ("rt", "rt"), ("rt",))
        S.tap("rt", rt[:, :], ("rt", "rt", "rt", "rt", "rt"))

    def lane_load_weights(self, lane):
        S, p = self, self.p
        p.dma("sync", S.WS0[:, :].rearrange("p (c r) -> p c r", r=DK),
              S.d_wkb[lane].rearrange("(c p) r -> p c r", p=128),
              reads=("d_wkb",), writes=("WS0",), tag="wk")
        p.dma("sync", S.WS1[:, :].rearrange("p (c r) -> p c r", r=DV),
              S.d_wvb[lane, 0:512, :].rearrange("(c p) r -> p c r", p=128),
              reads=("d_wvb",), writes=("WS1",), tag="wv")
        p.dma("sync", S.WS2[:, :].rearrange("p (c r) -> p c r", r=DV),
              S.d_wvb[lane, 512:1024, :].rearrange("(c p) r -> p c r", p=128),
              reads=("d_wvb",), writes=("WS2",), tag="wv")

    def lane_proj(self, lane, sub, hb):
        S = self
        base = sub * 4 + hb * 2
        ncol = sub * SUB + hb * 256
        KPc = self.RT_KP
        for h in range(H):
            for dc in range(DCH):
                S.mm(S.PA[:, 0:256],
                     S.WS0[:, dc * 512 + h * 128: dc * 512 + (h + 1) * 128],
                     S.nT[:, dc * T + ncol: dc * T + ncol + 256],
                     dc == 0, dc == DCH - 1, ("WS0", "nT"), ("PA",))
            S.cp(S.kfm[:, h * 256:(h + 1) * 256], S.PA[:, 0:256], ("PA",), S.K_kfm)
        for vh, (wvb, wvk) in enumerate(((S.WS1, "WS1"), (S.WS2, "WS2"))):
            for vt in range(2):
                W = S.W0 if vt == 0 else S.W1
                wkey = "W0" if vt == 0 else "W1"
                for dci in range(4):
                    dc = vh * 4 + dci
                    lhs = S.nT[:, dc * T + ncol + vt * 128: dc * T + ncol + vt * 128 + 128]
                    for nh in range(2):
                        S.mm(W[:, nh * 512:(nh + 1) * 512], lhs,
                             wvb[:, dci * 1024 + nh * 512: dci * 1024 + (nh + 1) * 512],
                             dc == 0, dc == DCH - 1, ("nT", wvk), (wkey,))
        for vt in range(2):
            W = S.W0 if vt == 0 else S.W1
            wkey = "W0" if vt == 0 else "W1"
            if lane < NM:
                kcol = S.rt[:, KPc + lane * 16 + base + vt: KPc + lane * 16 + base + vt + 1]
                S.ts(S.vtok[:, vt * 1024:(vt + 1) * 1024], W[:, :], kcol, None,
                     ALU.mult, reads=(wkey, "rt"), writes=S.K_vtok)
            else:
                S.cp(S.vtok[:, vt * 1024:(vt + 1) * 1024], W[:, :], (wkey,), S.K_vtok)
        for dc in range(DCH):
            S.mm(S.PD[0:16, 0:256],
                 S.wgk1b[:, lane * 128 + dc * 16: lane * 128 + (dc + 1) * 16],
                 S.nT[:, dc * T + ncol: dc * T + ncol + 256],
                 dc == 0, dc == DCH - 1, ("wgk1b", "nT"), ("PD",))
        S.cp(S.t1a[0:16, 0:256], S.PD[0:16, 0:256], ("PD",), ("t1a",))
        for vt in range(2):
            S.mm(S.PC[:, :], S.t1a[0:17, vt * 128:(vt + 1) * 128],
                 S.wgk2a[0:17, lane * 512:(lane + 1) * 512], True, True,
                 ("t1a", "t1a_one", "wgk2a"), ("PC",))
            gslice = S.gktok[:, vt * 512:(vt + 1) * 512]
            S.act(gslice, S.PC[:, :], AF.Exp, ("PC",), S.K_gktok, scale=-1.0)
            S.ts(gslice, gslice, 1.0, None, ALU.add, reads=S.K_gktok, writes=S.K_gktok)
            S.act(gslice, gslice, AF.Ln, S.K_gktok, S.K_gktok)
            if lane < NM:
                kcol = S.rt[:, KPc + lane * 16 + base + vt: KPc + lane * 16 + base + vt + 1]
                S.ts(gslice, gslice, kcol, None, ALU.mult,
                     reads=S.K_gktok + ("rt",), writes=S.K_gktok)

    def lane_chunk(self, lane, sub, hb, c, mainpass):
        S = self
        base = sub * 4 + hb * 2
        g = base + c
        st = S.sts[lane]
        stk = f"st{lane}"
        for h in range(H):
            S.mm(S.PA[:, h * 128:(h + 1) * 128],
                 S.gktok[:, c * 512 + h * 128: c * 512 + (h + 1) * 128],
                 S.cst_ug[:, :], True, True, S.K_gktok + ("cst_ug",), ("PA",))
        aC = S.PA[:, :].rearrange("p (h k) -> p h k", k=128)[:, :, 127:128]
        S.act(S.sm[:, 52:56].rearrange("p (h o) -> p h o", o=1), aC, AF.Identity,
              ("PA",), ("sm",))
        S.act(S.sm[:, 48:52], S.sm[:, 52:56], AF.Exp, ("sm",), ("sm",))
        if mainpass:
            S.act(S.eab, S.PA[:, :], AF.Exp, ("PA",), S.K_eab)
            S.act(S.eib, S.PA[:, :], AF.Exp, ("PA",), S.K_eib, scale=-1.0)
            qv = S.qfm.rearrange("p (h k) -> p h k", k=512)[:, :, hb * 256 + c * 128: hb * 256 + c * 128 + 128]
            S.tt(S.qcb.rearrange("p (h k) -> p h k", k=128), qv,
                 S.eab.rearrange("p (h k) -> p h k", k=128), ALU.mult,
                 S.K_qfm + S.K_eab, S.K_qcb)
            kv_ = S.kfm.rearrange("p (h k) -> p h k", k=256)[:, :, c * 128:(c + 1) * 128]
            S.tt(S.kpb.rearrange("p (h k) -> p h k", k=128), kv_,
                 S.eib.rearrange("p (h k) -> p h k", k=128), ALU.mult,
                 S.K_kfm + S.K_eib, S.K_kpb)
            for h in range(H):
                S.mm(S.PB[:, h * 128:(h + 1) * 128], S.kpb[:, h * 128:(h + 1) * 128],
                     S.qcb[:, h * 128:(h + 1) * 128], True, True,
                     S.K_kpb + S.K_qcb, ("PB",))
            ptm = S.cst_pt[:, :].rearrange("p (o k) -> p o k", o=1).broadcast_to([128, 4, 128])
            S.tt(S.ptb.rearrange("p (h k) -> p h k", k=128),
                 S.PB[:, :].rearrange("p (h k) -> p h k", k=128), ptm, ALU.mult,
                 ("PB", "cst_pt"), S.K_ptb)
            for h in range(H):
                S.mm(S.W0[:, h * 256:(h + 1) * 256], S.ptb[:, h * 128:(h + 1) * 128],
                     S.vtok[:, c * 1024 + h * 256: c * 1024 + (h + 1) * 256],
                     True, False, S.K_ptb + S.K_vtok, ("W0",))
                S.mm(S.W0[:, h * 256:(h + 1) * 256], S.qcb[:, h * 128:(h + 1) * 128],
                     st[:, h * 256:(h + 1) * 256], False, True,
                     S.K_qcb + (stk,), ("W0",))
            for h in range(H):
                S.ts(S.kpb[:, h * 128:(h + 1) * 128], S.kpb[:, h * 128:(h + 1) * 128],
                     S.sm[:, 48 + h:49 + h], None, ALU.mult,
                     reads=S.K_kpb + ("sm",), writes=S.K_kpb)
        else:
            for h in range(H):
                S.act(S.eab[:, h * 128:(h + 1) * 128], S.PA[:, h * 128:(h + 1) * 128],
                      AF.Exp, ("PA", "sm"), S.K_eab, scale=-1.0,
                      bias=S.sm[:, 52 + h:53 + h])
            kv_ = S.kfm.rearrange("p (h k) -> p h k", k=256)[:, :, c * 128:(c + 1) * 128]
            S.tt(S.kpb.rearrange("p (h k) -> p h k", k=128), kv_,
                 S.eab.rearrange("p (h k) -> p h k", k=128), ALU.mult,
                 S.K_kfm + S.K_eab, S.K_kpb)
            S.tt(S.sm[:, 128 + lane * 4:132 + lane * 4].rearrange("p (h o) -> p h o", o=1),
                 S.sm[:, 128 + lane * 4:132 + lane * 4].rearrange("p (h o) -> p h o", o=1),
                 S.sm[:, 52:56].rearrange("p (h o) -> p h o", o=1), ALU.add,
                 ("sm", "sm"), ("sm",))
        for h in range(H):
            S.tr(S.PCb[:, h * 128:(h + 1) * 128], S.kpb[:, h * 128:(h + 1) * 128],
                 S.K_kpb, ("PC",))
        S.cp(S.kptb, S.PCb[:, 0:512], ("PC",), S.K_kptb)
        for h in range(H):
            S.mm(S.W1[:, h * 256:(h + 1) * 256], S.kptb[:, h * 128:(h + 1) * 128],
                 S.vtok[:, c * 1024 + h * 256: c * 1024 + (h + 1) * 256],
                 True, True, S.K_kptb + S.K_vtok, ("W1",))
        for h in range(H):
            S.stt(st[:, h * 256:(h + 1) * 256], st[:, h * 256:(h + 1) * 256],
                  S.sm[:, 48 + h:49 + h], S.W1[:, h * 256:(h + 1) * 256],
                  ALU.mult, ALU.add, (stk, "sm", "W1"), (stk,))
        if mainpass:
            osl = S.oacc[:, (hb * 2 + c) * 1024:(hb * 2 + c + 1) * 1024]
            if lane == 0:
                wcol = S.rt[:, self.RT_WW + lane * 16 + g: self.RT_WW + lane * 16 + g + 1]
                S.ts(osl, S.W0[:, :], wcol, None, ALU.mult,
                     reads=("W0", "rt"), writes=S.K_oacc)
            elif lane < NM:
                wcol = S.rt[:, self.RT_WW + lane * 16 + g: self.RT_WW + lane * 16 + g + 1]
                S.stt(osl, S.W0[:, :], wcol, osl, ALU.mult, ALU.add,
                      ("W0", "rt") + S.K_oacc, S.K_oacc)
            else:
                S.tt(osl, osl, S.W0[:, :], ALU.add, S.K_oacc + ("W0",), S.K_oacc)

    def ph_prepass(self):
        S, p = self, self.p
        for L in range(5):
            S.mset(S.sts[L][:, :], 0.0, (f"st{L}",))
        S.mset(S.sm[:, 128:148], 0.0, ("sm",))
        for lane in range(5):
            S.lane_load_weights(lane)
            for sub in range(NSUB):
                for hb in range(2):
                    S.lane_proj(lane, sub, hb)
                    for c in range(2):
                        S.lane_chunk(lane, sub, hb, c, mainpass=False)
        for L in range(5):
            p.dma("gpsimd", S.d_stg_in[:, L * 1024:(L + 1) * 1024], S.sts[L][:, :],
                  reads=(f"st{L}",), writes=("d_stg_in",))
        p.dma("gpsimd", S.d_stgA_in[:, :], S.sm[:, 128:148],
              reads=("sm",), writes=("d_stgA_in",))
        p.cc("AllGather", ALU.bypass, GRP4, S.d_stg_in, S.d_stg_out,
             reads=("d_stg_in",), writes=("d_stg_out",), tag="stAG")
        p.cc("AllGather", ALU.bypass, GRP4, S.d_stgA_in, S.d_stgA_out,
             reads=("d_stgA_in",), writes=("d_stgA_out",), tag="sAAG")

    def ph_combine(self):
        S, p, sm = self, self.p, self.sm
        for j in range(4):
            p.dma("sync", sm[:, 152 + j * 20:172 + j * 20], S.d_stgA_out[j],
                  reads=("d_stgA_out",), writes=("sm",))
        for L in range(5):
            stg = [S.tmpC, S.tmpD]
            stgk = [S.K_tmpC, S.K_tmpD]
            for j in range(4):
                p.dma("sync", stg[j % 2], S.d_stg_out[j, :, L * 1024:(L + 1) * 1024],
                      reads=("d_stg_out",), writes=stgk[j % 2])
                for li in range(4):
                    scol = S.msk_s[:, j * 4 + li: j * 4 + li + 1]
                    acol = sm[:, 152 + li * 20 + L * 4: 152 + li * 20 + L * 4 + 4]
                    if li == 0:
                        S.ts(sm[:, 232:236], acol, scol, None, ALU.mult,
                             reads=("sm", "msk_s"), writes=("sm",))
                    else:
                        S.stt(sm[:, 232:236], acol, scol, sm[:, 232:236],
                              ALU.mult, ALU.add, ("sm", "msk_s", "sm"),
                              ("sm",))
                S.act(sm[:, 232:236], sm[:, 232:236], AF.Exp, ("sm",), ("sm",))
                S.ts(sm[:, 232:236], sm[:, 232:236], S.msk_sel[:, j:j + 1], None,
                     ALU.mult, reads=("sm", "msk_sel"), writes=("sm",))
                for h in range(H):
                    dsl = stg[j % 2][:, h * 256:(h + 1) * 256]
                    ssl = S.sts[L][:, h * 256:(h + 1) * 256]
                    if j == 0:
                        S.ts(ssl, dsl, sm[:, 232 + h:233 + h], None, ALU.mult,
                             reads=stgk[j % 2] + ("sm",), writes=(f"st{L}",))
                    else:
                        S.stt(ssl, dsl, sm[:, 232 + h:233 + h], ssl,
                              ALU.mult, ALU.add,
                              stgk[j % 2] + ("sm", f"st{L}"), (f"st{L}",))

    def ph_attn_main(self):
        S = self
        for sub in range(NSUB):
            S.p.dma("sync", S.WS0[:, :].rearrange("p (c r) -> p c r", r=DK),
                    S.d_wqb.rearrange("(c p) r -> p c r", p=128),
                    reads=("d_wqb",), writes=("WS0",), tag="wq")
            for h in range(H):
                for dc in range(DCH):
                    S.mm(S.PA[:, :],
                         S.WS0[:, dc * 512 + h * 128: dc * 512 + (h + 1) * 128],
                         S.nT[:, dc * T + sub * SUB: dc * T + (sub + 1) * SUB],
                         dc == 0, dc == DCH - 1, ("WS0", "nT"), ("PA",))
                S.ts(S.qfm[:, h * 512:(h + 1) * 512], S.PA[:, :], SCALE_Q, None,
                     ALU.mult, reads=("PA",), writes=S.K_qfm)
            for lane in range(5):
                S.lane_load_weights(lane)
                for hb in range(2):
                    S.lane_proj(lane, sub, hb)
                    for c in range(2):
                        S.lane_chunk(lane, sub, hb, c, mainpass=True)
            if sub == NSUB - 1:
                S.tap("oacc", S.oacc[:, :], S.K_oacc)
            self.ph_epilogue(sub)

    def ph_epilogue(self, sub):
        S = self
        for half, (dst, dkey) in enumerate(((S.WS0, "WS0"), (S.WS2, "WS2"))):
            S.p.dma("sync", dst[:, :].rearrange("p (c r) -> p c r", r=DV),
                    S.d_wgb[half * 512:(half + 1) * 512, :]
                    .rearrange("(c p) r -> p c r", p=128),
                    reads=("d_wgb",), writes=(dkey,), tag="wg")
        for r in range(4):
            g = sub * 4 + r
            osl = S.oacc[:, r * 1024:(r + 1) * 1024]
            for h in range(H):
                S.act(S.tmpB[:, h * 256:(h + 1) * 256], osl[:, h * 256:(h + 1) * 256],
                      AF.Square, S.K_oacc, S.K_tmpB + ("sm",),
                      accum=S.sm[:, 56 + h:57 + h])
            S.act(S.sm[:, 60:64], S.sm[:, 56:60], AF.Ln, ("sm",), ("sm",),
                  scale=1.0 / HDV, bias=RMS_EPS)
            S.act(S.sm[:, 56:60], S.sm[:, 60:64], AF.Exp, ("sm",), ("sm",),
                  scale=-0.5)
            for h in range(H):
                S.stt(S.tmpA[:, h * 256:(h + 1) * 256], osl[:, h * 256:(h + 1) * 256],
                      S.sm[:, 56 + h:57 + h], S.nwb[:, :], ALU.mult, ALU.mult,
                      S.K_oacc + ("sm", "nwb"), S.K_tmpA)
            for dc in range(DCH):
                src = S.WS0 if dc < 4 else S.WS2
                skey = "WS0" if dc < 4 else "WS2"
                lhs = S.nT[:, dc * T + g * 128: dc * T + (g + 1) * 128]
                for nh in range(2):
                    S.mm(S.W1[:, nh * 512:(nh + 1) * 512], lhs,
                         src[:, (dc % 4) * 1024 + nh * 512:(dc % 4) * 1024 + (nh + 1) * 512],
                         dc == 0, dc == DCH - 1, ("nT", skey), ("W1",))
            S.act(S.tmpB, S.W1[:, :], AF.Silu, ("W1",), S.K_tmpB)
            S.tt(S.tmpC, S.tmpA, S.tmpB, ALU.mult, S.K_tmpA + S.K_tmpB, S.K_tmpC)
            for cdc in range(DCH):
                S.tr(S.PCb[:, cdc * 128:(cdc + 1) * 128],
                     S.tmpC[:, cdc * 128:(cdc + 1) * 128], S.K_tmpC, ("PC",))
            S.cp(S.tmpD, S.PCb[:, 0:1024], ("PC",), S.K_tmpD)
            for oh in range(2):
                S.p.dma("sync", S.WS1[:, :].rearrange("p (c r) -> p c r", r=D),
                        S.d_wob[oh * 512:(oh + 1) * 512, :]
                        .rearrange("(c p) r -> p c r", p=128),
                        reads=("d_wob",), writes=("WS1",), tag="wo")
                for dci in range(4):
                    dc = oh * 4 + dci
                    for nh in range(2):
                        S.mm(S.W0[:, nh * 512:(nh + 1) * 512],
                             S.tmpD[:, dc * 128:(dc + 1) * 128],
                             S.WS1[:, dci * 1024 + nh * 512: dci * 1024 + (nh + 1) * 512],
                             dc == 0, dc == DCH - 1, (S.K_tmpD[0], "WS1"), ("W0",))
            ssl = S.stream[:, g * D:(g + 1) * D]
            S.tt(ssl, ssl, S.W0[:, :], ALU.add, ("stream", "W0"), ("stream",))

    def quant_2048(self, tag=""):
        S = self
        ks = tuple(set(S.K_fhh) | set(S.K_fam) | set(S.K_frc) | set(S.K_fw1)
                   | set(S.K_fw2) | set(S.K_fbs1) | set(S.K_fph))
        S.fp4_quant(S.f_hh, 2048, self._q_gs6, self._q_gs, S.f_am, S.f_rc,
                    S.f_w1, S.f_w2, S.f_ph, S.f_bs1, ks)
        for c in range(8):
            S.tr(S.PCb[:, c * 128:(c + 1) * 128], S.f_ph[:, c * 128:(c + 1) * 128],
                 S.K_fph, ("PC",), tag=tag)
            S.tr(S.PDb[:, c * 128:(c + 1) * 128],
                 S.f_ph[:, 1024 + c * 128:1024 + (c + 1) * 128], S.K_fph,
                 ("PD",), tag=tag)
        S.cp(S.hqt_s[:, 0:1024], S.PCb[:, 0:1024], ("PC",), S.K_hqt)
        S.cp(S.hqt_s[:, 1024:2048], S.PDb[:, 0:1024], ("PD",), S.K_hqt)

    def ph_wquant(self):
        S, p, sm = self, self.p, self.sm

        def qpair(srcs, dsts, gcol, wcol, dkey, tag):
            for half, sap in enumerate(srcs):
                p.dma("sync", S.f_hh[:, half * 1024:(half + 1) * 1024], sap,
                      writes=S.K_fhh, tag=tag)
            self._q_gs6 = sm[:, gcol:gcol + 1]
            self._q_gs = sm[:, wcol:wcol + 1]
            self.quant_2048(tag=tag)
            for half, dap in enumerate(dsts):
                p.dma("gpsimd", dap,
                      S.hqt_s[:, half * 1024:(half + 1) * 1024]
                      .rearrange("p (c k) -> p c k", k=128),
                      reads=S.K_hqt, writes=(dkey,), tag=tag)

        for name, shard, dout, gcol, wcol in (
                ("g", S.fw_gshard, S.d_gwq_in, 40, 32),
                ("u", S.fw_ushard, S.d_uwq_in, 41, 33)):
            for rt2 in range(2):
                srcs = [shard[(rt2 * 2 + h) * 128:(rt2 * 2 + h + 1) * 128, :]
                        for h in range(2)]
                dv = dout.rearrange("(c p) r -> p c r", p=128)
                dsts = [dv[:, :, (rt2 * 2 + h) * 128:(rt2 * 2 + h + 1) * 128]
                        for h in range(2)]
                qpair(srcs, dsts, gcol, wcol, "d_" + name + "wq_in", "wq" + name)
        for ic2 in range(2):
            srcs = [S.fw_dshard[:, (ic2 * 2 + h) * 1024:(ic2 * 2 + h + 1) * 1024]
                    for h in range(2)]
            dsts = [S.d_dwq_in[(ic2 * 2 + h) * 1024:(ic2 * 2 + h + 1) * 1024, :]
                    .rearrange("(c p) r -> p c r", p=128) for h in range(2)]
            qpair(srcs, dsts, 42, 34, "d_dwq_in", "wqd")
        p.cc("AllGather", ALU.bypass, GRP8, S.d_gwq_in, S.d_gwq,
             reads=("d_gwq_in",), writes=("d_gwq",), tag="gwAG")
        p.cc("AllGather", ALU.bypass, GRP8, S.d_uwq_in, S.d_uwq,
             reads=("d_uwq_in",), writes=("d_uwq",), tag="uwAG")
        p.cc("AllGather", ALU.bypass, GRP8, S.d_dwq_in, S.d_dwq,
             reads=("d_dwq_in",), writes=("d_dwq",), tag="dwAG")

    def ph_ffn0(self):
        S, p, sm = self, self.p, self.sm
        for t2 in range(8):
            for h in range(2):
                t = t2 * 2 + h
                xs = S.stream[:, t * D:(t + 1) * D]
                S.ts(S.f_hh[:, h * 1024:(h + 1) * 1024], xs, S.mrs[:, t:t + 1],
                     S.mrs[:, 16 + t:17 + t], ALU.subtract, ALU.mult,
                     reads=("stream", "mrs"), writes=S.K_fhh)
            self._q_gs6 = sm[:, 38:39]
            self._q_gs = sm[:, 24:25]
            self.quant_2048(tag="xq")
            for h in range(2):
                p.dma("gpsimd", S.d_xqT[t2 * 2 + h],
                      S.hqt_s[:, h * 1024:(h + 1) * 1024], reads=S.K_hqt,
                      writes=("d_xqT",), tag="xqW")
            if t2 == 0:
                S.tap("xq0", S.f_ph[:, :], S.K_fph)
                S.tap("smsc", S.sm[:, :], ("sm",))
        for sb in range(4):
            for ic in range(8):
                p.dma("sync", S.WS0[:, :].rearrange("p (c r) -> p c r", r=512),
                      S.d_gwq[ic].rearrange("(c p) r -> p c r", p=128),
                      reads=("d_gwq",), writes=("WS0",), tag="ldgw")
                p.dma("sync", S.WS1[:, :].rearrange("p (c r) -> p c r", r=512),
                      S.d_uwq[ic].rearrange("(c p) r -> p c r", p=128),
                      reads=("d_uwq",), writes=("WS1",), tag="lduw")
                for r8 in range(8):
                    for ib in range(4):
                        p.dma("sync", S.WS2[:, ib * 1024 + r8 * 128: ib * 1024 + (r8 + 1) * 128],
                              S.d_dwq[r8, ic * 512 + ib * 128: ic * 512 + (ib + 1) * 128, :],
                              reads=("d_dwq",), writes=("WS2",), tag="lddw")
                for tl in range(4):
                    t = sb * 4 + tl
                    p.dma("sync", S.xqt_s, S.d_xqT[t], reads=("d_xqT",),
                          writes=S.K_xqt, tag="ldxq")
                    for dc in range(DCH):
                        S.mm(S.PA[:, :], S.xqt_s[:, dc * 128:(dc + 1) * 128],
                             S.WS0[:, dc * 512:(dc + 1) * 512], dc == 0,
                             dc == DCH - 1, S.K_xqt + ("WS0",), ("PA",))
                    for dc in range(DCH):
                        S.mm(S.PB[:, :], S.xqt_s[:, dc * 128:(dc + 1) * 128],
                             S.WS1[:, dc * 512:(dc + 1) * 512], dc == 0,
                             dc == DCH - 1, S.K_xqt + ("WS1",), ("PB",))
                    S.act(S.f_sil, S.PA[:, :], AF.Silu, ("PA", "sm"), S.K_fsil,
                          scale=sm[:, 35:36])
                    S.stt(S.f_hh[:, tl * 512:(tl + 1) * 512], S.PB[:, :],
                          sm[:, 36:37], S.f_sil, ALU.mult, ALU.mult,
                          ("PB", "sm") + S.K_fsil, S.K_fhh)
                if sb == 0 and ic == 0:
                    S.tap("hh0", S.f_hh[:, :], S.K_fhh)
                self._q_gs6 = sm[:, 39:40]
                self._q_gs = sm[:, 25:26]
                self.quant_2048(tag="hq")
                if sb == 0 and ic == 0:
                    S.tap("ph0", S.f_ph[:, :], S.K_fph)
                for tl in range(4):
                    for ib in range(4):
                        for nh in range(2):
                            S.mm(S.W0[:, nh * 512:(nh + 1) * 512],
                                 S.hqt_s[:, tl * 512 + ib * 128: tl * 512 + (ib + 1) * 128],
                                 S.WS2[:, ib * 1024 + nh * 512: ib * 1024 + (nh + 1) * 512],
                                 ib == 0, ib == 3, S.K_hqt + ("WS2",), ("W0",))
                    ysl = S.yacc[:, tl * 1024:(tl + 1) * 1024]
                    if ic == 0:
                        S.cp(ysl, S.W0[:, :], ("W0",), S.K_yacc)
                    else:
                        S.tt(ysl, ysl, S.W0[:, :], ALU.add, S.K_yacc + ("W0",),
                             S.K_yacc)
            for tl in range(4):
                t = sb * 4 + tl
                ssl = S.stream[:, t * D:(t + 1) * D]
                S.stt(ssl, S.yacc[:, tl * 1024:(tl + 1) * 1024], sm[:, 37:38],
                      ssl, ALU.mult, ALU.add, S.K_yacc + ("sm", "stream"),
                      ("stream",))

    def ph_l1_prep(self):
        S, p = self, self.p
        for n in "qkvgo":
            p.dma("sync", S.rowc[:, :], S.a_b[n], writes=S.K_rowc, tag="l1b")
            S.cp(S.brow[n][:, :], S.rowc[:, :], S.K_rowc, ("brow",))
        p.dma("sync", S.bcol[:, 0:8],
              S.a_b["q"].rearrange("o (c p) -> (o p) c", p=128), writes=("bcol",),
              allow_slow_non_contiguous=True)
        S.ts(S.bcol[:, 8:16], S.bcol[:, 0:8], -1.0, None, ALU.mult,
             reads=("bcol",), writes=("bcol",))

    def ph_l1_attn(self, limit=99):
        S, p = self, self.p
        self._l1lim = limit
        self.ph_l1_prep()
        if limit < 1:
            return
        for t in range(NT):
            for W, ps, wname in ((S.W0, "W0", "k"), (S.W1, "W1", "v")):
                for nh in range(2):
                    S.mm(W[:, nh * 512:(nh + 1) * 512], S.cst_o1b[:, :],
                         S.brow[wname][:, nh * 512:(nh + 1) * 512], True, False,
                         ("cst_o1b", "brow"), (ps,))
                for vh, (wb, wk) in enumerate(((S.WS1, "WS1"), (S.WS2, "WS2"))):
                    S.p.dma("sync", wb[:, :].rearrange("p (c r) -> p c r", r=D),
                            S.d_aWb[wname][vh * 512:(vh + 1) * 512, :]
                            .rearrange("(c p) r -> p c r", p=128),
                            reads=("d_aWb" + wname,), writes=(wk,), tag="l1w" + wname)
                    for dci in range(4):
                        dc = vh * 4 + dci
                        lhs = S.nT[:, dc * T + t * 128: dc * T + (t + 1) * 128]
                        for nh in range(2):
                            S.mm(W[:, nh * 512:(nh + 1) * 512], lhs,
                                 wb[:, dci * 1024 + nh * 512: dci * 1024 + (nh + 1) * 512],
                                 False, dc == DCH - 1, ("nT", wk), (ps,))
            S.act(S.l1e1, S.W0[:, :], AF.Relu, ("W0",), S.K_l1e1)
            S.act(S.l1e2, S.W0[:, :], AF.Relu, ("W0",), S.K_l1e2, scale=-1.0)
            S.act(S.l1e2, S.l1e2, AF.Exp, S.K_l1e2, S.K_l1e2, scale=-1.0)
            S.tt(S.l1kf, S.l1e2, S.l1e1, ALU.add, S.K_l1e1 + S.K_l1e2, S.K_l1kf)
            S.cp(S.l1vf, S.W1[:, :], ("W1",), S.K_l1vf)
            if t == 5:
                S.tap("l1kf5", S.l1kf, S.K_l1kf)
                S.tap("l1vf5", S.l1vf, S.K_l1vf)
            for h in range(H1):
                ps = S.PC if h < 8 else S.PD
                psk = "PC" if h < 8 else "PD"
                S.mm(ps[0:64, (h % 8) * 64:(h % 8) * 64 + 64],
                     S.l1kf[:, h * 64:(h + 1) * 64], S.l1vf[:, h * 64:(h + 1) * 64],
                     True, True, S.K_l1kf + S.K_l1vf, (psk,))
            if t == 0:
                S.cp(S.kvb[0:64, 0:512], S.PC[0:64, :], ("PC",), S.K_kvb)
                S.cp(S.kvb[0:64, 512:1024], S.PD[0:64, :], ("PD",), S.K_kvb)
            else:
                S.tt(S.kvb[0:64, 0:512], S.kvb[0:64, 0:512], S.PC[0:64, :],
                     ALU.add, S.K_kvb + ("PC",), S.K_kvb)
                S.tt(S.kvb[0:64, 512:1024], S.kvb[0:64, 512:1024], S.PD[0:64, :],
                     ALU.add, S.K_kvb + ("PD",), S.K_kvb)
        p.dma("gpsimd", S.d_kv_in, S.kvb[:, :], reads=S.K_kvb, writes=("d_kv_in",))
        p.cc("AllReduce", ALU.add, GRP4, S.d_kv_in, S.d_kv_out,
             reads=("d_kv_in",), writes=("d_kv_out",), tag="kvAR")
        p.dma("sync", S.kvb[:, :], S.d_kv_out, reads=("d_kv_out",), writes=S.K_kvb)
        S.mset(S.kvb2[:, :], 0.0, S.K_kvb2)
        S.cp(S.kvb2[0:64, :].rearrange("q (h k) -> q h k", k=128)[:, :, 0:64],
             S.kvb[0:64, :].rearrange("q (h k) -> q h k", k=64)[:, 0:16:2, :],
             S.K_kvb, S.K_kvb2)
        p.dma("gpsimd",
              S.kvb2[64:128, :].rearrange("q (h k) -> q h k", k=128)[:, :, 64:128],
              S.kvb[0:64, :].rearrange("q (h k) -> q h k", k=64)[:, 1:16:2, :],
              reads=S.K_kvb, writes=S.K_kvb2)
        S.tap("kvb2", S.kvb2[:, :], S.K_kvb2)
        if limit < 3:
            return
        p.dma("gpsimd", S.d_xnt_in.rearrange("(c p) t -> p c t", p=128),
              S.nT[:, :].rearrange("p (c t) -> p c t", t=T),
              reads=("nT",), writes=("d_xnt_in",), tag="xntW")
        p.cc("AllGather", ALU.bypass, GRP8, S.d_xnt_in, S.d_xnt,
             reads=("d_xnt_in",), writes=("d_xnt",), tag="xntAG")
        for sub in range(NSUB):
            for ec in range(DCH):
                S.p.dma("sync",
                        S.WS0[:, 0:1024].rearrange("p (c r) -> p c r", r=128),
                        S.d_aWb["q"].rearrange("(c p) r -> p c r", p=128)
                        [:, :, ec * 128:(ec + 1) * 128],
                        reads=("d_aWbq",), writes=("WS0",), tag="l1wq")
                for dc in range(DCH):
                    S.mm(S.PA[:, :], S.WS0[:, dc * 128:(dc + 1) * 128],
                         S.nT[:, dc * T + sub * SUB: dc * T + (sub + 1) * SUB],
                         dc == 0, dc == DCH - 1, ("WS0", "nT"), ("PA",))
                S.act(S.l1e1[:, 0:512], S.PA[:, :], AF.Relu, ("PA", "bcol"),
                      S.K_l1e1, bias=S.bcol[:, ec:ec + 1])
                S.act(S.l1e2[:, 0:512], S.PA[:, :], AF.Relu, ("PA", "bcol"),
                      S.K_l1e2, scale=-1.0, bias=S.bcol[:, 8 + ec:9 + ec])
                S.act(S.l1e2[:, 0:512], S.l1e2[:, 0:512], AF.Exp, S.K_l1e2,
                      S.K_l1e2, scale=-1.0)
                S.tt(S.l1q[:, ec * 512:(ec + 1) * 512], S.l1e2[:, 0:512],
                     S.l1e1[:, 0:512], ALU.add, S.K_l1e1 + S.K_l1e2, S.K_l1q)
            if self._l1lim < 5:
                continue
            for half, (dst, dkey) in enumerate(((S.WS1, "WS1"), (S.WS2, "WS2"))):
                S.p.dma("sync", dst[:, :].rearrange("p (c r) -> p c r", r=D),
                        S.d_aWb["g"][half * 512:(half + 1) * 512, :]
                        .rearrange("(c p) r -> p c r", p=128),
                        reads=("d_aWbg",), writes=(dkey,), tag="l1wg")
            for tl in range(4):
                t = sub * 4 + tl
                for nh in range(2):
                    S.mm(S.W0[:, nh * 512:(nh + 1) * 512], S.cst_o1b[:, :],
                         S.brow["g"][:, nh * 512:(nh + 1) * 512], True, False,
                         ("cst_o1b", "brow"), ("W0",))
                for dc in range(DCH):
                    src = S.WS1 if dc < 4 else S.WS2
                    skey = "WS1" if dc < 4 else "WS2"
                    lhs = S.nT[:, dc * T + t * 128: dc * T + (t + 1) * 128]
                    for nh in range(2):
                        S.mm(S.W0[:, nh * 512:(nh + 1) * 512], lhs,
                             src[:, (dc % 4) * 1024 + nh * 512:(dc % 4) * 1024 + (nh + 1) * 512],
                             False, dc == DCH - 1, ("nT", skey), ("W0",))
                S.act(S.l1sg, S.W0[:, :], AF.Sigmoid, ("W0",), S.K_l1sg)
                if self._l1lim < 6:
                    continue
                for ec in range(DCH):
                    S.mm(S.W1[:, ec * 128:(ec + 1) * 128],
                         S.l1q[:, ec * 512 + tl * 128: ec * 512 + (tl + 1) * 128],
                         S.kvb2[:, ec * 128:(ec + 1) * 128],
                         True, True, S.K_l1q + S.K_kvb2, ("W1",))
                S.tt(S.l1ga, S.l1sg, S.W1[:, :], ALU.mult, S.K_l1sg + ("W1",),
                     S.K_l1ga)
                if sub == 0 and tl == 0:
                    S.tap("l1q0", S.l1q[:, :], S.K_l1q)
                    S.tap("l1sg0", S.l1sg, S.K_l1sg)
                    S.tap("l1ga0", S.l1ga, S.K_l1ga)
                for cdc in range(DCH):
                    S.tr(S.PCb[:, cdc * 128:(cdc + 1) * 128],
                         S.l1ga[:, cdc * 128:(cdc + 1) * 128], S.K_l1ga, ("PC",))
                S.cp(S.l1gt, S.PCb[:, 0:1024], ("PC",), S.K_l1gt)
                if self._l1lim < 7:
                    continue
                for nh in range(2):
                    S.mm(S.W0[:, nh * 512:(nh + 1) * 512], S.cst_o1b[:, :],
                         S.brow["o"][:, nh * 512:(nh + 1) * 512], True, False,
                         ("cst_o1b", "brow"), ("W0",))
                for oh in range(2):
                    S.p.dma("sync", S.WS0[:, :].rearrange("p (c r) -> p c r", r=D),
                            S.d_aWb["o"][oh * 512:(oh + 1) * 512, :]
                            .rearrange("(c p) r -> p c r", p=128),
                            reads=("d_aWbo",), writes=("WS0",), tag="l1wo")
                    for dci in range(4):
                        dc = oh * 4 + dci
                        for nh in range(2):
                            S.mm(S.W0[:, nh * 512:(nh + 1) * 512],
                                 S.l1gt[:, dc * 128:(dc + 1) * 128],
                                 S.WS0[:, dci * 1024 + nh * 512: dci * 1024 + (nh + 1) * 512],
                                 False, dc == DCH - 1,
                                 (S.K_l1gt[0], "WS0"), ("W0",))
                ssl = S.stream[:, t * D:(t + 1) * D]
                S.tt(ssl, ssl, S.W0[:, :], ALU.add, ("stream", "W0"), ("stream",))

    def transpose_weight_to_dram(self, src_dram, dst_dram, skey, dkey):
        S, p = self, self.p
        for rtile in range(8):
            stage = S.wstg[:, 0:1024]
            p.dma("sync", stage, src_dram[rtile * 128:(rtile + 1) * 128, :],
                  reads=(skey,), writes=S.K_wstg, tag="twl")
            S.cp(S.tmpC, stage, S.K_wstg, S.K_tmpC)
            for c in range(DCH):
                S.tr(S.PCb[:, c * 128:(c + 1) * 128], S.tmpC[:, c * 128:(c + 1) * 128],
                     S.K_tmpC, ("PC",))
            S.cp(S.tmpD, S.PCb[:, 0:1024], ("PC",), S.K_tmpD)
            dst = dst_dram.rearrange("(c p) r -> p c r", p=128)[:, :, rtile * 128:(rtile + 1) * 128]
            p.dma("gpsimd", dst, S.tmpD[:, :].rearrange("p (c k) -> p c k", k=128),
                  reads=S.K_tmpD, writes=(dkey,), tag="twS")

    def ph_l1_route(self):
        S, p, sm, rt = self, self.p, self.sm, self.rt
        stage = S.wstg[0:16, 0:1024]
        p.dma("sync", stage, S.m_gate_w, writes=S.K_wstg, tag="l1gw")
        S.cp(S.tmpC[0:16, :], stage, S.K_wstg, S.K_tmpC)
        for c in range(DCH):
            S.tr(S.PCb[:, c * 16:(c + 1) * 16], S.tmpC[0:16, c * 128:(c + 1) * 128],
                 S.K_tmpC, ("PC",))
        S.cp(S.gwt[:, :], S.PCb[:, 0:128], ("PC",), ("gwt",))
        for t in range(NT):
            for dc in range(DCH):
                S.mm(S.PD[:, 0:16], S.nT[:, dc * T + t * 128: dc * T + (t + 1) * 128],
                     S.gwt[:, dc * 16:(dc + 1) * 16], dc == 0, dc == DCH - 1,
                     ("nT", "gwt"), ("PD",))
            dst = rt[:, 0:256].rearrange("p (m k) -> p m k", k=16)[:, :, t:t + 1]
            S.act(dst, S.PD[:, 0:16].rearrange("p (m o) -> p m o", o=1),
                  AF.Sigmoid, ("PD",), ("rt",))
        p.dma("sync", S.rowb[0:1, 0:16], S.m_ebias, writes=("rowb",))
        S.mm(S.PD[:, 0:16], S.cst_o1f[:, :], S.rowb[0:1, 0:16], True, True,
             ("cst_o1f", "rowb"), ("PD",))
        S.cp(sm[:, 144:160], S.PD[:, 0:16], ("PD",), ("sm",))
        for e in range(E16):
            S.ts(rt[:, 256 + e * 16:256 + e * 16 + 16], rt[:, e * 16:e * 16 + 16],
                 sm[:, 144 + e:145 + e], None, ALU.add,
                 reads=("rt", "sm"), writes=("rt",))
        for e in range(E16):
            first = True
            for j in range(E16):
                if j == e:
                    continue
                S.tt(sm[:, 160:176], rt[:, 256 + j * 16:256 + j * 16 + 16],
                     rt[:, 256 + e * 16:256 + e * 16 + 16], ALU.is_gt,
                     ("rt",), ("sm",))
                if first:
                    S.cp(rt[:, 512 + e * 16:512 + e * 16 + 16], sm[:, 160:176],
                         ("sm",), ("rt",))
                    first = False
                else:
                    S.tt(rt[:, 512 + e * 16:512 + e * 16 + 16],
                         rt[:, 512 + e * 16:512 + e * 16 + 16], sm[:, 160:176],
                         ALU.add, ("rt", "sm"), ("rt",))
                if j < e:
                    S.tt(sm[:, 160:176], rt[:, 256 + j * 16:256 + j * 16 + 16],
                         rt[:, 256 + e * 16:256 + e * 16 + 16], ALU.is_equal,
                         ("rt",), ("sm",))
                    S.tt(rt[:, 512 + e * 16:512 + e * 16 + 16],
                         rt[:, 512 + e * 16:512 + e * 16 + 16], sm[:, 160:176],
                         ALU.add, ("rt", "sm"), ("rt",))
        S.ts(rt[:, 512:768], rt[:, 512:768], 7.5, None, ALU.is_lt,
             reads=("rt",), writes=("rt",))
        S.tt(rt[:, 256:512], rt[:, 512:768], rt[:, 0:256], ALU.mult,
             ("rt", "rt"), ("rt",))
        S.cp(sm[:, 176:192], rt[:, 256:272], ("rt",), ("sm",))
        for e in range(1, E16):
            S.tt(sm[:, 176:192], sm[:, 176:192], rt[:, 256 + e * 16:256 + e * 16 + 16],
                 ALU.add, ("sm", "rt"), ("sm",))
        S.recip(sm[:, 192:208], sm[:, 176:192], ("sm",), ("sm",))
        for e in range(E16):
            S.stt(rt[:, 256 + e * 16:256 + e * 16 + 16],
                  rt[:, 256 + e * 16:256 + e * 16 + 16], ROUTE_SCALE,
                  sm[:, 192:208], ALU.mult, ALU.mult, ("rt", "sm"), ("rt",))
        for t in range(NT):
            csrc = rt[:, 256:512].rearrange("p (m k) -> p m k", k=16)[:, :, t:t + 1]
            S.cp(sm[:, 224:240].rearrange("p (m o) -> p m o", o=1), csrc,
                 ("rt",), ("sm",))
            p.dma("gpsimd", S.d_c_in[t * 128:(t + 1) * 128, :], sm[:, 224:240],
                  reads=("sm",), writes=("d_c_in",), tag="cW")
        p.cc("AllGather", ALU.bypass, GRP8, S.d_c_in, S.d_c_out,
             reads=("d_c_in",), writes=("d_c_out",), tag="cAG")
        for j in range(8):
            stage = S.l1e1[:, 0:256]
            p.dma("sync", stage.rearrange("p (t e) -> p t e", e=16),
                  S.d_c_out[j].rearrange("(t p) e -> p t e", p=128),
                  reads=("d_c_out",), writes=S.K_l1e1, tag="cR")
            for half in range(2):
                S.tr(S.PD[:, 0:128], S.l1e1[:, half * 128:(half + 1) * 128],
                     S.K_l1e1, ("PD",), f32=True)
                S.cp(S.l1e2[:, 0:128], S.PD[:, 0:128], ("PD",), S.K_l1e2)
                for q4 in range(2):
                    S.mm(S.PA[:, q4 * 8:(q4 + 1) * 8],
                         S.l1e2[q4 * 64:(q4 + 1) * 64, 0:128],
                         S.esel_sb[q4 * 64:(q4 + 1) * 64, :],
                         True, True, S.K_l1e2 + ("esel_sb",), ("PA",))
                g0 = j * 16 + half * 8
                dst = S.cmy[:, :].rearrange("p (e k) -> p e k", k=128)[:, :, g0:g0 + 8]
                S.cp(dst, S.PA[:, 0:16].rearrange("p (g e) -> p e g", e=2),
                     ("PA",), ("cmy",))

    def ph_moe_wprep(self):
        S = self
        for e in range(EPC):
            self.transpose_weight_to_dram(S.m_wg[e], S.d_ewT[e, 0], "m_wg", "d_ewT")
            self.transpose_weight_to_dram(S.m_wu[e], S.d_ewT[e, 1], "m_wu", "d_ewT")
            self.transpose_weight_to_dram(S.m_wd[e], S.d_ewT[e, 2], "m_wd", "d_ewT")
        self.transpose_weight_to_dram(S.m_sg, S.d_swT[0], "m_sg", "d_swT")
        self.transpose_weight_to_dram(S.m_su, S.d_swT[1], "m_su", "d_swT")
        self.transpose_weight_to_dram(S.m_sd, S.d_swT[2], "m_sd", "d_swT")

    def moe_stageA(self, pr, gate_bufs, xsrc, ccol):
        S, p = self, self.p
        ps = S.moe_ps[pr]
        xb, xk = S.e_xnt2[pr]
        if xsrc is not None:
            p.dma("sync", xb.rearrange("p (c t) -> p c t", t=128), xsrc,
                  reads=("d_xnt",), writes=xk, tag="exnt")
            lhs = lambda dc: xb[:, dc * 128:(dc + 1) * 128]
            lhk = xk
        else:
            lhs, lhk = gate_bufs.pop("lhs"), gate_bufs.pop("lhk")
        for dc in range(DCH):
            wgb, wgk = gate_bufs["g"][0 if dc < 4 else 1]
            for nh in range(2):
                S.mm(ps["G"][nh],
                     lhs(dc),
                     wgb[:, (dc % 4) * 1024 + nh * 512:(dc % 4) * 1024 + (nh + 1) * 512],
                     dc == 0, dc == DCH - 1, lhk + (wgk,), ps["GK"])
        wub, wuk = gate_bufs["u"]
        for dc in range(DCH):
            for nh in range(2):
                S.mm(ps["U"][nh], lhs(dc),
                     wub[:, dc * 1024 + nh * 512: dc * 1024 + (nh + 1) * 512],
                     dc == 0, dc == DCH - 1, lhk + (wuk,), ps["UKn"][nh])
        sil, slk = S.e_sil2[pr]
        hb, hk = S.e_h2[pr]
        for nh in range(2):
            S.act(sil[:, nh * 512:(nh + 1) * 512], ps["G"][nh], AF.Silu,
                  ps["GK"], slk)
            if ccol is not None:
                S.stt(hb[:, nh * 512:(nh + 1) * 512], ps["U"][nh], ccol,
                      sil[:, nh * 512:(nh + 1) * 512], ALU.mult, ALU.mult,
                      ps["UK"] + ("cmy",) + slk, hk)
            else:
                S.tt(hb[:, nh * 512:(nh + 1) * 512], ps["U"][nh],
                     sil[:, nh * 512:(nh + 1) * 512], ALU.mult,
                     ps["UK"] + slk, hk)

    def moe_stageB(self, pr, down_bufs):
        S = self
        ps = S.moe_ps[pr]
        hb, hk = S.e_h2[pr]
        hT, hTk = S.e_hT2[pr]
        for c in range(DCH):
            S.tr(ps["Gb"][:, c * 128:(c + 1) * 128], hb[:, c * 128:(c + 1) * 128],
                 hk, ps["GK"])
        S.cp(hT, ps["Gb"], ps["GK"], hTk)
        for ib in range(DCH):
            wdb, wdk = down_bufs[0 if ib < 4 else 1]
            for nh in range(2):
                S.mm(ps["U"][nh], hT[:, ib * 128:(ib + 1) * 128],
                     wdb[:, (ib % 4) * 1024 + nh * 512:(ib % 4) * 1024 + (nh + 1) * 512],
                     ib == 0, ib == DCH - 1, hTk + (wdk,), ps["UKn"][nh])
        return ps

    def ph_l1_experts(self):
        S, p = self, self.p
        NG = 8 * NT
        for e in range(EPC):
            p.dma("sync", S.WS0[:, :].rearrange("p (c r) -> p c r", r=1024),
                  S.d_ewT[e, 2, 0:512, :].rearrange("(c p) r -> p c r", p=128),
                  reads=("d_ewT",), writes=("WS0",), tag="ewd0")
            p.dma("sync", S.WS1[:, :].rearrange("p (c r) -> p c r", r=1024),
                  S.d_ewT[e, 2, 512:1024, :].rearrange("(c p) r -> p c r", p=128),
                  reads=("d_ewT",), writes=("WS1",), tag="ewd1")
            p.dma("sync", S.WS2[:, :].rearrange("p (c r) -> p c r", r=1024),
                  S.d_ewT[e, 0, 0:512, :].rearrange("(c p) r -> p c r", p=128),
                  reads=("d_ewT",), writes=("WS2",), tag="ewg0")
            p.dma("sync", S.e_wg2.rearrange("p (c r) -> p c r", r=1024),
                  S.d_ewT[e, 0, 512:1024, :].rearrange("(c p) r -> p c r", p=128),
                  reads=("d_ewT",), writes=S.K_ewg2, tag="ewg1")
            p.dma("sync", S.e_wu.rearrange("p (c r) -> p c r", r=1024),
                  S.d_ewT[e, 1].rearrange("(c p) r -> p c r", p=128),
                  reads=("d_ewT",), writes=S.K_ewu, tag="ewu")
            gb = {"g": [(S.WS2, "WS2"), (S.e_wg2, S.K_ewg2[0])],
                  "u": (S.e_wu, S.K_ewu[0])}
            for g in range(NG + 1):
                if g < NG:
                    xsrc = (S.d_xnt[g // 16][:, (g % 16) * 128:(g % 16) * 128 + 128]
                            .rearrange("(c p) t -> p c t", p=128))
                    ccol = S.cmy[:, e * 128 + g: e * 128 + g + 1]
                    S.moe_stageA(g % 2, dict(gb), xsrc, ccol)
                if g >= 1:
                    gp = g - 1
                    ps = S.moe_stageB(gp % 2, [(S.WS0, "WS0"), (S.WS1, "WS1")])
                    yb, yk = S.e_y2[gp % 2]
                    for nh in range(2):
                        S.cp(yb[:, nh * 512:(nh + 1) * 512], ps["U"][nh],
                             ps["UKn"][nh], yk)
                    if e == 0:
                        p.dma("gpsimd", S.d_y_in[gp * 128:(gp + 1) * 128, :], yb,
                              reads=yk, writes=("d_y_in",), tag="yW")
                    else:
                        p.dma("gpsimd", S.d_y_in[gp * 128:(gp + 1) * 128, :], yb,
                              reads=yk + ("d_y_in",), writes=("d_y_in",), tag="yWa",
                              accum_op=ALU.add)
        p.cc("ReduceScatter", ALU.add, GRP8, S.d_y_in, S.d_y_out,
             reads=("d_y_in",), writes=("d_y_out",), tag="yRS")

    def ph_l1_shared(self):
        S, p = self, self.p
        p.dma("sync", S.WS0[:, :].rearrange("p (c r) -> p c r", r=1024),
              S.d_swT[2, 0:512, :].rearrange("(c p) r -> p c r", p=128),
              reads=("d_swT",), writes=("WS0",), tag="swd0")
        p.dma("sync", S.WS1[:, :].rearrange("p (c r) -> p c r", r=1024),
              S.d_swT[2, 512:1024, :].rearrange("(c p) r -> p c r", p=128),
              reads=("d_swT",), writes=("WS1",), tag="swd1")
        p.dma("sync", S.WS2[:, :].rearrange("p (c r) -> p c r", r=1024),
              S.d_swT[0, 0:512, :].rearrange("(c p) r -> p c r", p=128),
              reads=("d_swT",), writes=("WS2",), tag="swg0")
        p.dma("sync", S.e_wg2.rearrange("p (c r) -> p c r", r=1024),
              S.d_swT[0, 512:1024, :].rearrange("(c p) r -> p c r", p=128),
              reads=("d_swT",), writes=S.K_ewg2, tag="swg1")
        p.dma("sync", S.e_wu.rearrange("p (c r) -> p c r", r=1024),
              S.d_swT[1].rearrange("(c p) r -> p c r", p=128),
              reads=("d_swT",), writes=S.K_ewu, tag="swu")
        gb = {"g": [(S.WS2, "WS2"), (S.e_wg2, S.K_ewg2[0])],
              "u": (S.e_wu, S.K_ewu[0])}
        for t in range(NT + 1):
            if t < NT:
                d = dict(gb)
                d["lhs"] = (lambda dc, t=t:
                            S.nT[:, dc * T + t * 128: dc * T + (t + 1) * 128])
                d["lhk"] = ("nT",)
                S.moe_stageA(t % 2, d, None, None)
            if t >= 1:
                tp_ = t - 1
                ps = S.moe_stageB(tp_ % 2, [(S.WS0, "WS0"), (S.WS1, "WS1")])
                ssl = S.stream[:, tp_ * D:(tp_ + 1) * D]
                for nh in range(2):
                    S.tt(ssl[:, nh * 512:(nh + 1) * 512], ssl[:, nh * 512:(nh + 1) * 512],
                         ps["U"][nh], ALU.add, ("stream",) + ps["UKn"][nh], ("stream",))
        for t in range(NT):
            p.dma("sync", S.e_st, S.d_y_out[t * 128:(t + 1) * 128, :],
                  reads=("d_y_out",), writes=S.K_est, tag="yR")
            if t == NT - 1:
                S.tap("yrs15", S.e_st, S.K_est)
            ssl = S.stream[:, t * D:(t + 1) * D]
            S.tt(ssl, ssl, S.e_st, ALU.add, ("stream",) + S.K_est, ("stream",))

    def build(self, upto=None, skip=()):
        order = ["consts", "wprep", "gkload", "wamax", "ln", "routing", "wquant", "prepass",
                 "ffn0", "combine", "attn", "ln1", "moeprep", "l1a0", "l1a2", "l1a4", "l1a5", "l1a6", "l1attn",
                 "l1route", "l1experts", "l1shared"]
        todo = order[:order.index(upto) + 1] if upto else order
        todo = [t for t in todo if t not in skip]
        for ph in todo:
            if ph == "consts":
                self.ph_consts()
            elif ph == "wprep":
                self.ph_wprep()
            elif ph == "gkload":
                self.ph_gkload()
            elif ph == "wamax":
                self.ph_wamax()
            elif ph == "ln":
                self.ph_ln()
            elif ph == "routing":
                self.ph_routing()
            elif ph == "wquant":
                self.ph_wquant()
            elif ph == "prepass":
                self.ph_prepass()
            elif ph == "ffn0":
                self.ph_ffn0()
            elif ph == "combine":
                self.ph_combine()
            elif ph == "attn":
                self.ph_attn_main()
            elif ph == "ln1":
                self.ph_ln()
            elif ph == "moeprep":
                self.ph_moe_wprep()
            elif ph == "l1attn":
                self.ph_l1_attn()
            elif ph == "l1a0":
                if upto == "l1a0":
                    self.ph_l1_attn(limit=0)
            elif ph == "l1a2":
                if upto == "l1a2":
                    self.ph_l1_attn(limit=2)
            elif ph in ("l1a4", "l1a5", "l1a6"):
                if upto == ph:
                    self.ph_l1_attn(limit=int(ph[-1]))
            elif ph == "l1route":
                self.ph_l1_route()
            elif ph == "l1experts":
                self.ph_l1_experts()
            elif ph == "l1shared":
                self.ph_l1_shared()
        p = self.p
        p.dma("gpsimd", self.out_d.rearrange("(t p) d -> p t d", p=128),
              self.stream[:, :].rearrange("p (t d) -> p t d", d=D),
              reads=("stream",), writes=("out_d",), tag="stout")
        p.finalize()
        p.build_block()
        return self


def host_inputs(inputs, core):
    b, qt = core // 4, core % 4
    x = np.asarray(inputs["L0_attn_x"])
    d = {}
    d["x"] = np.ascontiguousarray(x[b, qt * T:(qt + 1) * T, :], dtype=np.float32)
    ren = {
        "w_gate": "L0_attn_w_gate", "w_q": "L0_attn_w_q", "w_k": "L0_attn_w_k",
        "w_v": "L0_attn_w_v", "w_gk1": "L0_attn_w_gk1", "w_gk2": "L0_attn_w_gk2",
        "b_gk": "L0_attn_b_gk", "w_ks": "L0_attn_w_ks", "w_vs": "L0_attn_w_vs",
        "w_gk1s": "L0_attn_w_gk1s", "w_gk2s": "L0_attn_w_gk2s",
        "w_g": "L0_attn_w_g", "w_o": "L0_attn_w_o",
        "a_Wq": "L1_attn_Wq", "a_Wk": "L1_attn_Wk", "a_Wv": "L1_attn_Wv",
        "a_Wg": "L1_attn_Wg", "a_Wo": "L1_attn_Wo",
        "m_gate_w": "L1_ffn_gate_w",
        "m_sg": "L1_ffn_shared_gate_w", "m_su": "L1_ffn_shared_up_w",
        "m_sd": "L1_ffn_shared_down_w",
    }
    for k, v in ren.items():
        d[k] = np.ascontiguousarray(np.asarray(inputs[v], dtype=np.float32))
    d["b_gks"] = np.asarray(inputs["L0_attn_b_gks"], np.float32).reshape(1, DK)
    d["norm_w"] = np.asarray(inputs["L0_attn_norm_w"], np.float32).reshape(1, HDV)
    d["f_s_in"] = np.asarray(inputs["L0_ffn_s_in"], np.float32).reshape(1, 1)
    d["f_s_in_down"] = np.asarray(inputs["L0_ffn_s_in_down"], np.float32).reshape(1, 1)
    gw = np.asarray(inputs["L0_ffn_gate_w"], np.float32)
    uw = np.asarray(inputs["L0_ffn_up_w"], np.float32)
    dw = np.asarray(inputs["L0_ffn_down_w"], np.float32)
    d["fw_gshard"] = np.ascontiguousarray(gw[core * 512:(core + 1) * 512, :])
    d["fw_ushard"] = np.ascontiguousarray(uw[core * 512:(core + 1) * 512, :])
    d["fw_dshard"] = np.ascontiguousarray(dw[core * 128:(core + 1) * 128, :])
    for n in "qkvgo":
        d["a_b" + n] = np.asarray(inputs["L1_attn_b" + n], np.float32).reshape(1, D)
    d["m_ebias"] = np.asarray(inputs["L1_ffn_expert_bias"], np.float32).reshape(1, E16)
    d["m_wg"] = np.ascontiguousarray(
        np.asarray(inputs["L1_ffn_w_gate"], np.float32)[core * EPC:(core + 1) * EPC])
    d["m_wu"] = np.ascontiguousarray(
        np.asarray(inputs["L1_ffn_w_up"], np.float32)[core * EPC:(core + 1) * EPC])
    d["m_wd"] = np.ascontiguousarray(
        np.asarray(inputs["L1_ffn_w_down"], np.float32)[core * EPC:(core + 1) * EPC])
    sm = np.zeros((128, 16), np.float32)
    sl = np.zeros((128, 4), np.float32)
    for j in range(4):
        if j < qt:
            sl[:, j] = 1.0
        for l in range(4):
            if j < l < qt:
                sm[:, j * 4 + l] = 1.0
    d["smask"], d["ssel"] = sm, sl
    cm = np.zeros((4, 2), np.float32)
    cm[:, 1] = 1.0
    for j in range(4):
        if j < qt:
            cm[j, 0] = 1.0
    d["cmask"] = cm
    es = np.zeros((128, 4 * EPC), np.float32)
    for q4 in range(2):
        for blk in range(4):
            for i in range(EPC):
                es[q4 * 64 + blk * 16 + core * EPC + i, blk * EPC + i] = 1.0
    d["esel"] = es
    return d


_BUILD_CACHE = {}


def kernel(**inputs):
    key = "full"
    if key not in _BUILD_CACHE:
        _BUILD_CACHE[key] = KB().build()
    kb = _BUILD_CACHE[key]
    in_maps = [host_inputs(inputs, c) for c in range(8)]
    res = run_bass_kernel_spmd(kb.nc, in_maps, core_ids=list(range(8)))
    out = np.empty((2, 8192, 1024), np.float32)
    for c in range(8):
        b, qt = c // 4, c % 4
        out[b, qt * T:(qt + 1) * T, :] = res.results[c]["out"]
    return out


# version 17
# speedup vs baseline: 1.4414x; 1.0852x over previous
import contextlib
import numpy as np

import concourse.bass as bass
import concourse.mybir as mybir
from concourse.bass_utils import run_bass_kernel_spmd

F32 = mybir.dt.float32
BF16 = mybir.dt.bfloat16
AF = mybir.ActivationFunctionType
ALU = mybir.AluOpType
AX = mybir.AxisListType

T, NT, D, DCH = 2048, 16, 1024, 8
NM, H, HDK, HDV, DK, DV = 4, 4, 128, 256, 512, 1024
CAP = 4096.0
GATE_NORM = 16.0
SUB, NSUB, CHUNK = 512, 4, 128
SCALE_Q = float(HDK) ** -0.5
LN_EPS = 1e-5
RMS_EPS = 1e-5
FP4_MAX, FP8_MAX = 6.0, 448.0
E16, EPC, I_FFN, I_EXP = 16, 2, 4096, 1024
ROUTE_SCALE = 2.826
H1, HD1 = 16, 64
K_HALF, K_ONE, K_TWO = 96.0, 192.0, 384.0
VELT = float(2 ** 20 + 1)
SUBN = 2.0 ** -6
K_SUBN = 1.5 * (2 ** 23) * (2 ** -9)
GRP4 = [[0, 1, 2, 3], [4, 5, 6, 7]]
GRP8 = [list(range(8))]


class Prog:
    def __init__(self, nc):
        self.nc = nc
        self.entries = []
        self.last_w = {}
        self.readers = {}
        self.gap_fn = None
        self._last_on = {}

    def _chan(self, e):
        return ("cc",) if e["kind"] == "cc" else (e["kind"], e["eng"])

    def emit(self, eng, fn, reads=(), writes=(), kind="compute", tag="", outw=None):
        ch = ("cc",) if kind == "cc" else (kind, eng)
        hist = self._last_on.get(ch, [])
        need_gap = False
        if kind == "compute" and self.gap_fn is not None and eng == "vector":
            rw = set(reads) | set(writes)
            if hist:
                p1 = hist[-1]
                if p1[1] is not None and p1[1] < 128 and (set(p1[0]) & rw):
                    need_gap = True
                if (len(hist) > 1 and p1[1] is not None and p1[1] < 128):
                    p2 = hist[-2]
                    if p2[1] is not None and p2[1] < 128 and (set(p2[0]) & rw):
                        need_gap = True
        if need_gap:
            ge = dict(idx=len(self.entries), eng=eng, fn=self.gap_fn(eng),
                      deps=set(), kind="compute", need_inc=False, tag="gap")
            self.entries.append(ge)
            hist = hist + [((), 1024)]
        self._last_on[ch] = (hist + [(tuple(writes), outw)])[-2:]
        idx = len(self.entries)
        deps = set()
        for k in reads:
            if k in self.last_w:
                deps.add(self.last_w[k])
        for k in writes:
            if k in self.last_w:
                deps.add(self.last_w[k])
            for r in self.readers.get(k, ()):
                deps.add(r)
        self.entries.append(dict(idx=idx, eng=eng, fn=fn, deps=deps,
                                 kind=kind, need_inc=False, tag=tag))
        for k in reads:
            self.readers.setdefault(k, []).append(idx)
        for k in writes:
            self.last_w[k] = idx
            self.readers[k] = []
        return idx

    def dma(self, eng, out, in_, reads=(), writes=(), tag="", **kw):
        return self.emit(eng,
                         lambda e, o=out, i=in_, kw=kw: e.dma_start(out=o, in_=i, **kw),
                         reads=reads, writes=writes, kind="dma", tag=tag)

    def cc(self, kind, op, groups, in_ap, out_ap, reads=(), writes=(), tag=""):
        return self.emit(
            "gpsimd",
            lambda e, k=kind, o=op, g=groups, ia=in_ap, oa=out_ap:
                e.collective_compute(k, o, replica_groups=g, ins=[ia], outs=[oa]),
            reads=reads, writes=writes, kind="cc", tag=tag)

    def finalize(self):
        for i, e in enumerate(self.entries):
            e["idx"] = i
        for e in self.entries:
            ce = self._chan(e)
            for d in e["deps"]:
                de = self.entries[d]
                if self._chan(de) != ce:
                    de["need_inc"] = True
        for e in self.entries:
            if e["kind"] in ("dma", "cc"):
                e["need_inc"] = True
        cnt = {}
        for e in self.entries:
            c = self._chan(e)
            n = cnt.get(c, 0) + (1 if e["need_inc"] else 0)
            cnt[c] = n
            e["cum"] = n
        self.totals = dict(cnt)
        hw = {}
        for e in self.entries:
            waits = {}
            for d in sorted(e["deps"]):
                de = self.entries[d]
                dc = self._chan(de)
                if dc == self._chan(e):
                    continue
                waits[dc] = max(waits.get(dc, 0), de["cum"])
            wl = []
            for dc in sorted(waits, key=str):
                cum = waits[dc]
                key = (e["eng"], dc)
                if hw.get(key, 0) >= cum:
                    continue
                hw[key] = cum
                wl.append((dc, cum * (16 if dc[0] == "dma" else 1)))
            e["waits"] = wl
        self.chans = sorted({self._chan(e) for e in self.entries}, key=str)

    def build_block(self):
        nc = self.nc
        sems = {}
        stack = contextlib.ExitStack()
        for c in self.chans:
            sems[c] = stack.enter_context(nc.semaphore("s_" + "_".join(map(str, c))))

        def run(eng_obj, name):
            for e in self.entries:
                if e["eng"] != name:
                    continue
                for dc, val in e["waits"]:
                    eng_obj.wait_ge(sems[dc], val)
                inst = e["fn"](eng_obj)
                if e["need_inc"]:
                    inst.then_inc(sems[self._chan(e)], 16 if e["kind"] == "dma" else 1)
            if name == "gpsimd":
                for c in self.chans:
                    tot = self.totals.get(c, 0)
                    if tot:
                        eng_obj.wait_ge(sems[c], tot * (16 if c[0] == "dma" else 1))

        with nc.Block() as block:
            block.sync(lambda e: run(e, "sync"))
            block.tensor(lambda e: run(e, "tensor"))
            block.vector(lambda e: run(e, "vector"))
            block.scalar(lambda e: run(e, "scalar"))
            block.gpsimd(lambda e: run(e, "gpsimd"))
        stack.close()


class KB:
    def __init__(self, taps=()):
        import ml_dtypes
        ml = ml_dtypes
        nc = bass.Bass("TRN2", target_bir_lowering=False)
        self.nc = nc
        self.taps = set(taps)
        self.tap_d = {}
        self.ctx = contextlib.ExitStack()

        def din(name, shape):
            return nc.dram_tensor(name, list(shape), F32, kind="ExternalInput").ap()

        self.x_in = din("x", (T, D))
        self.w_gate = din("w_gate", (D, NM))
        self.w_q = din("w_q", (D, DK))
        self.w_k = din("w_k", (NM, D, DK))
        self.w_v = din("w_v", (NM, D, DV))
        self.w_gk1 = din("w_gk1", (NM, D, 16))
        self.w_gk2 = din("w_gk2", (NM, 16, DK))
        self.b_gk = din("b_gk", (NM, DK))
        self.w_ks = din("w_ks", (D, DK))
        self.w_vs = din("w_vs", (D, DV))
        self.w_gk1s = din("w_gk1s", (D, 16))
        self.w_gk2s = din("w_gk2s", (16, DK))
        self.b_gks = din("b_gks", (1, DK))
        self.w_g = din("w_g", (D, DV))
        self.w_o = din("w_o", (DV, D))
        self.norm_w = din("norm_w", (1, HDV))
        self.fw_gshard = din("fw_gshard", (512, D))
        self.fw_ushard = din("fw_ushard", (512, D))
        self.fw_dshard = din("fw_dshard", (128, I_FFN))
        self.f_s_in = din("f_s_in", (1, 1))
        self.f_s_in_down = din("f_s_in_down", (1, 1))
        self.a_W = {n: din("a_W" + n, (D, D)) for n in "qkvgo"}
        self.a_b = {n: din("a_b" + n, (1, D)) for n in "qkvgo"}
        self.m_gate_w = din("m_gate_w", (E16, D))
        self.m_ebias = din("m_ebias", (1, E16))
        self.m_wg = din("m_wg", (EPC, I_EXP, D))
        self.m_wu = din("m_wu", (EPC, I_EXP, D))
        self.m_wd = din("m_wd", (EPC, D, I_EXP))
        self.m_sg = din("m_sg", (I_EXP, D))
        self.m_su = din("m_su", (I_EXP, D))
        self.m_sd = din("m_sd", (D, I_EXP))
        self.smask = din("smask", (128, 16))
        self.ssel = din("ssel", (128, 4))
        self.cmask = din("cmask", (4, 2))
        self.esel = din("esel", (128, 4 * EPC))
        self.out_d = nc.dram_tensor("out", [T, D], F32, kind="ExternalOutput").ap()

        tri = np.triu(np.ones((128, 128), np.float32))
        self.c_utri = nc.inline_tensor(np.ascontiguousarray(tri), name="c_utri").ap()
        self.c_utri_gk = nc.inline_tensor(
            np.ascontiguousarray((tri * np.float32(-1.0 / GATE_NORM)).astype(np.float32)),
            name="c_utri_gk").ap()
        self.c_ident = nc.inline_tensor(np.eye(128, dtype=ml.bfloat16), name="c_ident").ap()
        self.c_identf = nc.inline_tensor(np.eye(128, dtype=np.float32), name="c_identf").ap()
        self.c_ptmask = nc.inline_tensor(np.ascontiguousarray(tri).astype(ml.bfloat16),
                                         name="c_ptmask").ap()
        self.c_ones1f = nc.inline_tensor(np.ones((1, 128), np.float32), name="c_ones1f").ap()
        self.c_ones1b = nc.inline_tensor(np.ones((1, 128), ml.bfloat16), name="c_ones1b").ap()
        self.c_ones512b = nc.inline_tensor(np.ones((1, 512), ml.bfloat16), name="c_ones512b").ap()
        self.c_onescf = nc.inline_tensor(np.ones((128, 1), np.float32), name="c_onescf").ap()
        self.CBVALS = [LN_EPS, 1.0 / D, -0.5, 1.0 / HDV]
        cb_np = np.broadcast_to(np.array(self.CBVALS, np.float32)[None, :],
                                (128, len(self.CBVALS))).copy()
        self.c_cbank = nc.inline_tensor(cb_np, name="c_cbank").ap()

        def dscr(name, shape, dt=F32):
            return nc.dram_tensor(name, list(shape), dt, kind="Internal").ap()

        self.d_wkb = dscr("d_wkb", (5, D, DK), BF16)
        self.d_wvb = dscr("d_wvb", (5, D, DV), BF16)
        self.d_wqb = dscr("d_wqb", (D, DK), BF16)
        self.d_wgb = dscr("d_wgb", (D, DV), BF16)
        self.d_wob = dscr("d_wob", (DV, D), BF16)
        self.d_aWb = {n: dscr("d_aWb" + n, (D, D), BF16) for n in "qkvgo"}
        self.d_cnt_in = dscr("d_cnt_in", (1, 8))
        self.d_cnt_out = dscr("d_cnt_out", (1, 32))
        self.d_stg_in = dscr("d_stg_in", (128, 5 * 1024), BF16)
        self.d_stg_out = dscr("d_stg_out", (4, 128, 5 * 1024), BF16)
        self.d_stgA_in = dscr("d_stgA_in", (128, 20))
        self.d_stgA_out = dscr("d_stgA_out", (4, 128, 20))
        self.d_wmax_in = dscr("d_wmax_in", (1, 8))
        self.d_wmax_out = dscr("d_wmax_out", (1, 8))
        self.d_gwq_in = dscr("d_gwq_in", (D, 512), BF16)
        self.d_gwq = dscr("d_gwq", (8, D, 512), BF16)
        self.d_uwq_in = dscr("d_uwq_in", (D, 512), BF16)
        self.d_uwq = dscr("d_uwq", (8, D, 512), BF16)
        self.d_dwq_in = dscr("d_dwq_in", (I_FFN, 128), BF16)
        self.d_dwq = dscr("d_dwq", (8, I_FFN, 128), BF16)
        self.d_xqT = dscr("d_xqT", (NT, 128, DCH * 128), BF16)
        self.d_kv_in = dscr("d_kv_in", (64, 1024))
        self.d_kv_out = dscr("d_kv_out", (64, 1024))
        self.d_xnt_in = dscr("d_xnt_in", (D, T), BF16)
        self.d_xnt = dscr("d_xnt", (8, D, T), BF16)
        self.d_c_in = dscr("d_c_in", (T, E16))
        self.d_c_out = dscr("d_c_out", (8, T, E16))
        self.d_y_in = dscr("d_y_in", (8 * T, D), BF16)
        self.d_y_out = dscr("d_y_out", (T, D), BF16)
        self.d_ewT = dscr("d_ewT", (EPC, 3, I_EXP, D), BF16)
        self.d_swT = dscr("d_swT", (3, I_EXP, D), BF16)

        self.sb = {}

        def S(name, shape, dt=F32):
            t = self.ctx.enter_context(nc.sbuf_tensor(name, list(shape), dt))
            self.sb[name] = t
            return t
        self.S = S

        self.stream = S("stream", (128, NT * D))
        self.nT = S("nT", (128, DCH * T), BF16)
        self.sts = [S(f"st{i}", (128, H * HDV), BF16) for i in range(5)]
        self.XA = S("XA", (128, 24576), BF16)
        self.WS0 = S("WS0", (128, 4096), BF16)
        self.WS1 = S("WS1", (128, 4096), BF16)
        self.WS2 = S("WS2", (128, 4096), BF16)

        self.mrs = S("mrs", (128, 32))
        self.rt = S("rt", (128, 768))
        self.sm = S("sm", (128, 256))
        self.t1a = S("t1a", (32, 512), BF16)
        self.wgk2a = S("wgk2a", (32, 5 * 512), BF16)
        self.wgk1b = S("wgk1b", (128, 5 * 128), BF16)
        self.brow = {n: S("brow_" + n, (1, 1024), BF16) for n in "qkvgo"}
        self.bgkrow = S("bgkrow", (1, 512), BF16)
        self.bcol = S("bcol", (128, 16))
        self.gwt = S("gwt", (128, 128), BF16)
        self.rowb = S("rowb", (1, 128))
        self.wgate_b = S("wgate_b", (128, 32), BF16)
        self.cst_u = S("cst_u", (128, 128))
        self.cst_ug = S("cst_ug", (128, 128))
        self.cst_i = S("cst_i", (128, 128), BF16)
        self.cst_if = S("cst_if", (128, 128))
        self.cst_pt = S("cst_pt", (128, 128), BF16)
        self.cst_o1f = S("cst_o1f", (1, 128))
        self.cst_o1b = S("cst_o1b", (1, 128), BF16)
        self.cst_ocf = S("cst_ocf", (128, 1))
        self.nwb = S("nwb", (128, 256))
        self.msk_s = S("msk_s", (128, 16))
        self.msk_sel = S("msk_sel", (128, 4))
        self.msk_c = S("msk_c", (4, 2))
        self.esel_sb = S("esel_sb", (128, 4 * EPC))
        self.cnt4 = S("cnt4", (4, 8))
        self.wmax_row = S("wmax_row", (1, 8))
        self.srow = S("srow", (1, 8))
        self.nw_row = S("nw_row", (1, 256))
        self.cbank = S("cbank", (128, len(self.CBVALS)))
        self.cmy = S("cmy", (128, 256))
        self.junk = S("junk", (128, 256), BF16)
        for i, v in enumerate(self.CBVALS):
            nc.const_aps.register(F32, float(v), self.cbank[:, i:i + 1])

        def xav(off_b, cols, dt):
            es = 4 if dt == F32 else 2
            ap = self.XA[:, off_b // 2: (off_b + cols * es) // 2].bitcast(dt)
            g0, g1 = off_b // 4096, (off_b + cols * es - 1) // 4096
            return ap, tuple(f"xa{g}" for g in range(g0, g1 + 1))
        self.xav = xav

        self.oacc, self.K_oacc = xav(0, 4096, F32)
        self.qfm, self.K_qfm = xav(16384, 2048, BF16)
        self.kfm, self.K_kfm = xav(20480, 1024, BF16)
        self.eab, self.K_eab = xav(22528, 512, BF16)
        self.eib, self.K_eib = xav(23552, 512, BF16)
        self.vtok, self.K_vtok = xav(24576, 2048, BF16)
        self.gktok, self.K_gktok = xav(28672, 1024, F32)
        self.qcb, self.K_qcb = xav(32768, 512, BF16)
        self.kpb, self.K_kpb = xav(33792, 512, BF16)
        self.kptb, self.K_kptb = xav(34816, 512, BF16)
        self.ptb, self.K_ptb = xav(35840, 512, BF16)
        self.tmpA, self.K_tmpA = xav(36864, 1024, F32)
        self.tmpB, self.K_tmpB = xav(40960, 1024, F32)
        self.tmpC, self.K_tmpC = xav(45056, 1024, BF16)
        self.tmpD, self.K_tmpD = xav(47104, 1024, BF16)
        self.wstg, self.K_wstg = xav(36864, 2048, F32)
        self.xtr, self.K_xtr = xav(32768, 1024, BF16)
        self.yacc, self.K_yacc = xav(0, 4096, F32)
        self.xqt_s, self.K_xqt = xav(16384, 1024, BF16)
        self.f_sil, self.K_fsil = xav(18432, 512, F32)
        self.hqt_s, self.K_hqt = xav(20480, 2048, BF16)
        self.f_hh, self.K_fhh = xav(24576, 2048, F32)
        self.f_am, self.K_fam = xav(32768, 128, F32)
        self.f_rc, self.K_frc = xav(33280, 128, F32)
        self.f_w1, self.K_fw1 = xav(33792, 128, F32)
        self.f_w2, self.K_fw2 = xav(34304, 128, F32)
        self.f_bs1, self.K_fbs1 = xav(34816, 2048, BF16)
        self.f_ph, self.K_fph = xav(38912, 2048, BF16)
        self.l1q, self.K_l1q = xav(0, 4096, BF16)
        self.l1kf, self.K_l1kf = xav(8192, 1024, BF16)
        self.l1vf, self.K_l1vf = xav(10240, 1024, BF16)
        self.l1e1, self.K_l1e1 = xav(12288, 1024, F32)
        self.l1e2, self.K_l1e2 = xav(16384, 1024, F32)
        self.l1sg, self.K_l1sg = xav(20480, 1024, F32)
        self.l1ga, self.K_l1ga = xav(24576, 1024, BF16)
        self.l1gt, self.K_l1gt = xav(26624, 1024, BF16)
        self.e_xnt2 = [xav(0, 1024, BF16), xav(4096, 1024, BF16)]
        self.e_hT2 = [xav(2048, 1024, BF16), xav(6144, 1024, BF16)]
        self.e_h2 = [xav(8192, 1024, BF16), xav(12288, 1024, BF16)]
        self.e_y2 = [xav(10240, 1024, BF16), xav(14336, 1024, BF16)]
        self.e_sil2 = [xav(16384, 1024, F32), xav(20480, 1024, F32)]
        self.e_wu, self.K_ewu = xav(24576, 8192, BF16)
        self.e_wg2, self.K_ewg2 = xav(40960, 4096, BF16)
        self.e_st, self.K_est = xav(8192, 1024, BF16)
        _kv, self.K_kvb = xav(28672, 1024, F32)
        self.kvb = _kv[0:64, :]
        self.kvb2, self.K_kvb2 = xav(32768, 1024, BF16)
        self.rowc = self.wstg[0:1, 0:1024]
        self.K_rowc = self.K_wstg
        self.pstg = [xav(i * 4096, 1024, F32) for i in range(3)]
        self.pout = [xav(12288 + i * 4096, 2048, BF16) for i in range(3)]

        self.W0 = self.ctx.enter_context(nc.psum_tensor("W0", [128, 1024], F32))
        self.W1 = self.ctx.enter_context(nc.psum_tensor("W1", [128, 1024], F32))
        self.PA = self.ctx.enter_context(nc.psum_tensor("PA", [128, 512], F32))
        self.PB = self.ctx.enter_context(nc.psum_tensor("PB", [128, 512], F32))
        self.PC = self.ctx.enter_context(nc.psum_tensor("PC", [128, 512], F32))
        self.PD = self.ctx.enter_context(nc.psum_tensor("PD", [128, 512], F32))
        self.PCb = self.PC[:, :].bitcast(BF16)
        self.PDb = self.PD[:, :].bitcast(BF16)
        self.PAb = self.PA[:, :].bitcast(BF16)
        self.W0b = self.W0[:, :].bitcast(BF16)
        self.moe_ps = [
            dict(G=[self.W0[:, 0:512], self.W0[:, 512:1024]], GK=("W0",),
                 Gb=self.W0b[:, 0:1024], U=[self.W1[:, 0:512], self.W1[:, 512:1024]],
                 UK=("W1",), UKn=[("W1",), ("W1",)]),
            dict(G=[self.PA[:, :], self.PB[:, :]], GK=("PA", "PB"),
                 Gb=self.PAb[:, 0:1024], U=[self.PC[:, :], self.PD[:, :]],
                 UK=("PC", "PD"), UKn=[("PC",), ("PD",)]),
        ]

        self.p = Prog(nc)

        def _gap(eng):
            j = self.junk
            if eng == "scalar":
                return lambda e: e.copy(j[:, 128:256], j[:, 0:128])
            return lambda e: e.tensor_copy(j[:, 128:256], j[:, 0:128])
        self.p.gap_fn = _gap

    def mm(self, out, lhsT, rhs, start, stop, reads, writes, tag=""):
        self.p.emit("tensor",
                    lambda e, o=out, l=lhsT, r=rhs, s=start, t=stop:
                        e.matmul(o, l, r, start=s, stop=t),
                    reads=tuple(reads), writes=tuple(writes), tag=tag)

    def tr(self, out, in_, reads, writes, tag="", f32=False):
        idn = self.cst_if if f32 else self.cst_i
        ikey = "cst_if" if f32 else "cst_i"
        kp = in_.shape[0]
        self.p.emit("tensor",
                    lambda e, o=out, i=in_, n=idn, k=kp: e.transpose(o, i, n[0:k, 0:k]),
                    reads=tuple(reads) + (ikey,), writes=tuple(writes), tag=tag)

    def act(self, out, in_, func, reads, writes, bias=0.0, scale=1.0, accum=None, tag=""):
        w = 1 if accum is not None else out.free_size()
        self.p.emit("scalar",
                    lambda e, o=out, i=in_, f=func, b=bias, s=scale, a=accum:
                        e.activation(o, i, f, bias=b, scale=s, accum_out=a),
                    reads=tuple(reads), writes=tuple(writes), tag=tag, outw=w)

    def ts(self, out, in0, s1, s2, op0, op1=None, reads=(), writes=(), eng="vector", tag=""):
        w = out.free_size()
        if s2 is None:
            self.p.emit(eng, lambda e, o=out, i=in0, a=s1, p0=op0:
                        e.tensor_scalar(o, i, a, None, p0),
                        reads=tuple(reads), writes=tuple(writes), tag=tag, outw=w)
        else:
            self.p.emit(eng, lambda e, o=out, i=in0, a=s1, b=s2, p0=op0, p1=op1:
                        e.tensor_scalar(o, i, a, b, p0, p1),
                        reads=tuple(reads), writes=tuple(writes), tag=tag, outw=w)

    def tt(self, out, in0, in1, op, reads, writes, eng="vector", tag=""):
        self.p.emit(eng, lambda e, o=out, a=in0, b=in1, q=op: e.tensor_tensor(o, a, b, q),
                    reads=tuple(reads), writes=tuple(writes), tag=tag,
                    outw=out.free_size())

    def stt(self, out, in0, s, in1, op0, op1, reads, writes, eng="vector", tag=""):
        self.p.emit(eng, lambda e, o=out, a=in0, c=s, b=in1, p0=op0, p1=op1:
                    e.scalar_tensor_tensor(o, a, c, b, p0, p1),
                    reads=tuple(reads), writes=tuple(writes), tag=tag,
                    outw=out.free_size())

    def cp(self, out, in_, reads, writes, eng="vector", tag=""):
        w = out.free_size()
        if eng == "scalar":
            self.p.emit(eng, lambda e, o=out, i=in_: e.copy(o, i),
                        reads=tuple(reads), writes=tuple(writes), tag=tag, outw=w)
        else:
            self.p.emit(eng, lambda e, o=out, i=in_: e.tensor_copy(o, i),
                        reads=tuple(reads), writes=tuple(writes), tag=tag, outw=w)

    def red(self, out, in_, op, reads, writes, absval=False, tag=""):
        self.p.emit("vector", lambda e, o=out, i=in_, q=op, a=absval:
                    e.tensor_reduce(o, i, AX.X, q, apply_absolute_value=a),
                    reads=tuple(reads), writes=tuple(writes), tag=tag,
                    outw=0)

    def recip(self, out, in_, reads, writes, tag=""):
        self.p.emit("vector", lambda e, o=out, i=in_: e.reciprocal(o, i),
                    reads=tuple(reads), writes=tuple(writes), tag=tag, outw=0)

    def mset(self, ap, val, writes, eng="vector", tag=""):
        self.p.emit(eng, lambda e, a=ap, v=val: e.memset(a, v),
                    reads=(), writes=tuple(writes), tag=tag, outw=ap.free_size())

    def scan(self, out, d0, d1, init, op0, op1, reads, writes, tag=""):
        self.p.emit("vector", lambda e, o=out, a=d0, b=d1, i=init, p0=op0, p1=op1:
                    e.tensor_tensor_scan(o, a, b, i, p0, p1),
                    reads=tuple(reads), writes=tuple(writes), tag=tag, outw=0)

    def load_cast(self, dram_ap, dst, dkeys, tag="lc"):
        rows = dram_ap.shape[-2] if dram_ap.ndim >= 2 else 1
        cols = dram_ap.shape[-1]
        stage = self.wstg[0:rows, 0:cols]
        self.p.dma("sync", stage, dram_ap, reads=(), writes=self.K_wstg, tag=tag)
        self.cp(dst, stage, self.K_wstg, tuple(dkeys), tag=tag)

    def tap(self, name, ap, reads):
        if name in self.taps:
            d = self.nc.dram_tensor("tap_" + name, list(ap.shape), ap.dtype,
                                    kind="ExternalOutput").ap()
            self.tap_d[name] = d
            self.p.dma("gpsimd", d, ap, reads=tuple(reads), writes=(), tag="tap")

    def _wprep_mat(self, jobs, src, dst, dkey):
        rows, cols = src.shape[-2], src.shape[-1]
        for c in range(rows // 128):
            jobs.append((src[c * 128:(c + 1) * 128, :],
                         dst[c * 128:(c + 1) * 128, :], cols, dkey))

    def _emit_wprep(self, jobs, tag="wp"):
        S, p = self, self.p
        n = len(jobs)
        for i in range(n + 1):
            if i < n:
                src, dst, cols, dkey = jobs[i]
                b = i % 3
                p.dma("gpsimd", S.pstg[b][0][:, 0:cols], src,
                      writes=S.pstg[b][1], tag=tag)
            if i >= 1:
                src, dst, cols, dkey = jobs[i - 1]
                b = (i - 1) % 3
                stg, sk = S.pstg[b][0][:, 0:cols], S.pstg[b][1]
                out, ok = S.pout[b][0][:, 0:cols], S.pout[b][1]
                S.cp(out, stg, sk, ok, eng="gpsimd", tag=tag)
                p.dma("gpsimd", dst, out, reads=ok, writes=(dkey,), tag=tag)

    def ph_wprep1(self):
        S = self
        jobs = []
        for lane in range(NM):
            self._wprep_mat(jobs, S.w_k[lane], S.d_wkb[lane], "d_wkb")
        self._wprep_mat(jobs, S.w_ks, S.d_wkb[4], "d_wkb")
        for lane in range(NM):
            self._wprep_mat(jobs, S.w_v[lane], S.d_wvb[lane], "d_wvb")
        self._wprep_mat(jobs, S.w_vs, S.d_wvb[4], "d_wvb")
        self._emit_wprep(jobs, tag="wp1")

    def ph_wprep2(self):
        S = self
        jobs = []
        self._wprep_mat(jobs, S.w_q, S.d_wqb, "d_wqb")
        self._wprep_mat(jobs, S.w_g, S.d_wgb, "d_wgb")
        self._wprep_mat(jobs, S.w_o, S.d_wob, "d_wob")
        for n in "qkvgo":
            self._wprep_mat(jobs, S.a_W[n], S.d_aWb[n], "d_aWb" + n)
        self._emit_wprep(jobs, tag="wp2")

    def ph_gkload(self):
        S, p = self, self.p
        for lane in range(5):
            if lane < NM:
                wgk1, wgk2 = S.w_gk1[lane], S.w_gk2[lane]
                bgk = S.b_gk[lane:lane + 1, :]
            else:
                wgk1, wgk2, bgk = S.w_gk1s, S.w_gk2s, S.b_gks
            stage1 = S.wstg[:, 0:128]
            p.dma("sync", stage1.rearrange("p (c r) -> p c r", r=16),
                  wgk1.rearrange("(c p) r -> p c r", p=128), writes=S.K_wstg,
                  tag="wgk1")
            S.cp(S.wgk1b[:, lane * 128:(lane + 1) * 128], stage1, S.K_wstg,
                 ("wgk1b",), tag="wgk1")
            S.load_cast(wgk2[:, :], S.wgk2a[0:16, lane * 512:(lane + 1) * 512],
                        ("wgk2a",), tag="wgk2")
            S.load_cast(bgk, S.bgkrow[:, :], ("bgkrow",), tag="bgk")
            p.dma("sync", S.wgk2a[16:17, lane * 512:(lane + 1) * 512],
                  S.bgkrow[:, :], reads=("bgkrow",), writes=("wgk2a",), tag="bgk2")

    def ph_consts(self):
        p, S = self.p, self
        p.dma("sync", S.stream[:, :].rearrange("p (t d) -> p t d", d=D),
              S.x_in.rearrange("(t p) d -> p t d", p=128), writes=("stream",), tag="ldx")
        for dst, src, k in ((S.cst_u, S.c_utri, "cst_u"), (S.cst_ug, S.c_utri_gk, "cst_ug"),
                            (S.cst_i, S.c_ident, "cst_i"), (S.cst_if, S.c_identf, "cst_if"),
                            (S.cst_pt, S.c_ptmask, "cst_pt"), (S.cst_o1f, S.c_ones1f, "cst_o1f"),
                            (S.cst_o1b, S.c_ones1b, "cst_o1b"), (S.cst_ocf, S.c_onescf, "cst_ocf"),
                            (S.msk_s, S.smask, "msk_s"), (S.msk_sel, S.ssel, "msk_sel"),
                            (S.msk_c, S.cmask, "msk_c"), (S.esel_sb, S.esel, "esel_sb"),
                            (S.nw_row, S.norm_w, "nw_row")):
            p.dma("sync", dst[:, :], src, writes=(k,))
        p.dma("sync", S.cbank[:, :], S.c_cbank, writes=("cbank",))
        S.cp(S.sm[:, 255:256], S.cbank[:, 0:1], ("cbank",), ("sm",), eng="scalar")
        S.cp(S.sm[:, 254:255], S.cbank[:, 0:1], ("cbank",), ("sm",), eng="vector")
        p.dma("sync", S.srow[:, 0:1], S.f_s_in, writes=("srow",))
        p.dma("sync", S.srow[:, 1:2], S.f_s_in_down, writes=("srow",))
        S.mm(S.W0[:, 0:256], S.cst_o1f[:, :], S.nw_row[:, :], True, True,
             ("cst_o1f", "nw_row"), ("W0",))
        S.cp(S.nwb[:, :], S.W0[:, 0:256], ("W0",), ("nwb",))
        p.dma("sync", S.t1a[16:17, :], S.c_ones512b, writes=("t1a_one",))

    def ph_wamax(self):
        p, S, sm = self.p, self, self.sm

        def shard_amax(shard_ap, nrow, ncol, col):
            for i, r0 in enumerate(range(0, nrow, 128)):
                npiece = min(ncol, 2048)
                for j, c0 in enumerate(range(0, ncol, npiece)):
                    stage = S.wstg[:, 0:npiece]
                    p.dma("sync", stage, shard_ap[r0:r0 + 128, c0:c0 + npiece],
                          writes=S.K_wstg, tag="wamax")
                    dst = sm[:, col:col + 1] if (i == 0 and j == 0) else sm[:, col + 4:col + 5]
                    S.red(dst, stage, ALU.max, S.K_wstg, ("sm",), absval=True)
                    if not (i == 0 and j == 0):
                        S.tt(sm[:, col:col + 1], sm[:, col:col + 1], sm[:, col + 4:col + 5],
                             ALU.max, ("sm",), ("sm",))

        shard_amax(S.fw_gshard, 512, D, 0)
        shard_amax(S.fw_ushard, 512, D, 1)
        shard_amax(S.fw_dshard, 128, I_FFN, 2)
        S.tr(S.PA[0:4, 0:128], sm[:, 0:4], ("sm",), ("PA",), f32=True)
        S.red(sm[0:4, 8:9], S.PA[0:4, 0:128], ALU.max, ("PA",), ("sm",))
        p.dma("gpsimd", S.d_wmax_in[:, 0:4], sm[0:4, 8:9],
              reads=("sm",), writes=("d_wmax_in",))
        p.cc("AllReduce", ALU.max, GRP8, S.d_wmax_in, S.d_wmax_out,
             reads=("d_wmax_in",), writes=("d_wmax_out",), tag="wAR")
        p.dma("sync", S.wmax_row[:, :], S.d_wmax_out, reads=("d_wmax_out",),
              writes=("wmax_row",))
        S.mm(S.PA[:, 0:8], S.cst_o1f[:, :], S.wmax_row[:, :], True, True,
             ("cst_o1f", "wmax_row"), ("PA",))
        S.cp(sm[:, 16:24], S.PA[:, 0:8], ("PA",), ("sm",))
        S.mm(S.PA[:, 8:16], S.cst_o1f[:, :], S.srow[:, :], True, True,
             ("cst_o1f", "srow"), ("PA",))
        S.cp(sm[:, 24:32], S.PA[:, 8:16], ("PA",), ("sm",))
        for j in range(3):
            S.recip(sm[:, 32 + j:33 + j], sm[:, 16 + j:17 + j], ("sm",), ("sm",))
            S.ts(sm[:, 32 + j:33 + j], sm[:, 32 + j:33 + j], FP8_MAX * FP4_MAX, None,
                 ALU.mult, reads=("sm",), writes=("sm",))
            S.ts(sm[:, 40 + j:41 + j], sm[:, 32 + j:33 + j], 1.0 / 6.0, None,
                 ALU.mult, reads=("sm",), writes=("sm",))
        for j, gsa in enumerate((24, 24, 25)):
            S.recip(sm[:, 35 + j:36 + j], sm[:, gsa:gsa + 1], ("sm",), ("sm",))
            S.tt(sm[:, 35 + j:36 + j], sm[:, 35 + j:36 + j], sm[:, 16 + j:17 + j],
                 ALU.mult, ("sm",), ("sm",))
            S.ts(sm[:, 35 + j:36 + j], sm[:, 35 + j:36 + j],
                 1.0 / (FP8_MAX * FP4_MAX), None, ALU.mult, reads=("sm",), writes=("sm",))
        S.ts(sm[:, 38:39], sm[:, 24:25], 1.0 / 6.0, None, ALU.mult,
             reads=("sm",), writes=("sm",))
        S.ts(sm[:, 39:40], sm[:, 25:26], 1.0 / 6.0, None, ALU.mult,
             reads=("sm",), writes=("sm",))

    def e4m3_round(self, io_ap, w1, w2, keys):
        S = self
        S.ts(w1, io_ap, VELT, None, ALU.mult, reads=keys, writes=keys)
        S.tt(w2, w1, io_ap, ALU.subtract, keys, keys)
        S.tt(w1, w1, w2, ALU.subtract, keys, keys)
        S.ts(w2, io_ap, K_SUBN, None, ALU.add, reads=keys, writes=keys)
        S.ts(w2, w2, K_SUBN, None, ALU.subtract, reads=keys, writes=keys)
        S.ts(io_ap, io_ap, SUBN, None, ALU.is_ge, reads=keys, writes=keys)
        S.tt(w1, w1, w2, ALU.subtract, keys, keys)
        S.tt(w1, w1, io_ap, ALU.mult, keys, keys)
        S.tt(io_ap, w1, w2, ALU.add, keys, keys)

    def fp4_quant(self, x_ap, n, gs6_col, gs_col, sc8, rcp, w1, w2,
                  out_bf, s1, keys):
        S = self
        nb = n // 16
        keys = tuple(keys) + ("sm",)
        S.red(sc8, x_ap.rearrange("p (b k) -> p b k", k=16), ALU.max, keys, keys,
              absval=True)
        S.ts(sc8, sc8, gs6_col, FP8_MAX, ALU.mult, ALU.min, reads=keys, writes=keys)
        S.e4m3_round(sc8, w1, w2, keys)
        S.ts(w1, sc8, 0.0, None, ALU.is_equal, reads=keys, writes=keys)
        S.tt(w2, sc8, w1, ALU.add, keys, keys)
        S.recip(rcp, w2, keys, keys)
        S.ts(w2, w1, -1.0, 1.0, ALU.mult, ALU.add, reads=keys, writes=keys)
        S.stt(w2, w2, gs_col, w1, ALU.mult, ALU.add, keys, keys)
        S.tt(rcp, rcp, w2, ALU.mult, keys, keys)
        rb = rcp.rearrange("p b -> p b ()").broadcast_to([128, nb, 16])
        S.tt(x_ap.rearrange("p (b k) -> p b k", k=16),
             x_ap.rearrange("p (b k) -> p b k", k=16), rb, ALU.mult, keys, keys)
        S.ts(x_ap, x_ap, FP4_MAX, -FP4_MAX, ALU.min, ALU.max, reads=keys, writes=keys)

        def rne(dst, K):
            S.ts(dst, x_ap, K, None, ALU.add, reads=keys, writes=keys)
            S.ts(dst, dst, K, None, ALU.subtract, reads=keys, writes=keys)

        def clip(dst, lim):
            S.ts(dst, dst, lim, -lim, ALU.min, ALU.max, reads=keys, writes=keys)

        rne(s1, K_HALF); clip(s1, 2.0)
        rne(out_bf, K_ONE); clip(out_bf, 4.0)
        S.tt(s1, s1, out_bf, ALU.add, keys, keys)
        clip(out_bf, 2.0)
        S.tt(s1, s1, out_bf, ALU.subtract, keys, keys)
        rne(out_bf, K_TWO); clip(out_bf, 6.0)
        S.tt(s1, s1, out_bf, ALU.add, keys, keys)
        clip(out_bf, 4.0)
        S.tt(s1, s1, out_bf, ALU.subtract, keys, keys)
        sb8 = sc8.rearrange("p b -> p b ()").broadcast_to([128, nb, 16])
        S.tt(out_bf.rearrange("p (b k) -> p b k", k=16),
             s1.rearrange("p (b k) -> p b k", k=16), sb8, ALU.mult, keys, keys)

    def ph_ln(self):
        S = self
        for t in range(NT):
            xs = S.stream[:, t * D:(t + 1) * D]
            S.red(S.sm[:, 60:61], xs, ALU.add, ("stream",), ("sm",))
            S.ts(S.mrs[:, t:t + 1], S.sm[:, 60:61], 1.0 / D, None,
                 ALU.mult, reads=("sm",), writes=("mrs",))
            S.ts(S.tmpA, xs, S.mrs[:, t:t + 1], None, ALU.subtract,
                 reads=("stream", "mrs"), writes=S.K_tmpA)
            S.act(S.tmpB, S.tmpA, AF.Square, S.K_tmpA, S.K_tmpB + ("sm",),
                  accum=S.sm[:, 61:62])
            S.act(S.sm[:, 62:63], S.sm[:, 61:62], AF.Ln, ("sm",), ("sm",),
                  scale=1.0 / D, bias=LN_EPS)
            S.act(S.mrs[:, 16 + t:17 + t], S.sm[:, 62:63], AF.Exp,
                  ("sm",), ("mrs",), scale=-0.5)
            S.ts(S.tmpC, S.tmpA, S.mrs[:, 16 + t:17 + t], None, ALU.mult,
                 reads=S.K_tmpA + ("mrs",), writes=S.K_tmpC)
            for c in range(DCH):
                S.tr(S.PCb[:, c * 128:(c + 1) * 128], S.tmpC[:, c * 128:(c + 1) * 128],
                     S.K_tmpC, ("PC",))
            dst = S.nT[:, :].rearrange("p (c k) -> p c k", k=T)[:, :, t * 128:(t + 1) * 128]
            S.cp(dst, S.PCb[:, 0:1024].rearrange("p (c k) -> p c k", k=128),
                 ("PC",), ("nT",))
        S.tap("nT", S.nT[:, :], ("nT",))
        S.tap("mrs", S.mrs[:, :], ("mrs",))

    def ph_routing(self):
        S, p, sm, rt = self, self.p, self.sm, self.rt
        LG, RK, KP, WW, RR, EE = 0, 64, 128, 192, 256, 320
        self.RT_LG, self.RT_RK, self.RT_KP, self.RT_WW = LG, RK, KP, WW
        for c in range(DCH):
            S.load_cast(S.w_gate[c * 128:(c + 1) * 128, :],
                        S.wgate_b[:, c * 4:(c + 1) * 4], ("wgate_b",), tag="wgate")
        for t in range(NT):
            for c in range(DCH):
                S.mm(S.PD[:, 0:4], S.nT[:, c * T + t * 128: c * T + (t + 1) * 128],
                     S.wgate_b[:, c * 4:(c + 1) * 4], c == 0, c == DCH - 1,
                     ("nT", "wgate_b"), ("PD",))
            dst = rt[:, LG:LG + 64].rearrange("p (m k) -> p m k", k=16)[:, :, t:t + 1]
            S.cp(dst, S.PD[:, 0:4].rearrange("p (m o) -> p m o", o=1),
                 ("PD",), ("rt",))
        for m in range(NM):
            first = True
            for j in range(NM):
                if j == m:
                    continue
                S.tt(rt[:, EE:EE + 16], rt[:, LG + j * 16:LG + j * 16 + 16],
                     rt[:, LG + m * 16:LG + m * 16 + 16], ALU.is_gt,
                     ("rt",), ("rt",))
                if first:
                    S.cp(rt[:, RK + m * 16:RK + m * 16 + 16], rt[:, EE:EE + 16],
                         ("rt",), ("rt",))
                    first = False
                else:
                    S.tt(rt[:, RK + m * 16:RK + m * 16 + 16],
                         rt[:, RK + m * 16:RK + m * 16 + 16], rt[:, EE:EE + 16],
                         ALU.add, ("rt", "rt"), ("rt",))
                if j < m:
                    S.tt(rt[:, EE:EE + 16], rt[:, LG + j * 16:LG + j * 16 + 16],
                         rt[:, LG + m * 16:LG + m * 16 + 16], ALU.is_equal,
                         ("rt",), ("rt",))
                    S.tt(rt[:, RK + m * 16:RK + m * 16 + 16],
                         rt[:, RK + m * 16:RK + m * 16 + 16], rt[:, EE:EE + 16],
                         ALU.add, ("rt", "rt"), ("rt",))
        S.ts(rt[:, RR:RR + 64], rt[:, RK:RK + 64], 1.5, None, ALU.is_lt,
             reads=("rt",), writes=("rt",))
        S.tt(rt[:, EE:EE + 16], rt[:, LG:LG + 16], rt[:, LG + 16:LG + 32], ALU.max,
             ("rt",), ("rt",))
        S.tt(sm[:, 64:80], rt[:, LG + 32:LG + 48], rt[:, LG + 48:LG + 64], ALU.max,
             ("rt",), ("sm",))
        S.tt(rt[:, EE:EE + 16], rt[:, EE:EE + 16], sm[:, 64:80], ALU.max,
             ("rt", "sm"), ("rt",))
        for m in range(NM):
            S.tt(rt[:, WW + m * 16:WW + m * 16 + 16],
                 rt[:, LG + m * 16:LG + m * 16 + 16], rt[:, EE:EE + 16],
                 ALU.subtract, ("rt", "rt"), ("rt",))
        S.act(rt[:, WW:WW + 64], rt[:, WW:WW + 64], AF.Exp, ("rt",), ("rt",))
        S.tt(rt[:, WW:WW + 64], rt[:, WW:WW + 64], rt[:, RR:RR + 64], ALU.mult,
             ("rt", "rt"), ("rt",))
        S.tt(sm[:, 64:80], rt[:, WW:WW + 16], rt[:, WW + 16:WW + 32], ALU.add,
             ("rt",), ("sm",))
        S.tt(sm[:, 80:96], rt[:, WW + 32:WW + 48], rt[:, WW + 48:WW + 64], ALU.add,
             ("rt",), ("sm",))
        S.tt(sm[:, 64:80], sm[:, 64:80], sm[:, 80:96], ALU.add, ("sm",), ("sm",))
        S.recip(sm[:, 80:96], sm[:, 64:80], ("sm",), ("sm",))
        for m in range(NM):
            S.tt(rt[:, WW + m * 16:WW + m * 16 + 16],
                 rt[:, WW + m * 16:WW + m * 16 + 16], sm[:, 80:96], ALU.mult,
                 ("rt", "sm"), ("rt",))
        S.mm(S.W0[:, 0:64], S.cst_u[:, :], rt[:, RR:RR + 64], True, False,
             ("cst_u", "rt"), ("W0",))
        S.mm(S.PD[0:1, 0:64], S.cst_ocf[:, :], rt[:, RR:RR + 64], True, True,
             ("cst_ocf", "rt"), ("PD",))
        S.cp(S.rowb[0:1, 0:64], S.PD[0:1, 0:64], ("PD",), ("rowb",))
        for m in range(NM):
            S.scan(S.rowb[0:1, 64 + m * 16 + 1:64 + m * 16 + 16],
                   S.rowb[0:1, m * 16:m * 16 + 15], S.rowb[0:1, m * 16:m * 16 + 15],
                   0.0, ALU.add, ALU.bypass, ("rowb",), ("rowb",))
            S.mset(S.rowb[0:1, 64 + m * 16:64 + m * 16 + 1], 0.0, ("rowb",))
        S.mm(S.W0[:, 0:64], S.cst_o1f[:, :], S.rowb[0:1, 64:128], False, True,
             ("cst_o1f", "rowb"), ("W0",))
        S.cp(rt[:, RK:RK + 64], S.W0[:, 0:64], ("W0",), ("rt",))
        offv = S.rowb[0:1, 64:128].rearrange("a (m k) -> a m k", k=16)[:, :, 15:16]
        totv = S.rowb[0:1, 0:64].rearrange("a (m k) -> a m k", k=16)[:, :, 15:16]
        S.tt(S.rowb[0:1, 124:128].rearrange("a (m o) -> a m o", o=1), offv, totv,
             ALU.add, ("rowb",), ("rowb",))
        p.dma("gpsimd", S.d_cnt_in[:, 0:4], S.rowb[0:1, 124:128],
              reads=("rowb",), writes=("d_cnt_in",))
        p.cc("AllGather", ALU.bypass, GRP4, S.d_cnt_in, S.d_cnt_out,
             reads=("d_cnt_in",), writes=("d_cnt_out",), tag="cntAG")
        p.dma("sync", S.cnt4[:, :], S.d_cnt_out.rearrange("a (r c) -> (a r) c", c=8),
              reads=("d_cnt_out",), writes=("cnt4",))
        S.mm(S.PD[0:4, 0:2], S.cnt4[:, 0:4], S.msk_c[:, :], True, True,
             ("cnt4", "msk_c"), ("PD",))
        S.cp(sm[0:4, 104:106], S.PD[0:4, 0:2], ("PD",), ("sm",))
        S.tt(sm[0:4, 106:107], sm[0:4, 105:106], sm[0:4, 104:105], ALU.subtract,
             ("sm",), ("sm",))
        S.ts(sm[0:4, 106:107], sm[0:4, 106:107], CAP, None, ALU.subtract,
             reads=("sm",), writes=("sm",))
        p.dma("sync", S.rowb[0:1, 96:100], sm[0:4, 106:107],
              reads=("sm",), writes=("rowb",))
        S.mm(S.PD[:, 8:12], S.cst_o1f[:, :], S.rowb[0:1, 96:100], True, True,
             ("cst_o1f", "rowb"), ("PD",))
        S.cp(sm[:, 112:116], S.PD[:, 8:12], ("PD",), ("sm",))
        for m in range(NM):
            S.ts(rt[:, EE:EE + 16], rt[:, RK + m * 16:RK + m * 16 + 16],
                 sm[:, 112 + m:113 + m], None, ALU.is_gt,
                 reads=("rt", "sm"), writes=("rt",))
            S.tt(rt[:, KP + m * 16:KP + m * 16 + 16], rt[:, EE:EE + 16],
                 rt[:, RR + m * 16:RR + m * 16 + 16], ALU.mult,
                 ("rt", "rt"), ("rt",))
            S.tt(rt[:, WW + m * 16:WW + m * 16 + 16],
                 rt[:, WW + m * 16:WW + m * 16 + 16],
                 rt[:, KP + m * 16:KP + m * 16 + 16], ALU.mult,
                 ("rt", "rt"), ("rt",))
        S.tap("rt", rt[:, :], ("rt", "rt", "rt", "rt", "rt"))

    def lane_load_weights(self, lane):
        S, p = self, self.p
        p.dma("sync", S.WS0[:, :].rearrange("p (c r) -> p c r", r=DK),
              S.d_wkb[lane].rearrange("(c p) r -> p c r", p=128),
              reads=("d_wkb",), writes=("WS0",), tag="wk")
        p.dma("sync", S.WS1[:, :].rearrange("p (c r) -> p c r", r=DV),
              S.d_wvb[lane, 0:512, :].rearrange("(c p) r -> p c r", p=128),
              reads=("d_wvb",), writes=("WS1",), tag="wv")
        p.dma("sync", S.WS2[:, :].rearrange("p (c r) -> p c r", r=DV),
              S.d_wvb[lane, 512:1024, :].rearrange("(c p) r -> p c r", p=128),
              reads=("d_wvb",), writes=("WS2",), tag="wv")

    def lane_proj(self, lane, sub, hb):
        S = self
        base = sub * 4 + hb * 2
        ncol = sub * SUB + hb * 256
        KPc = self.RT_KP
        for h in range(H):
            for dc in range(DCH):
                S.mm(S.PA[:, 0:256],
                     S.WS0[:, dc * 512 + h * 128: dc * 512 + (h + 1) * 128],
                     S.nT[:, dc * T + ncol: dc * T + ncol + 256],
                     dc == 0, dc == DCH - 1, ("WS0", "nT"), ("PA",))
            S.cp(S.kfm[:, h * 256:(h + 1) * 256], S.PA[:, 0:256], ("PA",), S.K_kfm)
        for vh, (wvb, wvk) in enumerate(((S.WS1, "WS1"), (S.WS2, "WS2"))):
            for vt in range(2):
                W = S.W0 if vt == 0 else S.W1
                wkey = "W0" if vt == 0 else "W1"
                for dci in range(4):
                    dc = vh * 4 + dci
                    lhs = S.nT[:, dc * T + ncol + vt * 128: dc * T + ncol + vt * 128 + 128]
                    for nh in range(2):
                        S.mm(W[:, nh * 512:(nh + 1) * 512], lhs,
                             wvb[:, dci * 1024 + nh * 512: dci * 1024 + (nh + 1) * 512],
                             dc == 0, dc == DCH - 1, ("nT", wvk), (wkey,))
        for vt in range(2):
            W = S.W0 if vt == 0 else S.W1
            wkey = "W0" if vt == 0 else "W1"
            if lane < NM:
                kcol = S.rt[:, KPc + lane * 16 + base + vt: KPc + lane * 16 + base + vt + 1]
                S.ts(S.vtok[:, vt * 1024:(vt + 1) * 1024], W[:, :], kcol, None,
                     ALU.mult, reads=(wkey, "rt"), writes=S.K_vtok)
            else:
                S.cp(S.vtok[:, vt * 1024:(vt + 1) * 1024], W[:, :], (wkey,), S.K_vtok)
        for dc in range(DCH):
            S.mm(S.PD[0:16, 0:256],
                 S.wgk1b[:, lane * 128 + dc * 16: lane * 128 + (dc + 1) * 16],
                 S.nT[:, dc * T + ncol: dc * T + ncol + 256],
                 dc == 0, dc == DCH - 1, ("wgk1b", "nT"), ("PD",))
        S.cp(S.t1a[0:16, 0:256], S.PD[0:16, 0:256], ("PD",), ("t1a",))
        for vt in range(2):
            S.mm(S.PC[:, :], S.t1a[0:17, vt * 128:(vt + 1) * 128],
                 S.wgk2a[0:17, lane * 512:(lane + 1) * 512], True, True,
                 ("t1a", "t1a_one", "wgk2a"), ("PC",))
            gslice = S.gktok[:, vt * 512:(vt + 1) * 512]
            S.act(gslice, S.PC[:, :], AF.Exp, ("PC",), S.K_gktok, scale=-1.0)
            S.ts(gslice, gslice, 1.0, None, ALU.add, reads=S.K_gktok, writes=S.K_gktok)
            S.act(gslice, gslice, AF.Ln, S.K_gktok, S.K_gktok)
            if lane < NM:
                kcol = S.rt[:, KPc + lane * 16 + base + vt: KPc + lane * 16 + base + vt + 1]
                S.ts(gslice, gslice, kcol, None, ALU.mult,
                     reads=S.K_gktok + ("rt",), writes=S.K_gktok)

    def lane_chunk(self, lane, sub, hb, c, mainpass):
        S = self
        base = sub * 4 + hb * 2
        g = base + c
        st = S.sts[lane]
        stk = f"st{lane}"
        for h in range(H):
            S.mm(S.PA[:, h * 128:(h + 1) * 128],
                 S.gktok[:, c * 512 + h * 128: c * 512 + (h + 1) * 128],
                 S.cst_ug[:, :], True, True, S.K_gktok + ("cst_ug",), ("PA",))
        aC = S.PA[:, :].rearrange("p (h k) -> p h k", k=128)[:, :, 127:128]
        S.act(S.sm[:, 52:56].rearrange("p (h o) -> p h o", o=1), aC, AF.Identity,
              ("PA",), ("sm",))
        S.act(S.sm[:, 48:52], S.sm[:, 52:56], AF.Exp, ("sm",), ("sm",))
        if mainpass:
            S.act(S.eab, S.PA[:, :], AF.Exp, ("PA",), S.K_eab)
            S.act(S.eib, S.PA[:, :], AF.Exp, ("PA",), S.K_eib, scale=-1.0)
            qv = S.qfm.rearrange("p (h k) -> p h k", k=512)[:, :, hb * 256 + c * 128: hb * 256 + c * 128 + 128]
            S.tt(S.qcb.rearrange("p (h k) -> p h k", k=128), qv,
                 S.eab.rearrange("p (h k) -> p h k", k=128), ALU.mult,
                 S.K_qfm + S.K_eab, S.K_qcb)
            kv_ = S.kfm.rearrange("p (h k) -> p h k", k=256)[:, :, c * 128:(c + 1) * 128]
            S.tt(S.kpb.rearrange("p (h k) -> p h k", k=128), kv_,
                 S.eib.rearrange("p (h k) -> p h k", k=128), ALU.mult,
                 S.K_kfm + S.K_eib, S.K_kpb)
            for h in range(H):
                S.mm(S.PB[:, h * 128:(h + 1) * 128], S.kpb[:, h * 128:(h + 1) * 128],
                     S.qcb[:, h * 128:(h + 1) * 128], True, True,
                     S.K_kpb + S.K_qcb, ("PB",))
            ptm = S.cst_pt[:, :].rearrange("p (o k) -> p o k", o=1).broadcast_to([128, 4, 128])
            S.tt(S.ptb.rearrange("p (h k) -> p h k", k=128),
                 S.PB[:, :].rearrange("p (h k) -> p h k", k=128), ptm, ALU.mult,
                 ("PB", "cst_pt"), S.K_ptb)
            for h in range(H):
                S.mm(S.W0[:, h * 256:(h + 1) * 256], S.ptb[:, h * 128:(h + 1) * 128],
                     S.vtok[:, c * 1024 + h * 256: c * 1024 + (h + 1) * 256],
                     True, False, S.K_ptb + S.K_vtok, ("W0",))
                S.mm(S.W0[:, h * 256:(h + 1) * 256], S.qcb[:, h * 128:(h + 1) * 128],
                     st[:, h * 256:(h + 1) * 256], False, True,
                     S.K_qcb + (stk,), ("W0",))
            for h in range(H):
                S.ts(S.kpb[:, h * 128:(h + 1) * 128], S.kpb[:, h * 128:(h + 1) * 128],
                     S.sm[:, 48 + h:49 + h], None, ALU.mult,
                     reads=S.K_kpb + ("sm",), writes=S.K_kpb)
        else:
            for h in range(H):
                S.act(S.eab[:, h * 128:(h + 1) * 128], S.PA[:, h * 128:(h + 1) * 128],
                      AF.Exp, ("PA", "sm"), S.K_eab, scale=-1.0,
                      bias=S.sm[:, 52 + h:53 + h])
            kv_ = S.kfm.rearrange("p (h k) -> p h k", k=256)[:, :, c * 128:(c + 1) * 128]
            S.tt(S.kpb.rearrange("p (h k) -> p h k", k=128), kv_,
                 S.eab.rearrange("p (h k) -> p h k", k=128), ALU.mult,
                 S.K_kfm + S.K_eab, S.K_kpb)
            S.tt(S.sm[:, 128 + lane * 4:132 + lane * 4].rearrange("p (h o) -> p h o", o=1),
                 S.sm[:, 128 + lane * 4:132 + lane * 4].rearrange("p (h o) -> p h o", o=1),
                 S.sm[:, 52:56].rearrange("p (h o) -> p h o", o=1), ALU.add,
                 ("sm", "sm"), ("sm",))
        for h in range(H):
            S.tr(S.PCb[:, h * 128:(h + 1) * 128], S.kpb[:, h * 128:(h + 1) * 128],
                 S.K_kpb, ("PC",))
        S.cp(S.kptb, S.PCb[:, 0:512], ("PC",), S.K_kptb)
        for h in range(H):
            S.mm(S.W1[:, h * 256:(h + 1) * 256], S.kptb[:, h * 128:(h + 1) * 128],
                 S.vtok[:, c * 1024 + h * 256: c * 1024 + (h + 1) * 256],
                 True, True, S.K_kptb + S.K_vtok, ("W1",))
        for h in range(H):
            S.stt(st[:, h * 256:(h + 1) * 256], st[:, h * 256:(h + 1) * 256],
                  S.sm[:, 48 + h:49 + h], S.W1[:, h * 256:(h + 1) * 256],
                  ALU.mult, ALU.add, (stk, "sm", "W1"), (stk,))
        if mainpass:
            osl = S.oacc[:, (hb * 2 + c) * 1024:(hb * 2 + c + 1) * 1024]
            if lane == 0:
                wcol = S.rt[:, self.RT_WW + lane * 16 + g: self.RT_WW + lane * 16 + g + 1]
                S.ts(osl, S.W0[:, :], wcol, None, ALU.mult,
                     reads=("W0", "rt"), writes=S.K_oacc)
            elif lane < NM:
                wcol = S.rt[:, self.RT_WW + lane * 16 + g: self.RT_WW + lane * 16 + g + 1]
                S.stt(osl, S.W0[:, :], wcol, osl, ALU.mult, ALU.add,
                      ("W0", "rt") + S.K_oacc, S.K_oacc)
            else:
                S.tt(osl, osl, S.W0[:, :], ALU.add, S.K_oacc + ("W0",), S.K_oacc)

    def ph_prepass(self):
        S, p = self, self.p
        for L in range(5):
            S.mset(S.sts[L][:, :], 0.0, (f"st{L}",))
        S.mset(S.sm[:, 128:148], 0.0, ("sm",))
        for lane in range(5):
            S.lane_load_weights(lane)
            for sub in range(NSUB):
                for hb in range(2):
                    S.lane_proj(lane, sub, hb)
                    for c in range(2):
                        S.lane_chunk(lane, sub, hb, c, mainpass=False)
        for L in range(5):
            p.dma("gpsimd", S.d_stg_in[:, L * 1024:(L + 1) * 1024], S.sts[L][:, :],
                  reads=(f"st{L}",), writes=("d_stg_in",))
        p.dma("gpsimd", S.d_stgA_in[:, :], S.sm[:, 128:148],
              reads=("sm",), writes=("d_stgA_in",))
        p.cc("AllGather", ALU.bypass, GRP4, S.d_stg_in, S.d_stg_out,
             reads=("d_stg_in",), writes=("d_stg_out",), tag="stAG")
        p.cc("AllGather", ALU.bypass, GRP4, S.d_stgA_in, S.d_stgA_out,
             reads=("d_stgA_in",), writes=("d_stgA_out",), tag="sAAG")

    def ph_combine(self):
        S, p, sm = self, self.p, self.sm
        for j in range(4):
            p.dma("sync", sm[:, 152 + j * 20:172 + j * 20], S.d_stgA_out[j],
                  reads=("d_stgA_out",), writes=("sm",))
        for L in range(5):
            stg = [S.tmpC, S.tmpD]
            stgk = [S.K_tmpC, S.K_tmpD]
            for j in range(4):
                p.dma("sync", stg[j % 2], S.d_stg_out[j, :, L * 1024:(L + 1) * 1024],
                      reads=("d_stg_out",), writes=stgk[j % 2])
                for li in range(4):
                    scol = S.msk_s[:, j * 4 + li: j * 4 + li + 1]
                    acol = sm[:, 152 + li * 20 + L * 4: 152 + li * 20 + L * 4 + 4]
                    if li == 0:
                        S.ts(sm[:, 232:236], acol, scol, None, ALU.mult,
                             reads=("sm", "msk_s"), writes=("sm",))
                    else:
                        S.stt(sm[:, 232:236], acol, scol, sm[:, 232:236],
                              ALU.mult, ALU.add, ("sm", "msk_s", "sm"),
                              ("sm",))
                S.act(sm[:, 232:236], sm[:, 232:236], AF.Exp, ("sm",), ("sm",))
                S.ts(sm[:, 232:236], sm[:, 232:236], S.msk_sel[:, j:j + 1], None,
                     ALU.mult, reads=("sm", "msk_sel"), writes=("sm",))
                for h in range(H):
                    dsl = stg[j % 2][:, h * 256:(h + 1) * 256]
                    ssl = S.sts[L][:, h * 256:(h + 1) * 256]
                    if j == 0:
                        S.ts(ssl, dsl, sm[:, 232 + h:233 + h], None, ALU.mult,
                             reads=stgk[j % 2] + ("sm",), writes=(f"st{L}",))
                    else:
                        S.stt(ssl, dsl, sm[:, 232 + h:233 + h], ssl,
                              ALU.mult, ALU.add,
                              stgk[j % 2] + ("sm", f"st{L}"), (f"st{L}",))

    def ph_attn_main(self):
        S = self
        for sub in range(NSUB):
            S.p.dma("sync", S.WS0[:, :].rearrange("p (c r) -> p c r", r=DK),
                    S.d_wqb.rearrange("(c p) r -> p c r", p=128),
                    reads=("d_wqb",), writes=("WS0",), tag="wq")
            for h in range(H):
                for dc in range(DCH):
                    S.mm(S.PA[:, :],
                         S.WS0[:, dc * 512 + h * 128: dc * 512 + (h + 1) * 128],
                         S.nT[:, dc * T + sub * SUB: dc * T + (sub + 1) * SUB],
                         dc == 0, dc == DCH - 1, ("WS0", "nT"), ("PA",))
                S.ts(S.qfm[:, h * 512:(h + 1) * 512], S.PA[:, :], SCALE_Q, None,
                     ALU.mult, reads=("PA",), writes=S.K_qfm)
            for lane in range(5):
                S.lane_load_weights(lane)
                for hb in range(2):
                    S.lane_proj(lane, sub, hb)
                    for c in range(2):
                        S.lane_chunk(lane, sub, hb, c, mainpass=True)
            if sub == NSUB - 1:
                S.tap("oacc", S.oacc[:, :], S.K_oacc)
            self.ph_epilogue(sub)

    def ph_epilogue(self, sub):
        S = self
        for half, (dst, dkey) in enumerate(((S.WS0, "WS0"), (S.WS2, "WS2"))):
            S.p.dma("sync", dst[:, :].rearrange("p (c r) -> p c r", r=DV),
                    S.d_wgb[half * 512:(half + 1) * 512, :]
                    .rearrange("(c p) r -> p c r", p=128),
                    reads=("d_wgb",), writes=(dkey,), tag="wg")
        for r in range(4):
            g = sub * 4 + r
            osl = S.oacc[:, r * 1024:(r + 1) * 1024]
            for h in range(H):
                S.act(S.tmpB[:, h * 256:(h + 1) * 256], osl[:, h * 256:(h + 1) * 256],
                      AF.Square, S.K_oacc, S.K_tmpB + ("sm",),
                      accum=S.sm[:, 56 + h:57 + h])
            S.act(S.sm[:, 60:64], S.sm[:, 56:60], AF.Ln, ("sm",), ("sm",),
                  scale=1.0 / HDV, bias=RMS_EPS)
            S.act(S.sm[:, 56:60], S.sm[:, 60:64], AF.Exp, ("sm",), ("sm",),
                  scale=-0.5)
            for h in range(H):
                S.stt(S.tmpA[:, h * 256:(h + 1) * 256], osl[:, h * 256:(h + 1) * 256],
                      S.sm[:, 56 + h:57 + h], S.nwb[:, :], ALU.mult, ALU.mult,
                      S.K_oacc + ("sm", "nwb"), S.K_tmpA)
            for dc in range(DCH):
                src = S.WS0 if dc < 4 else S.WS2
                skey = "WS0" if dc < 4 else "WS2"
                lhs = S.nT[:, dc * T + g * 128: dc * T + (g + 1) * 128]
                for nh in range(2):
                    S.mm(S.W1[:, nh * 512:(nh + 1) * 512], lhs,
                         src[:, (dc % 4) * 1024 + nh * 512:(dc % 4) * 1024 + (nh + 1) * 512],
                         dc == 0, dc == DCH - 1, ("nT", skey), ("W1",))
            S.act(S.tmpB, S.W1[:, :], AF.Silu, ("W1",), S.K_tmpB)
            S.tt(S.tmpC, S.tmpA, S.tmpB, ALU.mult, S.K_tmpA + S.K_tmpB, S.K_tmpC)
            for cdc in range(DCH):
                S.tr(S.PCb[:, cdc * 128:(cdc + 1) * 128],
                     S.tmpC[:, cdc * 128:(cdc + 1) * 128], S.K_tmpC, ("PC",))
            S.cp(S.tmpD, S.PCb[:, 0:1024], ("PC",), S.K_tmpD)
            for oh in range(2):
                S.p.dma("sync", S.WS1[:, :].rearrange("p (c r) -> p c r", r=D),
                        S.d_wob[oh * 512:(oh + 1) * 512, :]
                        .rearrange("(c p) r -> p c r", p=128),
                        reads=("d_wob",), writes=("WS1",), tag="wo")
                for dci in range(4):
                    dc = oh * 4 + dci
                    for nh in range(2):
                        S.mm(S.W0[:, nh * 512:(nh + 1) * 512],
                             S.tmpD[:, dc * 128:(dc + 1) * 128],
                             S.WS1[:, dci * 1024 + nh * 512: dci * 1024 + (nh + 1) * 512],
                             dc == 0, dc == DCH - 1, (S.K_tmpD[0], "WS1"), ("W0",))
            ssl = S.stream[:, g * D:(g + 1) * D]
            S.tt(ssl, ssl, S.W0[:, :], ALU.add, ("stream", "W0"), ("stream",))

    def quant_q(self, tag=""):
        S = self
        ks = tuple(set(S.K_fhh) | set(S.K_fam) | set(S.K_frc) | set(S.K_fw1)
                   | set(S.K_fw2) | set(S.K_fbs1) | set(S.K_fph))
        S.fp4_quant(S.f_hh, 2048, self._q_gs6, self._q_gs, S.f_am, S.f_rc,
                    S.f_w1, S.f_w2, S.f_ph, S.f_bs1, ks)

    def quant_t(self, tag=""):
        S = self
        for c in range(8):
            S.tr(S.PCb[:, c * 128:(c + 1) * 128], S.f_ph[:, c * 128:(c + 1) * 128],
                 S.K_fph, ("PC",), tag=tag)
            S.tr(S.PDb[:, c * 128:(c + 1) * 128],
                 S.f_ph[:, 1024 + c * 128:1024 + (c + 1) * 128], S.K_fph,
                 ("PD",), tag=tag)
        S.cp(S.hqt_s[:, 0:1024], S.PCb[:, 0:1024], ("PC",), S.K_hqt)
        S.cp(S.hqt_s[:, 1024:2048], S.PDb[:, 0:1024], ("PD",), S.K_hqt)

    def quant_2048(self, tag=""):
        self.quant_q(tag=tag)
        self.quant_t(tag=tag)

    def ph_wquant(self):
        S, p, sm = self, self.p, self.sm

        def qpair(srcs, dsts, gcol, wcol, dkey, tag):
            for half, sap in enumerate(srcs):
                p.dma("sync", S.f_hh[:, half * 1024:(half + 1) * 1024], sap,
                      writes=S.K_fhh, tag=tag)
            self._q_gs6 = sm[:, gcol:gcol + 1]
            self._q_gs = sm[:, wcol:wcol + 1]
            self.quant_2048(tag=tag)
            for half, dap in enumerate(dsts):
                p.dma("gpsimd", dap,
                      S.hqt_s[:, half * 1024:(half + 1) * 1024]
                      .rearrange("p (c k) -> p c k", k=128),
                      reads=S.K_hqt, writes=(dkey,), tag=tag)

        for name, shard, dout, gcol, wcol in (
                ("g", S.fw_gshard, S.d_gwq_in, 40, 32),
                ("u", S.fw_ushard, S.d_uwq_in, 41, 33)):
            for rt2 in range(2):
                srcs = [shard[(rt2 * 2 + h) * 128:(rt2 * 2 + h + 1) * 128, :]
                        for h in range(2)]
                dv = dout.rearrange("(c p) r -> p c r", p=128)
                dsts = [dv[:, :, (rt2 * 2 + h) * 128:(rt2 * 2 + h + 1) * 128]
                        for h in range(2)]
                qpair(srcs, dsts, gcol, wcol, "d_" + name + "wq_in", "wq" + name)
        for ic2 in range(2):
            srcs = [S.fw_dshard[:, (ic2 * 2 + h) * 1024:(ic2 * 2 + h + 1) * 1024]
                    for h in range(2)]
            dsts = [S.d_dwq_in[(ic2 * 2 + h) * 1024:(ic2 * 2 + h + 1) * 1024, :]
                    .rearrange("(c p) r -> p c r", p=128) for h in range(2)]
            qpair(srcs, dsts, 42, 34, "d_dwq_in", "wqd")
        p.cc("AllGather", ALU.bypass, GRP8, S.d_gwq_in, S.d_gwq,
             reads=("d_gwq_in",), writes=("d_gwq",), tag="gwAG")
        p.cc("AllGather", ALU.bypass, GRP8, S.d_uwq_in, S.d_uwq,
             reads=("d_uwq_in",), writes=("d_uwq",), tag="uwAG")
        p.cc("AllGather", ALU.bypass, GRP8, S.d_dwq_in, S.d_dwq,
             reads=("d_dwq_in",), writes=("d_dwq",), tag="dwAG")

    def ph_ffn0(self):
        S, p, sm = self, self.p, self.sm
        for t2 in range(8):
            for h in range(2):
                t = t2 * 2 + h
                xs = S.stream[:, t * D:(t + 1) * D]
                S.ts(S.f_hh[:, h * 1024:(h + 1) * 1024], xs, S.mrs[:, t:t + 1],
                     S.mrs[:, 16 + t:17 + t], ALU.subtract, ALU.mult,
                     reads=("stream", "mrs"), writes=S.K_fhh)
            self._q_gs6 = sm[:, 38:39]
            self._q_gs = sm[:, 24:25]
            self.quant_2048(tag="xq")
            for h in range(2):
                p.dma("gpsimd", S.d_xqT[t2 * 2 + h],
                      S.hqt_s[:, h * 1024:(h + 1) * 1024], reads=S.K_hqt,
                      writes=("d_xqT",), tag="xqW")
            if t2 == 0:
                S.tap("xq0", S.f_ph[:, :], S.K_fph)
                S.tap("smsc", S.sm[:, :], ("sm",))
        def stageT(it):
            sb, ic = divmod(it, 8)
            self._q_gs6 = sm[:, 39:40]
            self._q_gs = sm[:, 25:26]
            self.quant_t(tag="hq")
            for tl in range(4):
                for ib in range(4):
                    for nh in range(2):
                        S.mm(S.W0[:, nh * 512:(nh + 1) * 512],
                             S.hqt_s[:, tl * 512 + ib * 128: tl * 512 + (ib + 1) * 128],
                             S.WS2[:, ib * 1024 + nh * 512: ib * 1024 + (nh + 1) * 512],
                             ib == 0, ib == 3, S.K_hqt + ("WS2",), ("W0",))
                ysl = S.yacc[:, tl * 1024:(tl + 1) * 1024]
                if ic == 0:
                    S.cp(ysl, S.W0[:, :], ("W0",), S.K_yacc)
                else:
                    S.tt(ysl, ysl, S.W0[:, :], ALU.add, S.K_yacc + ("W0",),
                         S.K_yacc)
            if ic == 7:
                for tl in range(4):
                    t = sb * 4 + tl
                    ssl = S.stream[:, t * D:(t + 1) * D]
                    S.stt(ssl, S.yacc[:, tl * 1024:(tl + 1) * 1024], sm[:, 37:38],
                          ssl, ALU.mult, ALU.add, S.K_yacc + ("sm", "stream"),
                          ("stream",))

        def loads2(it):
            sb, ic = divmod(it, 8)
            for r8 in range(8):
                for ib in range(4):
                    p.dma("sync", S.WS2[:, ib * 1024 + r8 * 128: ib * 1024 + (r8 + 1) * 128],
                          S.d_dwq[r8, ic * 512 + ib * 128: ic * 512 + (ib + 1) * 128, :],
                          reads=("d_dwq",), writes=("WS2",), tag="lddw")

        for it in range(32):
            sb, ic = divmod(it, 8)
            if it >= 1:
                self._q_gs6 = sm[:, 39:40]
                self._q_gs = sm[:, 25:26]
                self.quant_q(tag="hq")
                stageT(it - 1)
            p.dma("sync", S.WS0[:, :].rearrange("p (c r) -> p c r", r=512),
                  S.d_gwq[ic].rearrange("(c p) r -> p c r", p=128),
                  reads=("d_gwq",), writes=("WS0",), tag="ldgw")
            p.dma("sync", S.WS1[:, :].rearrange("p (c r) -> p c r", r=512),
                  S.d_uwq[ic].rearrange("(c p) r -> p c r", p=128),
                  reads=("d_uwq",), writes=("WS1",), tag="lduw")
            for tl in range(4):
                t = sb * 4 + tl
                p.dma("sync", S.xqt_s, S.d_xqT[t], reads=("d_xqT",),
                      writes=S.K_xqt, tag="ldxq")
                for dc in range(DCH):
                    S.mm(S.PA[:, :], S.xqt_s[:, dc * 128:(dc + 1) * 128],
                         S.WS0[:, dc * 512:(dc + 1) * 512], dc == 0,
                         dc == DCH - 1, S.K_xqt + ("WS0",), ("PA",))
                for dc in range(DCH):
                    S.mm(S.PB[:, :], S.xqt_s[:, dc * 128:(dc + 1) * 128],
                         S.WS1[:, dc * 512:(dc + 1) * 512], dc == 0,
                         dc == DCH - 1, S.K_xqt + ("WS1",), ("PB",))
                S.act(S.f_sil, S.PA[:, :], AF.Silu, ("PA", "sm"), S.K_fsil,
                      scale=sm[:, 35:36])
                S.stt(S.f_hh[:, tl * 512:(tl + 1) * 512], S.PB[:, :],
                      sm[:, 36:37], S.f_sil, ALU.mult, ALU.mult,
                      ("PB", "sm") + S.K_fsil, S.K_fhh)
            loads2(it)
        self._q_gs6 = sm[:, 39:40]
        self._q_gs = sm[:, 25:26]
        self.quant_q(tag="hq")
        stageT(31)

    def ph_l1_prep(self):
        S, p = self, self.p
        for n in "qkvgo":
            p.dma("sync", S.rowc[:, :], S.a_b[n], writes=S.K_rowc, tag="l1b")
            S.cp(S.brow[n][:, :], S.rowc[:, :], S.K_rowc, ("brow",))
        p.dma("sync", S.bcol[:, 0:8],
              S.a_b["q"].rearrange("o (c p) -> (o p) c", p=128), writes=("bcol",),
              allow_slow_non_contiguous=True)
        S.ts(S.bcol[:, 8:16], S.bcol[:, 0:8], -1.0, None, ALU.mult,
             reads=("bcol",), writes=("bcol",))

    def ph_l1_attn(self, limit=99):
        S, p = self, self.p
        self._l1lim = limit
        self.ph_l1_prep()
        if limit < 1:
            return
        for t in range(NT):
            for W, ps, wname in ((S.W0, "W0", "k"), (S.W1, "W1", "v")):
                for nh in range(2):
                    S.mm(W[:, nh * 512:(nh + 1) * 512], S.cst_o1b[:, :],
                         S.brow[wname][:, nh * 512:(nh + 1) * 512], True, False,
                         ("cst_o1b", "brow"), (ps,))
                for vh, (wb, wk) in enumerate(((S.WS1, "WS1"), (S.WS2, "WS2"))):
                    S.p.dma("sync", wb[:, :].rearrange("p (c r) -> p c r", r=D),
                            S.d_aWb[wname][vh * 512:(vh + 1) * 512, :]
                            .rearrange("(c p) r -> p c r", p=128),
                            reads=("d_aWb" + wname,), writes=(wk,), tag="l1w" + wname)
                    for dci in range(4):
                        dc = vh * 4 + dci
                        lhs = S.nT[:, dc * T + t * 128: dc * T + (t + 1) * 128]
                        for nh in range(2):
                            S.mm(W[:, nh * 512:(nh + 1) * 512], lhs,
                                 wb[:, dci * 1024 + nh * 512: dci * 1024 + (nh + 1) * 512],
                                 False, dc == DCH - 1, ("nT", wk), (ps,))
            S.act(S.l1e1, S.W0[:, :], AF.Relu, ("W0",), S.K_l1e1)
            S.act(S.l1e2, S.W0[:, :], AF.Relu, ("W0",), S.K_l1e2, scale=-1.0)
            S.act(S.l1e2, S.l1e2, AF.Exp, S.K_l1e2, S.K_l1e2, scale=-1.0)
            S.tt(S.l1kf, S.l1e2, S.l1e1, ALU.add, S.K_l1e1 + S.K_l1e2, S.K_l1kf)
            S.cp(S.l1vf, S.W1[:, :], ("W1",), S.K_l1vf)
            if t == 5:
                S.tap("l1kf5", S.l1kf, S.K_l1kf)
                S.tap("l1vf5", S.l1vf, S.K_l1vf)
            for h in range(H1):
                ps = S.PC if h < 8 else S.PD
                psk = "PC" if h < 8 else "PD"
                S.mm(ps[0:64, (h % 8) * 64:(h % 8) * 64 + 64],
                     S.l1kf[:, h * 64:(h + 1) * 64], S.l1vf[:, h * 64:(h + 1) * 64],
                     True, True, S.K_l1kf + S.K_l1vf, (psk,))
            if t == 0:
                S.cp(S.kvb[0:64, 0:512], S.PC[0:64, :], ("PC",), S.K_kvb)
                S.cp(S.kvb[0:64, 512:1024], S.PD[0:64, :], ("PD",), S.K_kvb)
            else:
                S.tt(S.kvb[0:64, 0:512], S.kvb[0:64, 0:512], S.PC[0:64, :],
                     ALU.add, S.K_kvb + ("PC",), S.K_kvb)
                S.tt(S.kvb[0:64, 512:1024], S.kvb[0:64, 512:1024], S.PD[0:64, :],
                     ALU.add, S.K_kvb + ("PD",), S.K_kvb)
        p.dma("gpsimd", S.d_kv_in, S.kvb[:, :], reads=S.K_kvb, writes=("d_kv_in",))
        p.cc("AllReduce", ALU.add, GRP4, S.d_kv_in, S.d_kv_out,
             reads=("d_kv_in",), writes=("d_kv_out",), tag="kvAR")
        p.dma("sync", S.kvb[:, :], S.d_kv_out, reads=("d_kv_out",), writes=S.K_kvb)
        S.mset(S.kvb2[:, :], 0.0, S.K_kvb2)
        S.cp(S.kvb2[0:64, :].rearrange("q (h k) -> q h k", k=128)[:, :, 0:64],
             S.kvb[0:64, :].rearrange("q (h k) -> q h k", k=64)[:, 0:16:2, :],
             S.K_kvb, S.K_kvb2)
        p.dma("gpsimd",
              S.kvb2[64:128, :].rearrange("q (h k) -> q h k", k=128)[:, :, 64:128],
              S.kvb[0:64, :].rearrange("q (h k) -> q h k", k=64)[:, 1:16:2, :],
              reads=S.K_kvb, writes=S.K_kvb2)
        S.tap("kvb2", S.kvb2[:, :], S.K_kvb2)
        if limit < 3:
            return
        p.dma("gpsimd", S.d_xnt_in.rearrange("(c p) t -> p c t", p=128),
              S.nT[:, :].rearrange("p (c t) -> p c t", t=T),
              reads=("nT",), writes=("d_xnt_in",), tag="xntW")
        p.cc("AllGather", ALU.bypass, GRP8, S.d_xnt_in, S.d_xnt,
             reads=("d_xnt_in",), writes=("d_xnt",), tag="xntAG")
        for sub in range(NSUB):
            for ec in range(DCH):
                S.p.dma("sync",
                        S.WS0[:, 0:1024].rearrange("p (c r) -> p c r", r=128),
                        S.d_aWb["q"].rearrange("(c p) r -> p c r", p=128)
                        [:, :, ec * 128:(ec + 1) * 128],
                        reads=("d_aWbq",), writes=("WS0",), tag="l1wq")
                for dc in range(DCH):
                    S.mm(S.PA[:, :], S.WS0[:, dc * 128:(dc + 1) * 128],
                         S.nT[:, dc * T + sub * SUB: dc * T + (sub + 1) * SUB],
                         dc == 0, dc == DCH - 1, ("WS0", "nT"), ("PA",))
                S.act(S.l1e1[:, 0:512], S.PA[:, :], AF.Relu, ("PA", "bcol"),
                      S.K_l1e1, bias=S.bcol[:, ec:ec + 1])
                S.act(S.l1e2[:, 0:512], S.PA[:, :], AF.Relu, ("PA", "bcol"),
                      S.K_l1e2, scale=-1.0, bias=S.bcol[:, 8 + ec:9 + ec])
                S.act(S.l1e2[:, 0:512], S.l1e2[:, 0:512], AF.Exp, S.K_l1e2,
                      S.K_l1e2, scale=-1.0)
                S.tt(S.l1q[:, ec * 512:(ec + 1) * 512], S.l1e2[:, 0:512],
                     S.l1e1[:, 0:512], ALU.add, S.K_l1e1 + S.K_l1e2, S.K_l1q)
            if self._l1lim < 5:
                continue
            for half, (dst, dkey) in enumerate(((S.WS1, "WS1"), (S.WS2, "WS2"))):
                S.p.dma("sync", dst[:, :].rearrange("p (c r) -> p c r", r=D),
                        S.d_aWb["g"][half * 512:(half + 1) * 512, :]
                        .rearrange("(c p) r -> p c r", p=128),
                        reads=("d_aWbg",), writes=(dkey,), tag="l1wg")
            for tl in range(4):
                t = sub * 4 + tl
                for nh in range(2):
                    S.mm(S.W0[:, nh * 512:(nh + 1) * 512], S.cst_o1b[:, :],
                         S.brow["g"][:, nh * 512:(nh + 1) * 512], True, False,
                         ("cst_o1b", "brow"), ("W0",))
                for dc in range(DCH):
                    src = S.WS1 if dc < 4 else S.WS2
                    skey = "WS1" if dc < 4 else "WS2"
                    lhs = S.nT[:, dc * T + t * 128: dc * T + (t + 1) * 128]
                    for nh in range(2):
                        S.mm(S.W0[:, nh * 512:(nh + 1) * 512], lhs,
                             src[:, (dc % 4) * 1024 + nh * 512:(dc % 4) * 1024 + (nh + 1) * 512],
                             False, dc == DCH - 1, ("nT", skey), ("W0",))
                S.act(S.l1sg, S.W0[:, :], AF.Sigmoid, ("W0",), S.K_l1sg)
                if self._l1lim < 6:
                    continue
                for ec in range(DCH):
                    S.mm(S.W1[:, ec * 128:(ec + 1) * 128],
                         S.l1q[:, ec * 512 + tl * 128: ec * 512 + (tl + 1) * 128],
                         S.kvb2[:, ec * 128:(ec + 1) * 128],
                         True, True, S.K_l1q + S.K_kvb2, ("W1",))
                S.tt(S.l1ga, S.l1sg, S.W1[:, :], ALU.mult, S.K_l1sg + ("W1",),
                     S.K_l1ga)
                if sub == 0 and tl == 0:
                    S.tap("l1q0", S.l1q[:, :], S.K_l1q)
                    S.tap("l1sg0", S.l1sg, S.K_l1sg)
                    S.tap("l1ga0", S.l1ga, S.K_l1ga)
                for cdc in range(DCH):
                    S.tr(S.PCb[:, cdc * 128:(cdc + 1) * 128],
                         S.l1ga[:, cdc * 128:(cdc + 1) * 128], S.K_l1ga, ("PC",))
                S.cp(S.l1gt, S.PCb[:, 0:1024], ("PC",), S.K_l1gt)
                if self._l1lim < 7:
                    continue
                for nh in range(2):
                    S.mm(S.W0[:, nh * 512:(nh + 1) * 512], S.cst_o1b[:, :],
                         S.brow["o"][:, nh * 512:(nh + 1) * 512], True, False,
                         ("cst_o1b", "brow"), ("W0",))
                for oh in range(2):
                    S.p.dma("sync", S.WS0[:, :].rearrange("p (c r) -> p c r", r=D),
                            S.d_aWb["o"][oh * 512:(oh + 1) * 512, :]
                            .rearrange("(c p) r -> p c r", p=128),
                            reads=("d_aWbo",), writes=("WS0",), tag="l1wo")
                    for dci in range(4):
                        dc = oh * 4 + dci
                        for nh in range(2):
                            S.mm(S.W0[:, nh * 512:(nh + 1) * 512],
                                 S.l1gt[:, dc * 128:(dc + 1) * 128],
                                 S.WS0[:, dci * 1024 + nh * 512: dci * 1024 + (nh + 1) * 512],
                                 False, dc == DCH - 1,
                                 (S.K_l1gt[0], "WS0"), ("W0",))
                ssl = S.stream[:, t * D:(t + 1) * D]
                S.tt(ssl, ssl, S.W0[:, :], ALU.add, ("stream", "W0"), ("stream",))

    def transpose_weight_to_dram(self, src_dram, dst_dram, skey, dkey):
        S, p = self, self.p
        for rtile in range(8):
            stage = S.wstg[:, 0:1024]
            p.dma("sync", stage, src_dram[rtile * 128:(rtile + 1) * 128, :],
                  reads=(skey,), writes=S.K_wstg, tag="twl")
            S.cp(S.tmpC, stage, S.K_wstg, S.K_tmpC)
            for c in range(DCH):
                S.tr(S.PCb[:, c * 128:(c + 1) * 128], S.tmpC[:, c * 128:(c + 1) * 128],
                     S.K_tmpC, ("PC",))
            S.cp(S.tmpD, S.PCb[:, 0:1024], ("PC",), S.K_tmpD)
            dst = dst_dram.rearrange("(c p) r -> p c r", p=128)[:, :, rtile * 128:(rtile + 1) * 128]
            p.dma("gpsimd", dst, S.tmpD[:, :].rearrange("p (c k) -> p c k", k=128),
                  reads=S.K_tmpD, writes=(dkey,), tag="twS")

    def ph_l1_route(self):
        S, p, sm, rt = self, self.p, self.sm, self.rt
        stage = S.wstg[0:16, 0:1024]
        p.dma("sync", stage, S.m_gate_w, writes=S.K_wstg, tag="l1gw")
        S.cp(S.tmpC[0:16, :], stage, S.K_wstg, S.K_tmpC)
        for c in range(DCH):
            S.tr(S.PCb[:, c * 16:(c + 1) * 16], S.tmpC[0:16, c * 128:(c + 1) * 128],
                 S.K_tmpC, ("PC",))
        S.cp(S.gwt[:, :], S.PCb[:, 0:128], ("PC",), ("gwt",))
        for t in range(NT):
            for dc in range(DCH):
                S.mm(S.PD[:, 0:16], S.nT[:, dc * T + t * 128: dc * T + (t + 1) * 128],
                     S.gwt[:, dc * 16:(dc + 1) * 16], dc == 0, dc == DCH - 1,
                     ("nT", "gwt"), ("PD",))
            dst = rt[:, 0:256].rearrange("p (m k) -> p m k", k=16)[:, :, t:t + 1]
            S.act(dst, S.PD[:, 0:16].rearrange("p (m o) -> p m o", o=1),
                  AF.Sigmoid, ("PD",), ("rt",))
        p.dma("sync", S.rowb[0:1, 0:16], S.m_ebias, writes=("rowb",))
        S.mm(S.PD[:, 0:16], S.cst_o1f[:, :], S.rowb[0:1, 0:16], True, True,
             ("cst_o1f", "rowb"), ("PD",))
        S.cp(sm[:, 144:160], S.PD[:, 0:16], ("PD",), ("sm",))
        for e in range(E16):
            S.ts(rt[:, 256 + e * 16:256 + e * 16 + 16], rt[:, e * 16:e * 16 + 16],
                 sm[:, 144 + e:145 + e], None, ALU.add,
                 reads=("rt", "sm"), writes=("rt",))
        for e in range(E16):
            first = True
            for j in range(E16):
                if j == e:
                    continue
                S.tt(sm[:, 160:176], rt[:, 256 + j * 16:256 + j * 16 + 16],
                     rt[:, 256 + e * 16:256 + e * 16 + 16], ALU.is_gt,
                     ("rt",), ("sm",))
                if first:
                    S.cp(rt[:, 512 + e * 16:512 + e * 16 + 16], sm[:, 160:176],
                         ("sm",), ("rt",))
                    first = False
                else:
                    S.tt(rt[:, 512 + e * 16:512 + e * 16 + 16],
                         rt[:, 512 + e * 16:512 + e * 16 + 16], sm[:, 160:176],
                         ALU.add, ("rt", "sm"), ("rt",))
                if j < e:
                    S.tt(sm[:, 160:176], rt[:, 256 + j * 16:256 + j * 16 + 16],
                         rt[:, 256 + e * 16:256 + e * 16 + 16], ALU.is_equal,
                         ("rt",), ("sm",))
                    S.tt(rt[:, 512 + e * 16:512 + e * 16 + 16],
                         rt[:, 512 + e * 16:512 + e * 16 + 16], sm[:, 160:176],
                         ALU.add, ("rt", "sm"), ("rt",))
        S.ts(rt[:, 512:768], rt[:, 512:768], 7.5, None, ALU.is_lt,
             reads=("rt",), writes=("rt",))
        S.tt(rt[:, 256:512], rt[:, 512:768], rt[:, 0:256], ALU.mult,
             ("rt", "rt"), ("rt",))
        S.cp(sm[:, 176:192], rt[:, 256:272], ("rt",), ("sm",))
        for e in range(1, E16):
            S.tt(sm[:, 176:192], sm[:, 176:192], rt[:, 256 + e * 16:256 + e * 16 + 16],
                 ALU.add, ("sm", "rt"), ("sm",))
        S.recip(sm[:, 192:208], sm[:, 176:192], ("sm",), ("sm",))
        for e in range(E16):
            S.stt(rt[:, 256 + e * 16:256 + e * 16 + 16],
                  rt[:, 256 + e * 16:256 + e * 16 + 16], ROUTE_SCALE,
                  sm[:, 192:208], ALU.mult, ALU.mult, ("rt", "sm"), ("rt",))
        for t in range(NT):
            csrc = rt[:, 256:512].rearrange("p (m k) -> p m k", k=16)[:, :, t:t + 1]
            S.cp(sm[:, 224:240].rearrange("p (m o) -> p m o", o=1), csrc,
                 ("rt",), ("sm",))
            p.dma("gpsimd", S.d_c_in[t * 128:(t + 1) * 128, :], sm[:, 224:240],
                  reads=("sm",), writes=("d_c_in",), tag="cW")
        p.cc("AllGather", ALU.bypass, GRP8, S.d_c_in, S.d_c_out,
             reads=("d_c_in",), writes=("d_c_out",), tag="cAG")
        for j in range(8):
            stage = S.l1e1[:, 0:256]
            p.dma("sync", stage.rearrange("p (t e) -> p t e", e=16),
                  S.d_c_out[j].rearrange("(t p) e -> p t e", p=128),
                  reads=("d_c_out",), writes=S.K_l1e1, tag="cR")
            for half in range(2):
                S.tr(S.PD[:, 0:128], S.l1e1[:, half * 128:(half + 1) * 128],
                     S.K_l1e1, ("PD",), f32=True)
                S.cp(S.l1e2[:, 0:128], S.PD[:, 0:128], ("PD",), S.K_l1e2)
                for q4 in range(2):
                    S.mm(S.PA[:, q4 * 8:(q4 + 1) * 8],
                         S.l1e2[q4 * 64:(q4 + 1) * 64, 0:128],
                         S.esel_sb[q4 * 64:(q4 + 1) * 64, :],
                         True, True, S.K_l1e2 + ("esel_sb",), ("PA",))
                g0 = j * 16 + half * 8
                dst = S.cmy[:, :].rearrange("p (e k) -> p e k", k=128)[:, :, g0:g0 + 8]
                S.cp(dst, S.PA[:, 0:16].rearrange("p (g e) -> p e g", e=2),
                     ("PA",), ("cmy",))

    def ph_moe_wprep(self):
        S, p = self, self.p
        jobs = []
        for e in range(EPC):
            for m, src in ((0, S.m_wg[e]), (1, S.m_wu[e]), (2, S.m_wd[e])):
                for rt in range(8):
                    jobs.append((src[rt * 128:(rt + 1) * 128, :],
                                 S.d_ewT[e, m], rt, "d_ewT"))
        for m, src in ((0, S.m_sg), (1, S.m_su), (2, S.m_sd)):
            for rt in range(8):
                jobs.append((src[rt * 128:(rt + 1) * 128, :],
                             S.d_swT[m], rt, "d_swT"))
        n = len(jobs)
        for i in range(n + 1):
            if i < n:
                src, ddst, rt, dkey = jobs[i]
                b = i % 3
                p.dma("sync", S.pstg[b][0][:, :], src, writes=S.pstg[b][1],
                      tag="mwp")
            if i >= 1:
                src, ddst, rt, dkey = jobs[i - 1]
                b = (i - 1) % 3
                stg, sk = S.pstg[b][0], S.pstg[b][1]
                pre = S.pout[b][0][:, 0:1024]
                post = S.pout[b][0][:, 1024:2048]
                ok = S.pout[b][1]
                tb, tbk = ((S.PCb, "PC") if (i - 1) % 2 == 0 else (S.PDb, "PD"))
                S.cp(pre, stg, sk, ok, tag="mwp")
                for c in range(DCH):
                    S.tr(tb[:, c * 128:(c + 1) * 128], pre[:, c * 128:(c + 1) * 128],
                         ok, (tbk,), tag="mwp")
                S.cp(post, tb[:, 0:1024], (tbk,), ok, eng="scalar", tag="mwp")
                dv = ddst.rearrange("(c p) r -> p c r", p=128)[:, :, rt * 128:(rt + 1) * 128]
                p.dma("sync", dv, post.rearrange("p (c k) -> p c k", k=128),
                      reads=ok, writes=(dkey,), tag="mwp")

    def moe_stageA(self, pr, gate_bufs, xsrc, ccol):
        S, p = self, self.p
        ps = S.moe_ps[pr]
        xb, xk = S.e_xnt2[pr]
        if xsrc is not None:
            p.dma("sync", xb.rearrange("p (c t) -> p c t", t=128), xsrc,
                  reads=("d_xnt",), writes=xk, tag="exnt")
            lhs = lambda dc: xb[:, dc * 128:(dc + 1) * 128]
            lhk = xk
        else:
            lhs, lhk = gate_bufs.pop("lhs"), gate_bufs.pop("lhk")
        for dc in range(DCH):
            wgb, wgk = gate_bufs["g"][0 if dc < 4 else 1]
            for nh in range(2):
                S.mm(ps["G"][nh],
                     lhs(dc),
                     wgb[:, (dc % 4) * 1024 + nh * 512:(dc % 4) * 1024 + (nh + 1) * 512],
                     dc == 0, dc == DCH - 1, lhk + (wgk,), ps["GK"])
        wub, wuk = gate_bufs["u"]
        for dc in range(DCH):
            for nh in range(2):
                S.mm(ps["U"][nh], lhs(dc),
                     wub[:, dc * 1024 + nh * 512: dc * 1024 + (nh + 1) * 512],
                     dc == 0, dc == DCH - 1, lhk + (wuk,), ps["UKn"][nh])
        sil, slk = S.e_sil2[pr]
        hb, hk = S.e_h2[pr]
        for nh in range(2):
            S.act(sil[:, nh * 512:(nh + 1) * 512], ps["G"][nh], AF.Silu,
                  ps["GK"], slk)
            if ccol is not None:
                S.stt(hb[:, nh * 512:(nh + 1) * 512], ps["U"][nh], ccol,
                      sil[:, nh * 512:(nh + 1) * 512], ALU.mult, ALU.mult,
                      ps["UK"] + ("cmy",) + slk, hk)
            else:
                S.tt(hb[:, nh * 512:(nh + 1) * 512], ps["U"][nh],
                     sil[:, nh * 512:(nh + 1) * 512], ALU.mult,
                     ps["UK"] + slk, hk)

    def moe_stageB(self, pr, down_bufs):
        S = self
        ps = S.moe_ps[pr]
        hb, hk = S.e_h2[pr]
        hT, hTk = S.e_hT2[pr]
        for c in range(DCH):
            S.tr(ps["Gb"][:, c * 128:(c + 1) * 128], hb[:, c * 128:(c + 1) * 128],
                 hk, ps["GK"])
        S.cp(hT, ps["Gb"], ps["GK"], hTk)
        for ib in range(DCH):
            wdb, wdk = down_bufs[0 if ib < 4 else 1]
            for nh in range(2):
                S.mm(ps["U"][nh], hT[:, ib * 128:(ib + 1) * 128],
                     wdb[:, (ib % 4) * 1024 + nh * 512:(ib % 4) * 1024 + (nh + 1) * 512],
                     ib == 0, ib == DCH - 1, hTk + (wdk,), ps["UKn"][nh])
        return ps

    def ph_l1_experts(self):
        S, p = self, self.p
        NG = 8 * NT
        for e in range(EPC):
            p.dma("sync", S.WS0[:, :].rearrange("p (c r) -> p c r", r=1024),
                  S.d_ewT[e, 2, 0:512, :].rearrange("(c p) r -> p c r", p=128),
                  reads=("d_ewT",), writes=("WS0",), tag="ewd0")
            p.dma("sync", S.WS1[:, :].rearrange("p (c r) -> p c r", r=1024),
                  S.d_ewT[e, 2, 512:1024, :].rearrange("(c p) r -> p c r", p=128),
                  reads=("d_ewT",), writes=("WS1",), tag="ewd1")
            p.dma("sync", S.WS2[:, :].rearrange("p (c r) -> p c r", r=1024),
                  S.d_ewT[e, 0, 0:512, :].rearrange("(c p) r -> p c r", p=128),
                  reads=("d_ewT",), writes=("WS2",), tag="ewg0")
            p.dma("sync", S.e_wg2.rearrange("p (c r) -> p c r", r=1024),
                  S.d_ewT[e, 0, 512:1024, :].rearrange("(c p) r -> p c r", p=128),
                  reads=("d_ewT",), writes=S.K_ewg2, tag="ewg1")
            p.dma("sync", S.e_wu.rearrange("p (c r) -> p c r", r=1024),
                  S.d_ewT[e, 1].rearrange("(c p) r -> p c r", p=128),
                  reads=("d_ewT",), writes=S.K_ewu, tag="ewu")
            gb = {"g": [(S.WS2, "WS2"), (S.e_wg2, S.K_ewg2[0])],
                  "u": (S.e_wu, S.K_ewu[0])}
            for g in range(NG + 1):
                if g < NG:
                    xsrc = (S.d_xnt[g // 16][:, (g % 16) * 128:(g % 16) * 128 + 128]
                            .rearrange("(c p) t -> p c t", p=128))
                    ccol = S.cmy[:, e * 128 + g: e * 128 + g + 1]
                    S.moe_stageA(g % 2, dict(gb), xsrc, ccol)
                if g >= 1:
                    gp = g - 1
                    ps = S.moe_stageB(gp % 2, [(S.WS0, "WS0"), (S.WS1, "WS1")])
                    yb, yk = S.e_y2[gp % 2]
                    for nh in range(2):
                        S.cp(yb[:, nh * 512:(nh + 1) * 512], ps["U"][nh],
                             ps["UKn"][nh], yk)
                    if e == 0:
                        p.dma("gpsimd", S.d_y_in[gp * 128:(gp + 1) * 128, :], yb,
                              reads=yk, writes=("d_y_in",), tag="yW")
                    else:
                        p.dma("gpsimd", S.d_y_in[gp * 128:(gp + 1) * 128, :], yb,
                              reads=yk + ("d_y_in",), writes=("d_y_in",), tag="yWa",
                              accum_op=ALU.add)
        p.cc("ReduceScatter", ALU.add, GRP8, S.d_y_in, S.d_y_out,
             reads=("d_y_in",), writes=("d_y_out",), tag="yRS")

    def ph_l1_shared(self):
        S, p = self, self.p
        p.dma("sync", S.WS0[:, :].rearrange("p (c r) -> p c r", r=1024),
              S.d_swT[2, 0:512, :].rearrange("(c p) r -> p c r", p=128),
              reads=("d_swT",), writes=("WS0",), tag="swd0")
        p.dma("sync", S.WS1[:, :].rearrange("p (c r) -> p c r", r=1024),
              S.d_swT[2, 512:1024, :].rearrange("(c p) r -> p c r", p=128),
              reads=("d_swT",), writes=("WS1",), tag="swd1")
        p.dma("sync", S.WS2[:, :].rearrange("p (c r) -> p c r", r=1024),
              S.d_swT[0, 0:512, :].rearrange("(c p) r -> p c r", p=128),
              reads=("d_swT",), writes=("WS2",), tag="swg0")
        p.dma("sync", S.e_wg2.rearrange("p (c r) -> p c r", r=1024),
              S.d_swT[0, 512:1024, :].rearrange("(c p) r -> p c r", p=128),
              reads=("d_swT",), writes=S.K_ewg2, tag="swg1")
        p.dma("sync", S.e_wu.rearrange("p (c r) -> p c r", r=1024),
              S.d_swT[1].rearrange("(c p) r -> p c r", p=128),
              reads=("d_swT",), writes=S.K_ewu, tag="swu")
        gb = {"g": [(S.WS2, "WS2"), (S.e_wg2, S.K_ewg2[0])],
              "u": (S.e_wu, S.K_ewu[0])}
        for t in range(NT + 1):
            if t < NT:
                d = dict(gb)
                d["lhs"] = (lambda dc, t=t:
                            S.nT[:, dc * T + t * 128: dc * T + (t + 1) * 128])
                d["lhk"] = ("nT",)
                S.moe_stageA(t % 2, d, None, None)
            if t >= 1:
                tp_ = t - 1
                ps = S.moe_stageB(tp_ % 2, [(S.WS0, "WS0"), (S.WS1, "WS1")])
                ssl = S.stream[:, tp_ * D:(tp_ + 1) * D]
                for nh in range(2):
                    S.tt(ssl[:, nh * 512:(nh + 1) * 512], ssl[:, nh * 512:(nh + 1) * 512],
                         ps["U"][nh], ALU.add, ("stream",) + ps["UKn"][nh], ("stream",))
        for t in range(NT):
            p.dma("sync", S.e_st, S.d_y_out[t * 128:(t + 1) * 128, :],
                  reads=("d_y_out",), writes=S.K_est, tag="yR")
            if t == NT - 1:
                S.tap("yrs15", S.e_st, S.K_est)
            ssl = S.stream[:, t * D:(t + 1) * D]
            S.tt(ssl, ssl, S.e_st, ALU.add, ("stream",) + S.K_est, ("stream",))

    def build(self, upto=None, skip=()):
        order = ["consts", "wprep1", "gkload", "wamax", "ln", "routing", "wprep2", "wquant", "prepass",
                 "ffn0", "combine", "attn", "ln1", "moeprep", "l1a0", "l1a2", "l1a4", "l1a5", "l1a6", "l1attn",
                 "l1route", "l1experts", "l1shared"]
        todo = order[:order.index(upto) + 1] if upto else order
        todo = [t for t in todo if t not in skip]
        for ph in todo:
            if ph == "consts":
                self.ph_consts()
            elif ph == "wprep1":
                self.ph_wprep1()
            elif ph == "wprep2":
                self.ph_wprep2()
            elif ph == "gkload":
                self.ph_gkload()
            elif ph == "wamax":
                self.ph_wamax()
            elif ph == "ln":
                self.ph_ln()
            elif ph == "routing":
                self.ph_routing()
            elif ph == "wquant":
                self.ph_wquant()
            elif ph == "prepass":
                self.ph_prepass()
            elif ph == "ffn0":
                self.ph_ffn0()
            elif ph == "combine":
                self.ph_combine()
            elif ph == "attn":
                self.ph_attn_main()
            elif ph == "ln1":
                self.ph_ln()
            elif ph == "moeprep":
                self.ph_moe_wprep()
            elif ph == "l1attn":
                self.ph_l1_attn()
            elif ph == "l1a0":
                if upto == "l1a0":
                    self.ph_l1_attn(limit=0)
            elif ph == "l1a2":
                if upto == "l1a2":
                    self.ph_l1_attn(limit=2)
            elif ph in ("l1a4", "l1a5", "l1a6"):
                if upto == ph:
                    self.ph_l1_attn(limit=int(ph[-1]))
            elif ph == "l1route":
                self.ph_l1_route()
            elif ph == "l1experts":
                self.ph_l1_experts()
            elif ph == "l1shared":
                self.ph_l1_shared()
        p = self.p
        p.dma("gpsimd", self.out_d.rearrange("(t p) d -> p t d", p=128),
              self.stream[:, :].rearrange("p (t d) -> p t d", d=D),
              reads=("stream",), writes=("out_d",), tag="stout")
        p.finalize()
        p.build_block()
        return self


def host_inputs(inputs, core):
    b, qt = core // 4, core % 4
    x = np.asarray(inputs["L0_attn_x"])
    d = {}
    d["x"] = np.ascontiguousarray(x[b, qt * T:(qt + 1) * T, :], dtype=np.float32)
    ren = {
        "w_gate": "L0_attn_w_gate", "w_q": "L0_attn_w_q", "w_k": "L0_attn_w_k",
        "w_v": "L0_attn_w_v", "w_gk1": "L0_attn_w_gk1", "w_gk2": "L0_attn_w_gk2",
        "b_gk": "L0_attn_b_gk", "w_ks": "L0_attn_w_ks", "w_vs": "L0_attn_w_vs",
        "w_gk1s": "L0_attn_w_gk1s", "w_gk2s": "L0_attn_w_gk2s",
        "w_g": "L0_attn_w_g", "w_o": "L0_attn_w_o",
        "a_Wq": "L1_attn_Wq", "a_Wk": "L1_attn_Wk", "a_Wv": "L1_attn_Wv",
        "a_Wg": "L1_attn_Wg", "a_Wo": "L1_attn_Wo",
        "m_gate_w": "L1_ffn_gate_w",
        "m_sg": "L1_ffn_shared_gate_w", "m_su": "L1_ffn_shared_up_w",
        "m_sd": "L1_ffn_shared_down_w",
    }
    for k, v in ren.items():
        d[k] = np.ascontiguousarray(np.asarray(inputs[v], dtype=np.float32))
    d["b_gks"] = np.asarray(inputs["L0_attn_b_gks"], np.float32).reshape(1, DK)
    d["norm_w"] = np.asarray(inputs["L0_attn_norm_w"], np.float32).reshape(1, HDV)
    d["f_s_in"] = np.asarray(inputs["L0_ffn_s_in"], np.float32).reshape(1, 1)
    d["f_s_in_down"] = np.asarray(inputs["L0_ffn_s_in_down"], np.float32).reshape(1, 1)
    gw = np.asarray(inputs["L0_ffn_gate_w"], np.float32)
    uw = np.asarray(inputs["L0_ffn_up_w"], np.float32)
    dw = np.asarray(inputs["L0_ffn_down_w"], np.float32)
    d["fw_gshard"] = np.ascontiguousarray(gw[core * 512:(core + 1) * 512, :])
    d["fw_ushard"] = np.ascontiguousarray(uw[core * 512:(core + 1) * 512, :])
    d["fw_dshard"] = np.ascontiguousarray(dw[core * 128:(core + 1) * 128, :])
    for n in "qkvgo":
        d["a_b" + n] = np.asarray(inputs["L1_attn_b" + n], np.float32).reshape(1, D)
    d["m_ebias"] = np.asarray(inputs["L1_ffn_expert_bias"], np.float32).reshape(1, E16)
    d["m_wg"] = np.ascontiguousarray(
        np.asarray(inputs["L1_ffn_w_gate"], np.float32)[core * EPC:(core + 1) * EPC])
    d["m_wu"] = np.ascontiguousarray(
        np.asarray(inputs["L1_ffn_w_up"], np.float32)[core * EPC:(core + 1) * EPC])
    d["m_wd"] = np.ascontiguousarray(
        np.asarray(inputs["L1_ffn_w_down"], np.float32)[core * EPC:(core + 1) * EPC])
    sm = np.zeros((128, 16), np.float32)
    sl = np.zeros((128, 4), np.float32)
    for j in range(4):
        if j < qt:
            sl[:, j] = 1.0
        for l in range(4):
            if j < l < qt:
                sm[:, j * 4 + l] = 1.0
    d["smask"], d["ssel"] = sm, sl
    cm = np.zeros((4, 2), np.float32)
    cm[:, 1] = 1.0
    for j in range(4):
        if j < qt:
            cm[j, 0] = 1.0
    d["cmask"] = cm
    es = np.zeros((128, 4 * EPC), np.float32)
    for q4 in range(2):
        for blk in range(4):
            for i in range(EPC):
                es[q4 * 64 + blk * 16 + core * EPC + i, blk * EPC + i] = 1.0
    d["esel"] = es
    return d


_BUILD_CACHE = {}


def kernel(**inputs):
    key = "full"
    if key not in _BUILD_CACHE:
        _BUILD_CACHE[key] = KB().build()
    kb = _BUILD_CACHE[key]
    in_maps = [host_inputs(inputs, c) for c in range(8)]
    res = run_bass_kernel_spmd(kb.nc, in_maps, core_ids=list(range(8)))
    out = np.empty((2, 8192, 1024), np.float32)
    for c in range(8):
        b, qt = c // 4, c % 4
        out[b, qt * T:(qt + 1) * T, :] = res.results[c]["out"]
    return out
